# version 16 (final)
# baseline (speedup 1.0000x reference)
import numpy as np
import ml_dtypes
from contextlib import ExitStack
import concourse.bass as bass
import concourse.mybir as mybir
from concourse.bass_utils import run_bass_kernel_spmd

F32 = mybir.dt.float32
BF16 = mybir.dt.bfloat16
AF = mybir.ActivationFunctionType
ALU = mybir.AluOpType
AX = mybir.AxisListType

B, T, D = 4, 8192, 1024
HTOK = 4096
SUP_EVEN = [0, 1, 2, 3, 12, 13, 14, 15]
SUP_ODD = [4, 5, 6, 7, 8, 9, 10, 11]
LN_EPS = 1e-5
BF = ml_dtypes.bfloat16


class Sem:
    __slots__ = ("h", "n", "name")

    def __init__(self, nc, name):
        self.h = nc.alloc_semaphore(name)
        self.n = 0
        self.name = name


class Prog:
    def __init__(self, nc):
        self.nc = nc
        self.sems = {}
        self._waited = {}

    def sem(self, name) -> Sem:
        if name not in self.sems:
            self.sems[name] = Sem(self.nc, name)
        return self.sems[name]

    def inc(self, inst, sem: Sem, k=1):
        inst.then_inc(sem.h, k)
        sem.n += k
        return sem.n

    def dma(self, eng, sem: Sem, out, in_):
        inst = eng.dma_start(out=out, in_=in_)
        self.inc(inst, sem, 16)
        return sem.n

    def wait(self, eng, sem: Sem, val=None):
        if val is None:
            val = sem.n
        if val <= 0:
            return
        key = (str(eng.engine), sem.name)
        if self._waited.get(key, -1) >= val:
            return
        eng.wait_ge(sem.h, val)
        self._waited[key] = val

    def fence(self, engines, sems):
        for e in engines:
            for s in sems:
                self.wait(e, s)

    def fence_all(self, engines):
        self.fence(engines, list(self.sems.values()))

    def snap(self):
        return ({k: s.n for k, s in self.sems.items()}, dict(self._waited))

    def restore(self, sn):
        counts, waited = sn
        for k, v in counts.items():
            self.sems[k].n = v
        self._waited = dict(waited)

    def assert_counts(self, sn_counts, tag=""):
        for k, v in sn_counts.items():
            assert self.sems[k].n == v, \
                f"switch body sem mismatch {tag}: {k} {self.sems[k].n} != {v}"


def ln_tile(p: Prog, src_f32, out_f32, sq_tmp, st_tmp, sV: Sem, sS: Sem,
            eps_ap, inv1024_ap):
    nc = p.nc
    v, s = nc.vector, nc.scalar
    p.wait(v, sS)
    p.inc(v.reduce_sum(st_tmp[:, 0:1], src_f32, axis=AX.X, negate=True), sV)
    p.wait(v, sV)
    p.inc(v.tensor_scalar_mul(st_tmp[:, 1:2], st_tmp[:, 0:1], 1.0 / 1024.0), sV)
    p.wait(s, sV)
    p.inc(s.activation(sq_tmp, src_f32, AF.Square, bias=st_tmp[:, 1:2], scale=1.0), sS)
    p.wait(v, sS)
    p.inc(v.reduce_sum(st_tmp[:, 2:3], sq_tmp, axis=AX.X), sV)
    p.wait(s, sV)
    p.inc(s.activation(st_tmp[:, 2:3], st_tmp[:, 2:3], AF.Sqrt,
                       bias=eps_ap, scale=inv1024_ap), sS)
    p.wait(v, sS)
    p.inc(v.reciprocal(st_tmp[:, 3:4], st_tmp[:, 2:3]), sV)
    p.wait(v, sV)
    p.inc(v.tensor_mul(st_tmp[:, 1:2], st_tmp[:, 1:2], st_tmp[:, 3:4]), sV)
    p.wait(s, sV)
    p.inc(s.activation(out_f32, src_f32, AF.Identity,
                       bias=st_tmp[:, 1:2], scale=st_tmp[:, 3:4]), sS)
    return sS.n


def emit_l1_ffn(p: Prog, io, stages="ABCD"):
    nc = p.nc
    tn, v, s, g, sy = nc.tensor, nc.vector, nc.scalar, nc.gpsimd, nc.sync
    x3 = io["l1f_x"].ap()
    x3bf = io["l1f_xbf"].ap()
    y = io["out_y"].ap()
    w1T, w2T = io["l1f_w1T"], io["l1f_w2T"]
    h1Tb, h2Tb = io["l1f_h1Tb"], io["l1f_h2Tb"]
    w3Tb = io["l1f_w3Tb"]
    sc = io["l1f_scales"].ap()
    idf = io["ident_f32"]
    idb = io["ident_bf16"]
    h_scr = io["scr_a"]
    g_scr = io["scr_b"]

    mm = p.sem("mm"); evS = p.sem("evS"); evV = p.sem("evV")
    trc = p.sem("trc")
    sV = p.sem("lnV"); sS = p.sem("lnS")
    preS = p.sem("f1pre"); wS = p.sem("f1wS")
    xtS = [p.sem("f1xt0"), p.sem("f1xt1")]
    hbS = [p.sem("f1hb0"), p.sem("f1hb1")]

    with (
        nc.sbuf_tensor("f1cp", [128, 4], F32) as cpool,
        nc.sbuf_tensor("f1sc", [128, 5 * 32], F32) as sc32,
        nc.sbuf_tensor("f1idb", [128, 128], BF16) as idbs,
        nc.sbuf_tensor("f1idf", [128, 128], F32) as idfs,
        nc.psum_tensor("f1ps", [128, 4 * 512], F32) as ps,
        nc.psum_tensor("f1pt", [128, 2 * 1024], BF16) as pt,
        nc.psum_tensor("f1pf", [128, 512], F32) as pf,
    ):
        p.inc(nc.vector.memset(cpool[:, 0:1], LN_EPS), sV)
        p.inc(nc.vector.memset(cpool[:, 1:2], 1.0 / 1024.0), sV)
        p.dma(sy, preS, idbs[:, :], idb.ap())
        p.dma(sy, preS, idfs[:, :], idf.ap())
        for r in range(5):
            for oc in range(32 if r < 4 else 8):
                p.dma(sy, preS, sc32[:, r * 32 + oc:r * 32 + oc + 1],
                      sc[r, oc * 128:(oc + 1) * 128].unsqueeze(1))
        pre_ready = preS.n

        with (
            nc.sbuf_tensor("f1w12", [128, 2 * 8 * 4096], BF16) as w12,
            nc.sbuf_tensor("f1xt", [128, 2 * 1024], BF16) as xt,
            nc.sbuf_tensor("f1x3T", [128, 2 * 8 * 512], BF16) as x3T,
            nc.sbuf_tensor("f1tmp", [128, 4 * 512], F32) as tmp,
            nc.sbuf_tensor("f1hb", [128, 2 * 512], BF16) as hb,
        ):
            for wi, wsrc in enumerate((w1T, w2T) if "A" in stages else ()):
                for c in range(8):
                    p.dma(g, wS, w12[:, (wi * 8 + c) * 4096:(wi * 8 + c + 1) * 4096],
                          wsrc.ap()[c * 128:(c + 1) * 128, :])
            w_ready = wS.n

            xt_free = []; x3T_free = []; tr_free = []; ps_free = []; hb_cnt = [0, 0]
            for ts in range(HTOK // 512 if "A" in stages else 0):
                for tt in range(4):
                    k = ts * 4 + tt
                    if len(xt_free) >= 2:
                        p.wait(sy, mm, xt_free[-2])
                    p.dma(sy, xtS[k % 2], xt[:, (k % 2) * 1024:(k % 2 + 1) * 1024],
                          x3bf[k * 128:(k + 1) * 128, :])
                    p.wait(tn, xtS[k % 2])
                    p.wait(tn, preS, pre_ready)
                    if len(x3T_free) >= 2:
                        p.wait(v, mm, x3T_free[-2])
                    for c in range(8):
                        i = k * 8 + c
                        if len(tr_free) >= 2:
                            p.wait(tn, trc, tr_free[-2])
                        inst = tn.transpose(
                            pt[:, (i % 2) * 1024:(i % 2) * 1024 + 128],
                            xt[:, (k % 2) * 1024 + c * 128:(k % 2) * 1024 + (c + 1) * 128],
                            idbs[:, :])
                        tok = p.inc(inst, mm)
                        if c == 7:
                            xt_free.append(tok)
                        p.wait(v, mm, tok)
                        cp = v.tensor_copy(
                            x3T[:, (ts % 2) * 4096 + c * 512 + tt * 128:
                                (ts % 2) * 4096 + c * 512 + (tt + 1) * 128],
                            pt[:, (i % 2) * 1024:(i % 2) * 1024 + 128])
                        tr_free.append(p.inc(cp, trc))
                slice_ready = trc.n
                p.wait(tn, wS, w_ready)
                p.wait(tn, trc, slice_ready)
                last_mm = None
                for oc in range(32):
                    pr = oc % 2
                    if len(ps_free) >= 2:
                        p.wait(tn, evV, ps_free[-2])
                    for wi in range(2):
                        psl = ps[:, (2 * pr + wi) * 512:(2 * pr + wi + 1) * 512]
                        for c in range(8):
                            inst = tn.matmul(
                                psl,
                                w12[:, (wi * 8 + c) * 4096 + oc * 128:(wi * 8 + c) * 4096 + (oc + 1) * 128],
                                x3T[:, (ts % 2) * 4096 + c * 512:(ts % 2) * 4096 + (c + 1) * 512],
                                start=(c == 0), stop=(c == 7))
                        tok = p.inc(inst, mm)
                    last_mm = tok
                    t1f = tmp[:, pr * 512:(pr + 1) * 512]
                    t2f = tmp[:, (2 + pr) * 512:(3 + pr) * 512]
                    p.wait(s, mm, tok - 1)
                    p.wait(s, preS, pre_ready)
                    if len(ps_free) >= 2:
                        p.wait(s, evV, ps_free[-2])
                    p.inc(s.activation(t1f, ps[:, 2 * pr * 512:(2 * pr + 1) * 512],
                                       AF.Silu, bias=0.0,
                                       scale=sc32[:, oc:oc + 1]), evS)
                    p.wait(v, mm, tok)
                    p.wait(v, preS, pre_ready)
                    p.inc(v.tensor_scalar(t2f, ps[:, (2 * pr + 1) * 512:(2 * pr + 2) * 512],
                                          sc32[:, 32 + oc:32 + oc + 1], None, ALU.mult), evV)
                    p.wait(v, evV)
                    p.wait(v, evS)
                    if hb_cnt[pr] >= 1:
                        p.wait(v, hbS[pr], hb_cnt[pr] * 16)
                    mul = v.tensor_tensor(hb[:, pr * 512:(pr + 1) * 512], t1f, t2f, ALU.mult)
                    ps_free.append(p.inc(mul, evV))
                    p.wait(g, evV)
                    p.dma(g, hbS[pr],
                          bass.AP(h_scr, oc * 128 * 4096 + ts * 512, [[4096, 128], [1, 512]]),
                          hb[:, pr * 512:(pr + 1) * 512])
                    hb_cnt[pr] += 1
                x3T_free.append(last_mm)

        def dense4k(src_scr, dst_scr, wTb, sc_row):
            p.fence_all([sy, g, nc.tensor, nc.vector, nc.scalar])
            evS_base = evS.n
            hsS = p.sem("f1hs")
            wbS = [p.sem("f1wb0"), p.sem("f1wb1")]
            gbS = [p.sem("f1gb0"), p.sem("f1gb1")]
            with (
                nc.sbuf_tensor(f"f1hs{sc_row}", [128, 32 * 2048], BF16) as hs,
                nc.sbuf_tensor(f"f1wbt{sc_row}", [128, 2 * 32 * 128], BF16) as wb,
                nc.sbuf_tensor(f"f1gbt{sc_row}", [128, 2 * 4 * 512], BF16) as gb,
            ):
                hs_free_mm = 0
                wb_free = []
                gb_cnt = [gbS[0].n // 16, gbS[1].n // 16]

                def wb_load(kw):
                    if len(wb_free) >= 1:
                        p.wait(g, mm, wb_free[-1])
                    p.dma(g, wbS[kw % 2], wb[:, (kw % 2) * 4096:(kw % 2 + 1) * 4096],
                          bass.AP(wTb, (kw % 32) * 128 * 4096, [[4096, 128], [1, 4096]]))

                wb_load(0)
                ko = 0
                for mt in range(2):
                    p.wait(sy, mm, hs_free_mm)
                    for c in range(32):
                        p.dma(sy, hsS, hs[:, c * 2048:(c + 1) * 2048],
                              bass.AP(src_scr, c * 128 * 4096 + mt * 2048, [[4096, 128], [1, 2048]]))
                    hs_ready = hsS.n
                    for oc in range(32):
                        pr = ko % 2
                        if ko + 1 < 64:
                            wb_load(ko + 1)
                        p.wait(tn, wbS[pr])
                        p.wait(tn, hsS, hs_ready)
                        last = None
                        for sub in range(4):
                            if ko > 0:
                                p.wait(tn, evS, evS_base + (ko - 1) * 4 + sub + 1)
                            psl = ps[:, sub * 512:(sub + 1) * 512]
                            for c in range(32):
                                inst = tn.matmul(
                                    psl, wb[:, pr * 4096 + c * 128:pr * 4096 + (c + 1) * 128],
                                    hs[:, c * 2048 + sub * 512:c * 2048 + (sub + 1) * 512],
                                    start=(c == 0), stop=(c == 31))
                            last = p.inc(inst, mm)
                            p.wait(s, mm, last)
                            if gb_cnt[pr] >= 1 and sub == 0:
                                p.wait(s, gbS[pr], gb_cnt[pr] * 16)
                            p.inc(s.activation(gb[:, pr * 2048 + sub * 512:pr * 2048 + (sub + 1) * 512],
                                               psl, AF.Gelu, bias=0.0,
                                               scale=sc32[:, sc_row * 32 + oc:sc_row * 32 + oc + 1]), evS)
                        wb_free.append(last)
                        p.wait(g, evS)
                        p.dma(g, gbS[pr],
                              bass.AP(dst_scr, oc * 128 * 4096 + mt * 2048, [[4096, 128], [1, 2048]]),
                              gb[:, pr * 2048:(pr + 1) * 2048])
                        gb_cnt[pr] += 1
                        ko += 1
                    hs_free_mm = mm.n

        if "B" in stages:
            dense4k(h_scr, g_scr, h1Tb, 2)
        if "C" in stages:
            dense4k(g_scr, h_scr, h2Tb, 3)

        p.fence_all([sy, g, nc.tensor, nc.vector, nc.scalar])
        w3S = p.sem("f1w3S")
        g2S = [p.sem("f1g20"), p.sem("f1g21")]
        xrS = [p.sem("f1xr0"), p.sem("f1xr1")]
        ydS = [p.sem("f1yd0"), p.sem("f1yd1")]
        with (
            nc.sbuf_tensor("f1g2", [128, 2 * 32 * 512], BF16) as g2s,
            nc.sbuf_tensor("f1w3", [128, 8 * 4096], BF16) as w3s,
            nc.sbuf_tensor("f1y4T", [128, 8 * 512], F32) as y4T,
            nc.sbuf_tensor("f1yt", [128, 2 * 1024], F32) as yt,
            nc.sbuf_tensor("f1xr", [128, 2 * 1024], F32) as xr,
            nc.sbuf_tensor("f1sq", [128, 1024], F32) as sq,
            nc.sbuf_tensor("f1st", [128, 4], F32) as st,
        ):
            for oc in range(8 if "D" in stages else 0):
                p.dma(g, w3S, w3s[:, oc * 4096:(oc + 1) * 4096],
                      bass.AP(w3Tb, oc * 128 * 4096, [[4096, 128], [1, 4096]]))
            w3_ready = w3S.n

            g2_free = [0, 0]
            tr32_free = []; yd_cnt = [0, 0]; xr_guard = []
            y4_last_tr = 0
            for ts in range(HTOK // 512 if "D" in stages else 1):
                pr = ts % 2
                p.wait(sy, mm, g2_free[pr])
                for c in range(32):
                    p.dma(sy, g2S[pr], g2s[:, pr * 16384 + c * 512:pr * 16384 + (c + 1) * 512],
                          bass.AP(h_scr, c * 128 * 4096 + ts * 512, [[4096, 128], [1, 512]]))
                p.wait(tn, w3S, w3_ready)
                p.wait(tn, g2S[pr])
                last = None
                ev_base = evV.n
                for oc in range(8):
                    psl = ps[:, (oc % 4) * 512:(oc % 4 + 1) * 512]
                    if oc >= 4:
                        p.wait(tn, evV, ev_base + oc - 4 + 1)
                    for c in range(32):
                        inst = tn.matmul(psl,
                                         w3s[:, oc * 4096 + c * 128:oc * 4096 + (c + 1) * 128],
                                         g2s[:, pr * 16384 + c * 512:pr * 16384 + (c + 1) * 512],
                                         start=(c == 0), stop=(c == 31))
                    last = p.inc(inst, mm)
                    p.wait(v, mm, last)
                    p.wait(v, mm, y4_last_tr)
                    p.wait(v, preS, pre_ready)
                    p.inc(v.tensor_scalar(y4T[:, oc * 512:(oc + 1) * 512], psl,
                                          sc32[:, 4 * 32 + oc:4 * 32 + oc + 1], None, ALU.mult), evV)
                g2_free[pr] = last
                y4_ready = evV.n
                def xr_load(kk):
                    if len(xr_guard) >= 1:
                        p.wait(s, trc, xr_guard[-1])
                    p.dma(s, xrS[kk % 2], xr[:, (kk % 2) * 1024:(kk % 2 + 1) * 1024],
                          x3[kk * 128:(kk + 1) * 128, :])
                for tt in range(4):
                    k = ts * 4 + tt
                    if tt == 0:
                        xr_load(k)
                    if tt + 1 < 4:
                        xr_load(k + 1)
                    p.wait(tn, evV, y4_ready)
                    for oc in range(8):
                        i = k * 8 + oc
                        if len(tr32_free) >= 1:
                            p.wait(tn, trc, tr32_free[-1])
                        inst = tn.transpose(pf[:, 0:128],
                                            y4T[:, oc * 512 + tt * 128:oc * 512 + (tt + 1) * 128],
                                            idfs[:, :])
                        ttok = p.inc(inst, mm)
                        if tt == 3 and oc == 7:
                            y4_last_tr = ttok
                        p.wait(v, mm, ttok)
                        p.wait(v, xrS[k % 2])
                        if yd_cnt[k % 2] >= 1 and oc == 0:
                            p.wait(v, ydS[k % 2], yd_cnt[k % 2] * 16)
                        add = v.tensor_tensor(
                            yt[:, (k % 2) * 1024 + oc * 128:(k % 2) * 1024 + (oc + 1) * 128],
                            pf[:, 0:128],
                            xr[:, (k % 2) * 1024 + oc * 128:(k % 2) * 1024 + (oc + 1) * 128],
                            ALU.add)
                        tr32_free.append(p.inc(add, trc))
                    xr_guard.append(trc.n)
                    p.wait(v, trc)
                    p.wait(s, trc)
                    ytile = yt[:, (k % 2) * 1024:(k % 2 + 1) * 1024]
                    p.wait(s, sV)
                    ln_tile(p, ytile, ytile, sq[:, :], st, sV, sS,
                            cpool[:, 0:1], cpool[:, 1:2])
                    p.wait(g, sS)
                    p.dma(g, ydS[k % 2], y[k * 128:(k + 1) * 128, :], ytile)
                    yd_cnt[k % 2] += 1
            p.wait(g, ydS[0])
            p.wait(g, ydS[1])


def prep_host_l1f(inputs):
    h = {}
    h["ident_f32"] = np.eye(128, dtype=np.float32)
    h["ident_bf16"] = np.eye(128, dtype=np.float32).astype(BF)
    h["l1f_w1T"] = np.ascontiguousarray(inputs["L1_ffn_w1_w"].T)
    h["l1f_w2T"] = np.ascontiguousarray(inputs["L1_ffn_w2_w"].T)

    def blk(wT, noc):
        n_in = wT.shape[0]
        return np.ascontiguousarray(
            wT.reshape(n_in // 128, 128, noc, 128).transpose(2, 1, 0, 3))

    h["l1f_h1Tb"] = blk(np.ascontiguousarray(inputs["L1_ffn_h1_w"].T), 32)
    h["l1f_h2Tb"] = blk(np.ascontiguousarray(inputs["L1_ffn_h2_w"].T), 32)
    h["l1f_w3Tb"] = blk(np.ascontiguousarray(inputs["L1_ffn_w3_w"].T), 8)
    sc = np.zeros((5, 4096), dtype=np.float32)
    sc[0] = np.asarray(inputs["L1_ffn_w1_s"])
    sc[1] = np.asarray(inputs["L1_ffn_w2_s"])
    sc[2] = np.asarray(inputs["L1_ffn_h1_s"])
    sc[3] = np.asarray(inputs["L1_ffn_h2_s"])
    sc[4, :1024] = np.asarray(inputs["L1_ffn_w3_s"])
    h["l1f_scales"] = sc
    return h


def declare_l1f(nc, kind_act="ExternalInput"):
    io = {}
    io["l1f_x"] = nc.dram_tensor("l1f_x", [HTOK, 1024], F32, kind=kind_act)
    io["l1f_xbf"] = nc.dram_tensor("l1f_xbf", [HTOK, 1024], BF16, kind="ExternalInput")
    io["out_y"] = nc.dram_tensor("out_y", [HTOK, 1024], F32, kind="ExternalOutput")
    io["l1f_w1T"] = nc.dram_tensor("l1f_w1T", [1024, 4096], F32, kind="ExternalInput")
    io["l1f_w2T"] = nc.dram_tensor("l1f_w2T", [1024, 4096], F32, kind="ExternalInput")
    io["l1f_h1Tb"] = nc.dram_tensor("l1f_h1Tb", [32, 128, 32, 128], F32, kind="ExternalInput")
    io["l1f_h2Tb"] = nc.dram_tensor("l1f_h2Tb", [32, 128, 32, 128], F32, kind="ExternalInput")
    io["l1f_w3Tb"] = nc.dram_tensor("l1f_w3Tb", [8, 128, 32, 128], F32, kind="ExternalInput")
    io["l1f_scales"] = nc.dram_tensor("l1f_scales", [5, 4096], F32, kind="ExternalInput")
    io["ident_f32"] = nc.dram_tensor("ident_f32", [128, 128], F32, kind="ExternalInput")
    io["ident_bf16"] = nc.dram_tensor("ident_bf16", [128, 128], BF16, kind="ExternalInput")
    io["scr_a"] = nc.dram_tensor("scr_a", [4096 * 4096], BF16, kind="Internal")
    io["scr_b"] = nc.dram_tensor("scr_b", [4096 * 4096], BF16, kind="Internal")
    return io


def build_l1f_program(stages="ABCD"):
    nc = bass.Bass("TRN2", target_bir_lowering=False)
    io = declare_l1f(nc)
    p = Prog(nc)
    emit_l1_ffn(p, io, stages)
    return nc


def emit_l0_ffn(p: Prog, io):
    nc = p.nc
    tn, v, s, g, sy = nc.tensor, nc.vector, nc.scalar, nc.gpsimd, nc.sync
    x1 = io["l0f_x"].ap()
    x1bf = io["l0f_xbf"].ap()
    y = io["l0f_y"].ap()
    ybf_dram = io["l0f_ybf"].ap()
    We1b = io["l0f_We1b"]
    We3b = io["l0f_We3b"]
    We2b = io["l0f_We2b"]
    ws1b = io["l0f_ws1b"]
    ws3b = io["l0f_ws3b"]
    ws2b = io["l0f_ws2b"]
    gwb = io["l0f_gwb"]
    gbr = io["l0f_gbrep"]
    be1 = io["l0f_be1"].ap()
    be3 = io["l0f_be3"].ap()
    be2 = io["l0f_be2"]
    bs1 = io["l0f_bs1"].ap()
    bs3 = io["l0f_bs3"].ap()
    bs2 = io["l0f_bs2"].ap()
    oneh = io["l0f_onehot"]
    idf = io["ident_f32"]
    idb = io["ident_bf16"]

    mm = p.sem("mm"); evS = p.sem("evS"); evV = p.sem("evV")
    trc = p.sem("trc"); gsem = trc
    sV = p.sem("lnV"); sS = p.sem("lnS")
    preS = p.sem("f0pre"); wshS = p.sem("f0wsh")
    xtS = [p.sem("f0xt0"), p.sem("f0xt1")]
    we13S = [p.sem("f0we0"), p.sem("f0we1")]
    we2S = p.sem("f0we2")
    xrS = [p.sem("f0xr0"), p.sem("f0xr1")]
    ydS = [p.sem("f0yd0"), p.sem("f0yd1")]

    NMT = 8
    MT = 1024

    with ExitStack() as stk:
        def SB(*a):
            return stk.enter_context(nc.sbuf_tensor(*a))
        cpool = SB("f0cp", [128, 4], F32)
        idbs = SB("f0idb", [128, 128], BF16)
        idfs = SB("f0idf", [128, 128], F32)
        gwsb = SB("f0gw", [128, 8 * 16], BF16)
        gbsb = SB("f0gbr", [128, 16], F32)
        ohsb = SB("f0oh", [16, 16 * 128], BF16)
        be1sb = SB("f0be1", [128, 64], F32)
        be3sb = SB("f0be3", [128, 64], F32)
        be2sb = SB("f0be2", [16, 1024], BF16)
        bssb = SB("f0bs", [128, 3 * 8], F32)
        wsh = SB("f0wsh", [128, 3 * 8 * 1024], BF16)
        x1T = SB("f0x1T", [128, 8 * MT], BF16)
        zacc = SB("f0za", [128, 8 * MT], F32)
        we13 = SB("f0we13", [128, 2 * 8192], BF16)
        we2sb = SB("f0we2", [128, 4 * 1024], BF16)
        xt = SB("f0xt", [128, 2 * 1024], BF16)
        hbuf = SB("f0hb", [128, 8 * 512], BF16)
        tmp = SB("f0tmp", [128, 6 * 512], F32)
        shT = SB("f0sh", [128, 8 * 512], BF16)
        gt = SB("f0gt", [128, 6 * 16], F32)
        gsc = SB("f0gs", [128, 16], F32)
        cbf = SB("f0cb", [128, 32], BF16)
        cT16 = SB("f0cT", [16, MT], BF16)
        yt = SB("f0yt", [128, 2 * 1024], F32)
        ybf = SB("f0ybf", [128, 2 * 1024], BF16)
        xr = SB("f0xr", [128, 2 * 1024], F32)
        sq = SB("f0sq", [128, 1024], F32)
        st = SB("f0st", [128, 4], F32)
        PS = stk.enter_context(nc.psum_tensor("f0ps", [128, 7 * 512], F32))
        def bank(i, w=512):
            return PS[:, i * 512:i * 512 + w]

        b6bf = PS[:, 6 * 512:7 * 512].bitcast(BF16)
        b3bf = PS[:, 3 * 512:4 * 512].bitcast(BF16)

        p.inc(v.memset(cpool[:, 0:1], LN_EPS), sV)
        p.inc(v.memset(cpool[:, 1:2], 1.0 / 1024.0), sV)
        p.dma(sy, preS, idbs[:, :], idb.ap())
        p.dma(sy, preS, idfs[:, :], idf.ap())
        p.dma(sy, preS, gbsb[:, :], gbr.ap())
        for e in range(16):
            for ic in range(4):
                p.dma(sy, preS, be1sb[:, e * 4 + ic:e * 4 + ic + 1],
                      be1[e, ic * 128:(ic + 1) * 128].unsqueeze(1))
                p.dma(sy, preS, be3sb[:, e * 4 + ic:e * 4 + ic + 1],
                      be3[e, ic * 128:(ic + 1) * 128].unsqueeze(1))
        for r, src in enumerate((bs1, bs3, bs2)):
            for ic in range(8):
                p.dma(sy, preS, bssb[:, r * 8 + ic:r * 8 + ic + 1],
                      src[ic * 128:(ic + 1) * 128].unsqueeze(1))
        preG = p.sem("f0preG")
        p.dma(g, preG, gwsb[:, :], bass.AP(gwb, 0, [[8 * 16, 128], [1, 8 * 16]]))
        p.dma(g, preG, ohsb[:, :], bass.AP(oneh, 0, [[16 * 128, 16], [1, 16 * 128]]))
        p.dma(g, preG, be2sb[:, :], bass.AP(be2, 0, [[1024, 16], [1, 1024]]))
        for wi, wsrc in enumerate((ws1b, ws3b, ws2b)):
            p.dma(g, wshS, wsh[:, wi * 8192:(wi + 1) * 8192],
                  bass.AP(wsrc, 0, [[8 * 1024, 128], [1, 8 * 1024]]))
        pre_ready = preS.n
        preg_ready = preG.n
        wsh_ready = wshS.n

        xt_cnt = 0
        tr_free = []
        pend_gate = []
        pend_cbf_tok = [0]
        x1T_free_mm = 0
        cb_read_evV = 0
        we13_free = []
        we2_free_mm = 0
        zacc_free = 0
        xr_guard = []
        yd_cnt = [0, 0]
        ydbS = p.sem("ydb")
        ybf_cnt = [0]
        eptr_free = []
        gate_seq = 0

        pend_tile = [None]
        xtmm = []

        def l0f_tile_body(mt, tt):
            k = mt * 8 + tt
            if mt > 0 and tt == 0:
                p.wait(tn, trc)
            p.wait(tn, xtS[k % 2])
            p.wait(tn, preS, pre_ready)
            if mt > 0:
                p.wait(tn, mm, x1T_free_mm)
                p.wait(v, mm, x1T_free_mm)
            for c in range(8):
                i = k * 8 + c
                b = 4 + (i % 2)
                bview = PS[:, b * 512:(b + 1) * 512].bitcast(BF16)
                if len(tr_free) >= 2:
                    p.wait(tn, trc, tr_free[-2])
                inst = tn.transpose(bview[:, 0:128],
                                    xt[:, (k % 2) * 1024 + c * 128:(k % 2) * 1024 + (c + 1) * 128],
                                    idbs[:, :])
                tok = p.inc(inst, mm)
                p.wait(v, mm, tok)
                cp = v.tensor_copy(x1T[:, c * MT + tt * 128:c * MT + (tt + 1) * 128],
                                   bview[:, 0:128])
                tr_free.append(p.inc(cp, trc))
                if c == 7:
                    xtmm.append(tok)
            if len(pend_gate) >= 2:
                emit_gate_transpose(p, b3bf, idbs, cT16, cbf, *pend_gate.pop(0))
            p.wait(tn, trc)
            p.wait(tn, p.sem("f0preG"), preg_ready)
            for c in range(8):
                inst = tn.matmul(bank(6, 16),
                                 x1T[:, c * MT + tt * 128:c * MT + (tt + 1) * 128],
                                 gwsb[:, c * 16:(c + 1) * 16],
                                 start=(c == 0), stop=(c == 7))
            p.inc(inst, mm)
            emit_gate_tile(p, io, gt, gsc, gbsb, cbf, bank(6, 16), b6bf, idbs,
                           cT16, tt, mm, evV, evS, trc, gsem, pend_cbf_tok)
            pend_gate.append((pend_cbf_tok[0], tt))

        for mt in range(NMT):
            for tt in range(8):
                k = mt * 8 + tt
                if xtmm:
                    p.wait(sy, mm, xtmm[-1])
                p.dma(sy, xtS[k % 2], xt[:, (k % 2) * 1024:(k % 2 + 1) * 1024],
                      x1bf[k * 128:(k + 1) * 128, :])
                xt_cnt += 1
                if pend_tile[0] is not None:
                    l0f_tile_body(*pend_tile[0])
                pend_tile[0] = (mt, tt)
            if pend_tile[0] is not None:
                l0f_tile_body(*pend_tile[0])
                pend_tile[0] = None
            while pend_gate:
                emit_gate_transpose(p, b3bf, idbs, cT16, cbf, *pend_gate.pop(0))
            gate_done = trc.n
            x1T_ready = trc.n

            hA_free = {}
            hb_free = {}
            zo_free = {}
            tmp_free = {}
            for e in range(16):
                pr = e % 2
                if len(we13_free) >= 2:
                    p.wait(g, mm, we13_free[-2])
                p.dma(g, we13S[pr], we13[:, pr * 8192:pr * 8192 + 4096],
                      bass.AP(We1b, e * 128 * 4096, [[4096, 128], [1, 4096]]))
                p.dma(g, we13S[pr], we13[:, pr * 8192 + 4096:pr * 8192 + 8192],
                      bass.AP(We3b, e * 128 * 4096, [[4096, 128], [1, 4096]]))
                p.wait(g, mm, we2_free_mm)
                p.dma(g, we2S, we2sb[:, :],
                      bass.AP(We2b, e * 128 * 4096, [[4096, 128], [1, 4096]]))
                p.wait(tn, we13S[pr])
                p.wait(tn, trc, x1T_ready)
                hb_ready = [0, 0]
                for sub in range(2):
                    p.wait(tn, trc, gate_done)
                    p.wait(tn, evV, cb_read_evV)
                    inst = tn.matmul(bank(6),
                                     ohsb[:, e * 128:(e + 1) * 128],
                                     cT16[:, sub * 512:(sub + 1) * 512],
                                     start=True, stop=True)
                    cb_tok = p.inc(inst, mm)
                    for ic in range(4):
                        icp = ic % 2
                        bA = 2 * icp
                        bB = bA + 1
                        p.wait(tn, evV, hA_free.get(icp, 0))
                        for c in range(8):
                            tn.matmul(bank(bA),
                                      we13[:, pr * 8192 + c * 512 + ic * 128:pr * 8192 + c * 512 + (ic + 1) * 128],
                                      x1T[:, c * MT + sub * 512:c * MT + (sub + 1) * 512],
                                      start=(c == 0), stop=(c == 7))
                            inst = tn.matmul(bank(bB),
                                             we13[:, pr * 8192 + 4096 + c * 512 + ic * 128:pr * 8192 + 4096 + c * 512 + (ic + 1) * 128],
                                             x1T[:, c * MT + sub * 512:c * MT + (sub + 1) * 512],
                                             start=(c == 0), stop=(c == 7))
                        htok = p.inc(inst, mm)
                        t0 = icp * 1536
                        hslot = sub * 4 + ic
                        p.wait(s, mm, htok)
                        p.wait(s, evV, tmp_free.get(icp, 0))
                        p.inc(s.activation(tmp[:, t0:t0 + 512], bank(bA), AF.Silu,
                                           bias=be1sb[:, e * 4 + ic:e * 4 + ic + 1], scale=1.0), evS)
                        p.wait(v, mm, htok)
                        p.wait(v, evV, tmp_free.get(icp, 0))
                        p.inc(v.tensor_scalar(tmp[:, t0 + 512:t0 + 1024], bank(bB),
                                              be3sb[:, e * 4 + ic:e * 4 + ic + 1], None, ALU.add), evV)
                        p.wait(v, evV)
                        p.wait(v, evS)
                        p.wait(v, mm, hb_free.get(hslot, 0))
                        p.inc(v.tensor_tensor(tmp[:, t0 + 1024:t0 + 1536], tmp[:, t0:t0 + 512],
                                              tmp[:, t0 + 512:t0 + 1024], ALU.mult), evV)
                        p.wait(v, evV)
                        p.wait(v, mm, cb_tok)
                        tok = p.inc(v.tensor_tensor(hbuf[:, hslot * 512:(hslot + 1) * 512],
                                                    tmp[:, t0 + 1024:t0 + 1536], bank(6), ALU.mult), evV)
                        hA_free[icp] = tok
                        tmp_free[icp] = tok
                    cb_read_evV = evV.n
                    hb_ready[sub] = evV.n
                p.wait(tn, we2S)
                last_o = None
                for sub in range(2):
                    p.wait(tn, evV, hb_ready[sub])
                    for d in range(8):
                        dp = d % 2
                        bO = 4 + dp
                        p.wait(tn, evV, zo_free.get(dp, 0))
                        for ic in range(4):
                            inst = tn.matmul(bank(bO),
                                             we2sb[:, ic * 1024 + d * 128:ic * 1024 + (d + 1) * 128],
                                             hbuf[:, (sub * 4 + ic) * 512:(sub * 4 + ic + 1) * 512],
                                             start=(ic == 0), stop=(ic == 3))
                        otok = p.inc(inst, mm)
                        last_o = otok
                        p.wait(v, mm, otok)
                        if mt > 0:
                            p.wait(v, trc, zacc_free)
                        dst = zacc[:, d * MT + sub * 512:d * MT + (sub + 1) * 512]
                        if e == 0:
                            zo_free[dp] = p.inc(v.tensor_copy(dst, bank(bO)), evV)
                        else:
                            zo_free[dp] = p.inc(v.tensor_tensor(dst, bank(bO), dst, ALU.add), evV)
                    for ic in range(4):
                        hb_free[sub * 4 + ic] = last_o
                we2_free_mm = last_o
                we13_free.append(mm.n)
            p.wait(tn, wshS, wsh_ready)
            sh_free = {}
            for sub in range(2):
                for ic in range(8):
                    icp = ic % 2
                    bA = 2 * icp
                    bB = bA + 1
                    p.wait(tn, evV, hA_free.get(icp, 0))
                    for c in range(8):
                        tn.matmul(bank(bA),
                                  wsh[:, 0 * 8192 + c * 1024 + ic * 128:0 * 8192 + c * 1024 + (ic + 1) * 128],
                                  x1T[:, c * MT + sub * 512:c * MT + (sub + 1) * 512],
                                  start=(c == 0), stop=(c == 7))
                        inst = tn.matmul(bank(bB),
                                         wsh[:, 1 * 8192 + c * 1024 + ic * 128:1 * 8192 + c * 1024 + (ic + 1) * 128],
                                         x1T[:, c * MT + sub * 512:c * MT + (sub + 1) * 512],
                                         start=(c == 0), stop=(c == 7))
                    htok = p.inc(inst, mm)
                    t0 = icp * 1536
                    p.wait(s, mm, htok)
                    p.wait(s, evV, tmp_free.get(icp, 0))
                    p.inc(s.activation(tmp[:, t0:t0 + 512], bank(bA), AF.Silu,
                                       bias=bssb[:, ic:ic + 1], scale=1.0), evS)
                    p.wait(v, mm, htok)
                    p.wait(v, evV, tmp_free.get(icp, 0))
                    p.inc(v.tensor_scalar(tmp[:, t0 + 512:t0 + 1024], bank(bB),
                                          bssb[:, 8 + ic:8 + ic + 1], None, ALU.add), evV)
                    p.wait(v, evV)
                    p.wait(v, evS)
                    p.wait(v, mm, sh_free.get(ic, 0))
                    tok = p.inc(v.tensor_tensor(shT[:, ic * 512:(ic + 1) * 512],
                                                tmp[:, t0:t0 + 512], tmp[:, t0 + 512:t0 + 1024], ALU.mult), evV)
                    hA_free[icp] = tok
                    tmp_free[icp] = tok
                p.wait(tn, evV, evV.n)
                p.wait(tn, preS, pre_ready)
                last_o = None
                for d in range(8):
                    dp = d % 2
                    bO = 4 + dp
                    p.wait(tn, evV, zo_free.get(dp, 0))
                    for ic in range(8):
                        tn.matmul(bank(bO),
                                  wsh[:, 2 * 8192 + ic * 1024 + d * 128:2 * 8192 + ic * 1024 + (d + 1) * 128],
                                  shT[:, ic * 512:(ic + 1) * 512],
                                  start=(ic == 0), stop=False)
                    inst = tn.matmul(bank(bO),
                                     be2sb[0:16, d * 128:(d + 1) * 128],
                                     cT16[:, sub * 512:(sub + 1) * 512],
                                     start=False, stop=True)
                    otok = p.inc(inst, mm)
                    last_o = otok
                    p.wait(v, mm, otok)
                    dst = zacc[:, d * MT + sub * 512:d * MT + (sub + 1) * 512]
                    stt = v.scalar_tensor_tensor(dst, bank(bO),
                                                 bssb[:, 16 + d:16 + d + 1], dst,
                                                 ALU.add, ALU.add)
                    zo_free[dp] = p.inc(stt, evV)
                for ic in range(8):
                    sh_free[ic] = last_o
            z_ready = evV.n
            def xr_load(kk):
                if len(xr_guard) >= 1:
                    p.wait(s, trc, xr_guard[-1])
                p.dma(s, xrS[kk % 2], xr[:, (kk % 2) * 1024:(kk % 2 + 1) * 1024],
                      x1[kk * 128:(kk + 1) * 128, :])
            for tt in range(8):
                k = mt * 8 + tt
                if tt == 0:
                    xr_load(k)
                if tt + 1 < 8:
                    xr_load(k + 1)
                p.wait(tn, evV, z_ready)
                for d in range(8):
                    bt = (6, 5, 4, 3)[d % 4]
                    if len(eptr_free) >= 4:
                        p.wait(tn, trc, eptr_free[-4])
                    inst = tn.transpose(bank(bt, 128),
                                        zacc[:, d * MT + tt * 128:d * MT + (tt + 1) * 128],
                                        idfs[:, :])
                    ttok = p.inc(inst, mm)
                    p.wait(v, mm, ttok)
                    p.wait(v, xrS[k % 2])
                    if yd_cnt[k % 2] >= 1 and d == 0:
                        p.wait(v, ydS[k % 2], yd_cnt[k % 2] * 16)
                    add = v.tensor_tensor(
                        yt[:, (k % 2) * 1024 + d * 128:(k % 2) * 1024 + (d + 1) * 128],
                        bank(bt, 128),
                        xr[:, (k % 2) * 1024 + d * 128:(k % 2) * 1024 + (d + 1) * 128],
                        ALU.add)
                    eptr_free.append(p.inc(add, trc))
                xr_guard.append(trc.n)
                p.wait(v, trc)
                p.wait(s, trc)
                ytile = yt[:, (k % 2) * 1024:(k % 2 + 1) * 1024]
                p.wait(s, sV)
                ln_tile(p, ytile, ytile, sq[:, :], st, sV, sS,
                        cpool[:, 0:1], cpool[:, 1:2])
                p.wait(g, sS)
                p.dma(g, ydS[k % 2], y[k * 128:(k + 1) * 128, :], ytile)
                yd_cnt[k % 2] += 1
                p.wait(v, sS)
                if ybf_cnt[0] >= 1:
                    p.wait(v, ydbS)
                p.inc(v.tensor_copy(ybf[:, (k % 2) * 1024:(k % 2 + 1) * 1024], ytile), trc)
                p.wait(sy, trc)
                p.dma(sy, ydbS, ybf_dram[k * 128:(k + 1) * 128, :],
                      ybf[:, (k % 2) * 1024:(k % 2 + 1) * 1024])
                ybf_cnt[0] += 1
            zacc_free = trc.n
            x1T_free_mm = mm.n
        p.wait(g, ydS[0])
        p.wait(g, ydS[1])


def emit_gate_transpose(p: Prog, b6bf, idbs, cT16, cbf, cbtok, tt):
    nc = p.nc
    tn, v = nc.tensor, nc.vector
    mm = p.sem("mm"); evV = p.sem("evV"); trc = p.sem("trc")
    p.wait(tn, evV, cbtok)
    gg = getattr(p, "_gate_tr", None)
    if gg is None:
        gg = p._gate_tr = [0]
    if gg[0]:
        p.wait(tn, trc, gg[0])
    inst = tn.transpose(b6bf[0:16, 0:128], cbf[:, (tt % 2) * 16:(tt % 2 + 1) * 16],
                        idbs[:, :])
    tok = p.inc(inst, mm)
    p.wait(v, mm, tok)
    cp = v.tensor_copy(cT16[0:16, tt * 128:(tt + 1) * 128], b6bf[0:16, 0:128])
    gg[0] = p.inc(cp, trc)


def emit_gate_tile(p: Prog, io, gt, gsc, gbsb, cbf, s16psum, b6bf, idbs, cT16,
                   tt, mm, evV, evS, trc, gsem, pend_cbf_tok):
    nc = p.nc
    v, s, tn = nc.vector, nc.scalar, nc.tensor
    T = gt
    orig = T[:, 0:16]
    sgb = T[:, 16:32]
    stl = T[:, 32:48]
    scr = T[:, 48:64]
    scr2 = T[:, 64:80]
    mx = gsc[:, 0:1]
    sm = gsc[:, 1:2]
    gs4 = gsc[:, 4:8]
    t2g = gsc[:, 8:9]
    t2 = gsc[:, 9:10]

    p.wait(v, mm)
    p.wait(v, evS)
    p.inc(v.reduce_max(mx, s16psum, axis=AX.X, negate=True), evV)
    p.wait(s, evV)
    p.wait(s, mm)
    p.inc(s.activation(orig, s16psum, AF.Exp, bias=mx, scale=1.0), evS)
    p.wait(v, evS)
    p.inc(v.reduce_sum(sm, orig, axis=AX.X), evV)
    p.wait(v, evV)
    p.inc(v.reciprocal(sm, sm), evV)
    p.wait(v, evV)
    p.inc(v.tensor_scalar(orig, orig, sm, None, ALU.mult), evV)
    p.wait(v, evV)
    p.inc(v.tensor_tensor(sgb, orig, gbsb[:, :], ALU.add), evV)
    p.wait(v, evV)
    c0, c1, c2, c3 = (sgb[:, j:16:4] for j in range(4))
    p.inc(v.tensor_tensor(scr[:, 0:4], c0, c1, ALU.max), evV)
    p.inc(v.tensor_tensor(scr[:, 4:8], c0, c1, ALU.min), evV)
    p.inc(v.tensor_tensor(scr[:, 8:12], c2, c3, ALU.max), evV)
    p.inc(v.tensor_tensor(scr[:, 12:16], c2, c3, ALU.min), evV)
    p.wait(v, evV)
    p.inc(v.tensor_tensor(scr2[:, 0:4], scr[:, 0:4], scr[:, 8:12], ALU.max), evV)
    p.inc(v.tensor_tensor(scr2[:, 4:8], scr[:, 0:4], scr[:, 8:12], ALU.min), evV)
    p.inc(v.tensor_tensor(scr2[:, 8:12], scr[:, 4:8], scr[:, 12:16], ALU.max), evV)
    p.wait(v, evV)
    p.inc(v.tensor_tensor(scr2[:, 4:8], scr2[:, 4:8], scr2[:, 8:12], ALU.max), evV)
    p.wait(v, evV)
    p.inc(v.tensor_tensor(gs4, scr2[:, 0:4], scr2[:, 4:8], ALU.add), evV)
    p.wait(v, evV)
    g0, g1, g2, g3 = (gs4[:, j:j + 1] for j in range(4))
    p.inc(v.tensor_tensor(scr[:, 0:1], g0, g1, ALU.max), evV)
    p.inc(v.tensor_tensor(scr[:, 1:2], g0, g1, ALU.min), evV)
    p.inc(v.tensor_tensor(scr[:, 2:3], g2, g3, ALU.max), evV)
    p.inc(v.tensor_tensor(scr[:, 3:4], g2, g3, ALU.min), evV)
    p.wait(v, evV)
    p.inc(v.tensor_tensor(scr[:, 4:5], scr[:, 0:1], scr[:, 2:3], ALU.min), evV)
    p.inc(v.tensor_tensor(scr[:, 5:6], scr[:, 1:2], scr[:, 3:4], ALU.max), evV)
    p.wait(v, evV)
    p.inc(v.tensor_tensor(t2g, scr[:, 4:5], scr[:, 5:6], ALU.max), evV)
    p.wait(v, evV)
    p.inc(v.tensor_single_scalar(gs4, gs4, t2g, ALU.is_ge), evV)
    p.wait(v, evV)
    kb = bass.AP(gs4.tensor, gs4.offset, [gs4.ap[0], [1, 4], [0, 4]])
    p.inc(v.tensor_tensor(stl, sgb, kb, ALU.mult), evV)
    p.inc(v.tensor_scalar(scr, kb, 1e30, -1e30, ALU.mult, ALU.add), evV)
    p.wait(v, evV)
    p.inc(v.tensor_tensor(stl, stl, scr, ALU.add), evV)
    p.wait(v, evV)
    a0, a1 = stl[:, 0:16:2], stl[:, 1:16:2]
    p.inc(v.tensor_tensor(scr[:, 0:8], a0, a1, ALU.max), evV)
    p.inc(v.tensor_tensor(scr[:, 8:16], a0, a1, ALU.min), evV)
    p.wait(v, evV)
    for lvl, n in ((0, 4), (1, 2), (2, 1)):
        Ma, Mb = scr[:, 0:2 * n:2], scr[:, 1:2 * n:2]
        Sa, Sb = scr[:, 8:8 + 2 * n:2], scr[:, 9:8 + 2 * n:2]
        p.inc(v.tensor_tensor(scr2[:, 0:n], Ma, Mb, ALU.max), evV)
        p.inc(v.tensor_tensor(scr2[:, 8:8 + n], Ma, Mb, ALU.min), evV)
        p.inc(v.tensor_tensor(scr2[:, 4:4 + n], Sa, Sb, ALU.max), evV)
        p.wait(v, evV)
        p.inc(v.tensor_tensor(scr2[:, 8:8 + n], scr2[:, 8:8 + n], scr2[:, 4:4 + n], ALU.max), evV)
        p.wait(v, evV)
        p.inc(v.tensor_copy(scr[:, 0:n], scr2[:, 0:n]), evV)
        p.inc(v.tensor_copy(scr[:, 8:8 + n], scr2[:, 8:8 + n]), evV)
        p.wait(v, evV)
    p.inc(v.tensor_copy(t2, scr[:, 8:9]), evV)
    p.wait(v, evV)
    p.inc(v.tensor_single_scalar(scr[:, 0:16], stl, t2, ALU.is_ge), evV)
    p.wait(v, evV)
    p.inc(v.tensor_tensor(cbf[:, (tt % 2) * 16:(tt % 2 + 1) * 16], orig, scr[:, 0:16], ALU.mult), evV)
    p.wait(v, evV)
    pend_cbf_tok[0] = evV.n


def prep_host_l0f(inputs):
    h = {}
    h["ident_f32"] = np.eye(128, dtype=np.float32)
    h["ident_bf16"] = np.eye(128, dtype=np.float32).astype(BF)
    We1 = np.asarray(inputs["L0_ffn_We1"])
    We3 = np.asarray(inputs["L0_ffn_We3"])
    We2 = np.asarray(inputs["L0_ffn_We2"])
    h["l0f_We1b"] = np.ascontiguousarray(We1.reshape(16, 8, 128, 512).transpose(0, 2, 1, 3))
    h["l0f_We3b"] = np.ascontiguousarray(We3.reshape(16, 8, 128, 512).transpose(0, 2, 1, 3))
    h["l0f_We2b"] = np.ascontiguousarray(We2.reshape(16, 4, 128, 1024).transpose(0, 2, 1, 3))
    for nm, key in (("ws1b", "L0_ffn_ws1"), ("ws3b", "L0_ffn_ws3"), ("ws2b", "L0_ffn_ws2")):
        w = np.asarray(inputs[key])
        h["l0f_" + nm] = np.ascontiguousarray(w.reshape(8, 128, 1024).transpose(1, 0, 2))
    gw = np.asarray(inputs["L0_ffn_gate_w"])
    gwT = np.ascontiguousarray(gw.T)
    h["l0f_gwb"] = np.ascontiguousarray(gwT.reshape(8, 128, 16).transpose(1, 0, 2))
    h["l0f_gbrep"] = np.broadcast_to(np.asarray(inputs["L0_ffn_gate_b"]), (128, 16)).copy()
    h["l0f_be1"] = np.asarray(inputs["L0_ffn_be1"])
    h["l0f_be3"] = np.asarray(inputs["L0_ffn_be3"])
    h["l0f_be2"] = np.asarray(inputs["L0_ffn_be2"])
    h["l0f_bs1"] = np.asarray(inputs["L0_ffn_bs1"])
    h["l0f_bs3"] = np.asarray(inputs["L0_ffn_bs3"])
    h["l0f_bs2"] = np.asarray(inputs["L0_ffn_bs2"])
    onehot = np.zeros((16, 16 * 128), dtype=np.float32)
    for e in range(16):
        onehot[e, e * 128:(e + 1) * 128] = 1.0
    h["l0f_onehot"] = onehot.astype(BF)
    return h


def declare_l0f(nc, kind_act="ExternalInput"):
    io = {}
    io["l0f_x"] = nc.dram_tensor("l0f_x", [T, 1024], F32, kind=kind_act)
    io["l0f_xbf"] = nc.dram_tensor("l0f_xbf", [T, 1024], BF16, kind="ExternalInput")
    io["l0f_y"] = nc.dram_tensor("l0f_y", [T, 1024], F32, kind="ExternalOutput")
    io["l0f_ybf"] = nc.dram_tensor("l0f_ybf", [T, 1024], BF16, kind="ExternalOutput")
    io["l0f_We1b"] = nc.dram_tensor("l0f_We1b", [16, 128, 8, 512], F32, kind="ExternalInput")
    io["l0f_We3b"] = nc.dram_tensor("l0f_We3b", [16, 128, 8, 512], F32, kind="ExternalInput")
    io["l0f_We2b"] = nc.dram_tensor("l0f_We2b", [16, 128, 4, 1024], F32, kind="ExternalInput")
    io["l0f_ws1b"] = nc.dram_tensor("l0f_ws1b", [128, 8, 1024], F32, kind="ExternalInput")
    io["l0f_ws3b"] = nc.dram_tensor("l0f_ws3b", [128, 8, 1024], F32, kind="ExternalInput")
    io["l0f_ws2b"] = nc.dram_tensor("l0f_ws2b", [128, 8, 1024], F32, kind="ExternalInput")
    io["l0f_gwb"] = nc.dram_tensor("l0f_gwb", [128, 8, 16], F32, kind="ExternalInput")
    io["l0f_gbrep"] = nc.dram_tensor("l0f_gbrep", [128, 16], F32, kind="ExternalInput")
    io["l0f_be1"] = nc.dram_tensor("l0f_be1", [16, 512], F32, kind="ExternalInput")
    io["l0f_be3"] = nc.dram_tensor("l0f_be3", [16, 512], F32, kind="ExternalInput")
    io["l0f_be2"] = nc.dram_tensor("l0f_be2", [16, 1024], F32, kind="ExternalInput")
    io["l0f_bs1"] = nc.dram_tensor("l0f_bs1", [1024], F32, kind="ExternalInput")
    io["l0f_bs3"] = nc.dram_tensor("l0f_bs3", [1024], F32, kind="ExternalInput")
    io["l0f_bs2"] = nc.dram_tensor("l0f_bs2", [1024], F32, kind="ExternalInput")
    io["l0f_onehot"] = nc.dram_tensor("l0f_onehot", [16, 16 * 128], BF16, kind="ExternalInput")
    io["ident_f32"] = nc.dram_tensor("ident_f32", [128, 128], F32, kind="ExternalInput")
    io["ident_bf16"] = nc.dram_tensor("ident_bf16", [128, 128], BF16, kind="ExternalInput")
    return io


def build_l0f_program():
    nc = bass.Bass("TRN2", target_bir_lowering=False)
    io = declare_l0f(nc)
    p = Prog(nc)
    emit_l0_ffn(p, io)
    return nc


INV_SQRT_HD = 1.0 / float(np.sqrt(128.0))


def emit_l1_attn(p: Prog, io):
    nc = p.nc
    tn, v, s, g, sy = nc.tensor, nc.vector, nc.scalar, nc.gpsimd, nc.sync
    x_full = io["l1a_x_full"].ap()
    xbf_full = io["l1a_xbf"].ap()
    y = io["l1a_y"].ap()
    ybf_dram = io["l1a_ybf"].ap()
    Wb = io["l1a_Wb"]
    tab = io["l1a_tab"]
    rbl = io["l1a_rblast"]
    idf = io["ident_f32"]
    idb = io["ident_bf16"]
    KT_scr = io["l1a_KT"]
    km_scr = io["l1a_km"]
    V_scr = io["l1a_V"]
    QT_scr = io["l1a_QT"]
    UT_scr = io["l1a_UT"]

    mm = p.sem("mm"); evS = p.sem("evS"); evV = p.sem("evV")
    trc = p.sem("trc")
    sV = p.sem("lnV"); sS = p.sem("lnS")
    preS = p.sem("a1pre"); preG = p.sem("a1preG")
    xtS = [p.sem("a1xt0"), p.sem("a1xt1")]
    kstS = [p.sem("a1kst0"), p.sem("a1kst1")]
    vstS = [p.sem("a1vst0"), p.sem("a1vst1")]
    ktlS = [p.sem("a1ktl0"), p.sem("a1ktl1"), p.sem("a1ktl2"), p.sem("a1ktl3")]
    vtlS = [p.sem("a1vtl0"), p.sem("a1vtl1"), p.sem("a1vtl2"), p.sem("a1vtl3")]
    qtlS = [p.sem("a1qtl0"), p.sem("a1qtl1")]
    utlS = [p.sem("a1utl0"), p.sem("a1utl1")]
    xrS = [p.sem("a1xr0"), p.sem("a1xr1")]
    ydS = [p.sem("a1yd0"), p.sem("a1yd1")]

    with ExitStack() as stk:
        def SB(*a):
            return stk.enter_context(nc.sbuf_tensor(*a))
        cpool = SB("a1cp", [128, 4], F32)
        idbs = SB("a1idb", [128, 128], BF16)
        idfs = SB("a1idf", [128, 128], F32)
        rbsb = SB("a1rbl", [128, 8], F32)
        onesb = SB("a1ones", [128, 1], BF16)
        stkP1 = stk.enter_context(ExitStack())

        def SBP1(*a):
            return stkP1.enter_context(nc.sbuf_tensor(*a))
        Wsb = SBP1("a1W", [128, 4 * 8 * 1024], BF16)
        xt = SBP1("a1xt", [128, 2 * 1024], BF16)
        xTs = SBP1("a1xT", [128, 2 * 8 * 512], BF16)
        kbuf = SBP1("a1kb", [128, 2 * 512], BF16)
        absb = SBP1("a1ab", [128, 512], BF16)
        kmrow = SBP1("a1kmr", [1, 512], F32)
        kmcols = SBP1("a1km", [128, 32], F32)
        vbuf = SBP1("a1vb", [128, 2 * 1024], BF16)
        qub = SBP1("a1qu", [128, 2 * 512], BF16)
        PS = stk.enter_context(nc.psum_tensor("a1ps", [128, 7 * 512], F32))

        def bank(i, w=512):
            return PS[:, i * 512:i * 512 + w]

        def bankbf(i):
            return PS[:, i * 512:(i + 1) * 512].bitcast(BF16)

        p.inc(v.memset(cpool[:, 0:1], LN_EPS), sV)
        p.inc(v.memset(cpool[:, 1:2], 1.0 / 1024.0), sV)
        p.inc(v.memset(onesb[:, :], 1.0), sV)
        p.dma(sy, preS, idbs[:, :], idb.ap())
        p.dma(sy, preS, idfs[:, :], idf.ap())
        p.dma(sy, preS, rbsb[:, :], rbl.ap())
        for wi in range(4):
            p.dma(g, preG, Wsb[:, wi * 8192:(wi + 1) * 8192],
                  bass.AP(Wb, wi * 128 * 8192, [[8192, 128], [1, 8192]]))
        pre_ready = preS.n
        preg_ready = preG.n
        p.wait(v, preS, pre_ready)
        p.inc(v.tensor_scalar_mul(rbsb[:, :], rbsb[:, :], INV_SQRT_HD), evV)

        xt_cnt = 0
        tr_free = []
        kst_cnt = [0, 0]
        vst_cnt = [0, 0]
        kml_cnt = [0, 0]
        kmcol_free = [0, 0]
        kq_free = []
        v_free = []

        pend_xt = [None]
        xtmm = []

        def flush_xT(p_):
            if pend_xt[0] is None:
                return
            k, ts_par = pend_xt[0]
            pend_xt[0] = None
            p_.wait(tn, xtS[k % 2])
            p_.wait(tn, preS, pre_ready)
            for c in range(8):
                i = k * 8 + c
                if len(tr_free) >= 2:
                    p_.wait(tn, trc, tr_free[-2])
                bv = bankbf(i % 2)
                inst = tn.transpose(bv[:, 0:128],
                                    xt[:, (k % 2) * 1024 + c * 128:(k % 2) * 1024 + (c + 1) * 128],
                                    idbs[:, :])
                tok = p_.inc(inst, mm)
                p_.wait(v, mm, tok)
                cp = v.tensor_copy(
                    xTs[:, ts_par * 4096 + c * 512 + (k % 4) * 128:ts_par * 4096 + c * 512 + (k % 4 + 1) * 128],
                    bv[:, 0:128])
                tr_free.append(p_.inc(cp, trc))
                if c == 7:
                    xtmm.append(tok)

        def build_xT(src_ap, k, ts_par):
            nonlocal xt_cnt
            if xtmm:
                p.wait(sy, mm, xtmm[-1])
            p.dma(sy, xtS[k % 2], xt[:, (k % 2) * 1024:(k % 2 + 1) * 1024],
                  src_ap[(k % 32768) * 128:(k % 32768 + 1) * 128, :])
            xt_cnt += 1
            flush_xT(p)
            pend_xt[0] = (k, ts_par)

        for fs in range(16):
            par = fs % 2
            if fs >= 2:
                p.wait(v, mm)
            for tt in range(4):
                build_xT(xbf_full, fs * 4 + tt, par)
            flush_xT(p)
            p.wait(tn, trc)
            p.wait(tn, preG, preg_ready)
            for h in range(8):
                bK = 2 + (h % 2)
                if h >= 2 or fs > 0:
                    p.wait(tn, evS, evS.n)
                for c in range(8):
                    inst = tn.matmul(bank(bK),
                                     Wsb[:, 1 * 8192 + c * 1024 + h * 128:1 * 8192 + c * 1024 + (h + 1) * 128],
                                     xTs[:, par * 4096 + c * 512:par * 4096 + (c + 1) * 512],
                                     start=(c == 0), stop=(c == 7))
                ktok = p.inc(inst, mm)
                p.wait(s, mm, ktok)
                if kst_cnt[h % 2] >= 1:
                    p.wait(s, kstS[h % 2], kst_cnt[h % 2] * 16)
                p.inc(s.activation(kbuf[:, (h % 2) * 512:(h % 2 + 1) * 512], bank(bK),
                                   AF.Silu, bias=0.0, scale=1.0), evS)
                p.wait(s, evS)
                p.inc(s.activation(absb[:, :], kbuf[:, (h % 2) * 512:(h % 2 + 1) * 512],
                                   AF.Abs, bias=0.0, scale=1.0), evS)
                p.inc(nc.vector.engine_nop(), evV)
                p.wait(tn, evS, evS.n)
                inst = tn.matmul(bank(6)[0:1, :], onesb[:, :], absb[:, :],
                                 start=True, stop=True)
                kmtok = p.inc(inst, mm)
                p.wait(s, mm, kmtok)
                p.wait(s, p.sem("a1kmS"))
                p.inc(s.activation(kmrow[:, :], bank(6)[0:1, :], AF.Sign,
                                   bias=0.0, scale=1.0), evS)
                p.wait(sy, evS)
                p.dma(sy, p.sem("a1kmS"),
                      bass.AP(km_scr, h * 8192 + fs * 512, [[512, 1], [1, 512]]),
                      kmrow[:, :])
                p.wait(g, evS)
                p.dma(g, kstS[h % 2],
                      bass.AP(KT_scr, h * 128 * 8192 + fs * 512, [[8192, 128], [1, 512]]),
                      kbuf[:, (h % 2) * 512:(h % 2 + 1) * 512])
                kst_cnt[h % 2] += 1
            for tt in range(4):
                bV = 2 + (tt % 2)
                p.wait(tn, evV, evV.n)
                p.wait(tn, evS, evS.n)
                for half in range(2):
                    for c in range(8):
                        inst = tn.matmul(bank(bV) if half == 0 else bank(4 + (tt % 2)),
                                         xTs[:, par * 4096 + c * 512 + tt * 128:par * 4096 + c * 512 + (tt + 1) * 128],
                                         Wsb[:, 2 * 8192 + c * 1024 + half * 512:2 * 8192 + c * 1024 + (half + 1) * 512],
                                         start=(c == 0), stop=(c == 7))
                vtok = p.inc(inst, mm)
                p.wait(s, mm, vtok)
                p.wait(s, evS)
                if vst_cnt[tt % 2] >= 1:
                    p.wait(s, vstS[tt % 2], vst_cnt[tt % 2] * 16)
                p.inc(s.activation(vbuf[:, (tt % 2) * 1024:(tt % 2) * 1024 + 512], bank(bV),
                                   AF.Silu, bias=0.0, scale=1.0), evS)
                p.inc(s.activation(vbuf[:, (tt % 2) * 1024 + 512:(tt % 2 + 1) * 1024],
                                   bank(4 + (tt % 2)), AF.Silu, bias=0.0, scale=1.0), evS)
                kmldS = p.sem("a1kml0") if tt % 2 == 0 else p.sem("a1kml1")
                kml_cnt[tt % 2] += 8
                p.wait(sy, p.sem("a1kmS"))
                p.wait(sy, evV, kmcol_free[tt % 2])
                for h in range(8):
                    p.dma(sy, kmldS, kmcols[:, (tt % 2) * 16 + h:(tt % 2) * 16 + h + 1],
                          bass.AP(km_scr, h * 8192 + (fs * 4 + tt) * 128, [[1, 128], [1, 1]]))
                p.wait(v, evS)
                p.wait(v, kmldS, kml_cnt[tt % 2] * 16)
                kmlast = None
                for h in range(8):
                    kmlast = p.inc(v.tensor_scalar(
                        vbuf[:, (tt % 2) * 1024 + h * 128:(tt % 2) * 1024 + (h + 1) * 128],
                        vbuf[:, (tt % 2) * 1024 + h * 128:(tt % 2) * 1024 + (h + 1) * 128],
                        kmcols[:, (tt % 2) * 16 + h:(tt % 2) * 16 + h + 1], None, ALU.mult), evV)
                kmcol_free[tt % 2] = kmlast
                p.wait(g, evV)
                p.dma(g, vstS[tt % 2],
                      bass.AP(V_scr, (fs * 4 + tt) * 128 * 1024, [[1024, 128], [1, 1024]]),
                      vbuf[:, (tt % 2) * 1024:(tt % 2 + 1) * 1024])
                vst_cnt[tt % 2] += 1
        for os_ in range(16):
            par = os_ % 2
            p.wait(v, mm)
            for tt in range(4):
                build_xT(xbf_full, os_ * 4 + tt, par)
            flush_xT(p)
            p.wait(tn, trc)
            for h in range(8):
                bQ = 2 + (h % 2)
                p.wait(tn, evS, evS.n)
                for c in range(8):
                    inst = tn.matmul(bank(bQ),
                                     Wsb[:, 0 * 8192 + c * 1024 + h * 128:0 * 8192 + c * 1024 + (h + 1) * 128],
                                     xTs[:, par * 4096 + c * 512:par * 4096 + (c + 1) * 512],
                                     start=(c == 0), stop=(c == 7))
                qtok = p.inc(inst, mm)
                p.wait(s, mm, qtok)
                if kst_cnt[h % 2] >= 1:
                    p.wait(s, kstS[h % 2], kst_cnt[h % 2] * 16)
                p.inc(s.activation(qub[:, (h % 2) * 512:(h % 2 + 1) * 512], bank(bQ),
                                   AF.Silu, bias=0.0, scale=1.0), evS)
                p.wait(g, evS)
                p.dma(g, kstS[h % 2],
                      bass.AP(QT_scr, h * 128 * 8192 + os_ * 512, [[8192, 128], [1, 512]]),
                      qub[:, (h % 2) * 512:(h % 2 + 1) * 512])
                kst_cnt[h % 2] += 1
            for h in range(8):
                bU = 2 + (h % 2)
                p.wait(tn, evS, evS.n)
                for c in range(8):
                    inst = tn.matmul(bank(bU),
                                     Wsb[:, 3 * 8192 + c * 1024 + h * 128:3 * 8192 + c * 1024 + (h + 1) * 128],
                                     xTs[:, par * 4096 + c * 512:par * 4096 + (c + 1) * 512],
                                     start=(c == 0), stop=(c == 7))
                utok = p.inc(inst, mm)
                p.wait(s, mm, utok)
                if kst_cnt[h % 2] >= 1:
                    p.wait(s, kstS[h % 2], kst_cnt[h % 2] * 16)
                p.inc(s.activation(qub[:, (h % 2) * 512:(h % 2 + 1) * 512], bank(bU),
                                   AF.Silu, bias=0.0, scale=1.0), evS)
                p.wait(g, evS)
                p.dma(g, kstS[h % 2],
                      bass.AP(UT_scr, h * 128 * 8192 + os_ * 512, [[8192, 128], [1, 512]]),
                      qub[:, (h % 2) * 512:(h % 2 + 1) * 512])
                kst_cnt[h % 2] += 1

        p.fence_all([sy, g, tn, v, s])
        stkP1.close()
        QTs = SB("a1QT", [128, 8 * 512], BF16)
        UTs = SB("a1UT", [128, 8 * 512], BF16)
        KTt = SB("a1KTt", [128, 4 * 4 * 128], BF16)
        Vt = SB("a1Vt", [128, 4 * 512], BF16)
        PTb = SB("a1PT", [128, 4 * 512], BF16)
        wtmp = SB("a1wt", [128, 4 * 512], BF16)
        outT = SB("a1oT", [128, 8 * 512], F32)
        yt = SB("a1yt", [128, 2 * 1024], F32)
        xr = SB("a1xr", [128, 2 * 1024], F32)
        sq = SB("a1sq", [128, 1024], F32)
        st = SB("a1st", [128, 4], F32)
        ybf = SB("a1ybf", [128, 2 * 1024], BF16)
        tabsb = SB("a1tab", [128, 8 * 2944], BF16)
        for h in range(8):
            p.dma(g, preG, tabsb[:, h * 2944:(h + 1) * 2944],
                  bass.AP(tab, h * 128 * 2944, [[2944, 128], [1, 2944]]))
        p.wait(v, preG)
        for h in range(8):
            p.inc(v.tensor_scalar_mul(tabsb[:, h * 2944:(h + 1) * 2944],
                                      tabsb[:, h * 2944:(h + 1) * 2944], INV_SQRT_HD), evV)
        tab_ready = evV.n
        engs = [tn, v, s, g, sy]
        pid = nc.partition_id(engines=[e.engine for e in engs])
        paridx = pid & 1
        idxs = [paridx] * len(engs)
        base_snap = p.snap()
        body_counts = []
        for bi in nc.Switch(engines=engs, index=idxs, n=2):
            p.restore(base_snap)
            p._waited = {}
            SUPS = SUP_EVEN if bi == 0 else SUP_ODD
            emit_l1a_body(p, SUPS, locals())
            body_counts.append({k: sm.n for k, sm in p.sems.items()})
        assert body_counts[0] == body_counts[1], "l1a switch body sem mismatch"
        p._waited = {}
        p.wait(g, ydS[0])
        p.wait(g, ydS[1])


def emit_l1a_body(p: Prog, SUPS, env):
    nc = p.nc
    tn, v, s, g, sy = nc.tensor, nc.vector, nc.scalar, nc.gpsimd, nc.sync
    mm = p.sem("mm"); evS = p.sem("evS"); evV = p.sem("evV")
    trc = p.sem("trc")
    sV = p.sem("lnV"); sS = p.sem("lnS")
    ktlS = [p.sem("a1ktl0"), p.sem("a1ktl1"), p.sem("a1ktl2"), p.sem("a1ktl3")]
    vtlS = [p.sem("a1vtl0"), p.sem("a1vtl1"), p.sem("a1vtl2"), p.sem("a1vtl3")]
    qtlS = [p.sem("a1qtl0"), p.sem("a1qtl1")]
    utlS = [p.sem("a1utl0"), p.sem("a1utl1")]
    xrS = [p.sem("a1xr0"), p.sem("a1xr1")]
    ydS = [p.sem("a1yd0"), p.sem("a1yd1")]
    kstS = [p.sem("a1kst0"), p.sem("a1kst1")]
    vstS = [p.sem("a1vst0"), p.sem("a1vst1")]
    (bank, bankbf, KTt, Vt, PTb, wtmp, outT, QTs, UTs, tabsb, rbsb, yt, xr,
     sq, st, cpool, idfs, KT_scr, V_scr, QT_scr, UT_scr, x_full, y, tab_ready) = (
        env["bank"], env["bankbf"], env["KTt"], env["Vt"], env["PTb"], env["wtmp"],
        env["outT"], env["QTs"], env["UTs"], env["tabsb"], env["rbsb"], env["yt"],
        env["xr"], env["sq"], env["st"], env["cpool"], env["idfs"], env["KT_scr"],
        env["V_scr"], env["QT_scr"], env["UT_scr"], env["x_full"], env["y"],
        env["tab_ready"])
    ybf = env["ybf"]; ybf_dram = env["ybf_dram"]
    ydbS = p.sem("ydb")
    ybf_cnt = [0]

    p.wait(sy, kstS[0]); p.wait(sy, kstS[1])
    p.wait(sy, vstS[0]); p.wait(sy, vstS[1])
    p.wait(v, p.sem("evV"), tab_ready)

    ktl_cnt = [0, 0, 0, 0]
    vtl_cnt = [0, 0, 0, 0]
    qtl_cnt = [0, 0]
    utl_cnt = [0, 0]
    qts_free_mm = 0
    uts_free = 0
    kt_free = []
    pt_free = []
    sb_free = []
    wtmp_free = []
    acc_free = 0
    xr_guard = []
    yd_cnt = [0, 0]
    eptr_free = []
    outT_free_mm = 0
    isc = [0]

    for qs in range(8):
        sup = SUPS[qs]
        q0 = sup * 512
        qpar = qs % 2
        nkt = 4 * (sup + 1)
        p.wait(sy, mm, qts_free_mm)
        for h in range(8):
            p.dma(sy, qtlS[qpar], QTs[:, h * 512:(h + 1) * 512],
                  bass.AP(QT_scr, h * 128 * 8192 + q0, [[8192, 128], [1, 512]]))
            p.dma(sy, utlS[qpar], UTs[:, h * 512:(h + 1) * 512],
                  bass.AP(UT_scr, h * 128 * 8192 + q0, [[8192, 128], [1, 512]]))
        qtl_cnt[qpar] += 8
        utl_cnt[qpar] += 8
        for hg in range(2):
            for kt in range(nkt):
                kpar = kt % 4
                d0 = q0 - kt * 128
                if len(kt_free) >= 4:
                    p.wait(g, mm, kt_free[-4])
                for hh in range(4):
                    h = hg * 4 + hh
                    p.dma(g, ktlS[kpar], KTt[:, kpar * 512 + hh * 128:kpar * 512 + (hh + 1) * 128],
                          bass.AP(KT_scr, h * 128 * 8192 + kt * 128, [[8192, 128], [1, 128]]))
                ktl_cnt[kpar] += 4
                if len(kt_free) >= 4:
                    p.wait(s, mm, kt_free[-4])
                p.dma(s, vtlS[kpar], Vt[:, kpar * 512:(kpar + 1) * 512],
                      bass.AP(V_scr, (kt * 128) * 1024 + hg * 512, [[1024, 128], [1, 512]]))
                vtl_cnt[kpar] += 1
                p.wait(tn, ktlS[kpar], ktl_cnt[kpar] * 16)
                p.wait(tn, vtlS[kpar], vtl_cnt[kpar] * 16)
                p.wait(tn, qtlS[qpar], qtl_cnt[qpar] * 16)
                stoks = []
                for hh in range(4):
                    h = hg * 4 + hh
                    bS = 4 + (isc[0] % 3)
                    isc[0] += 1
                    if len(sb_free) >= 3:
                        fsem, ftok = sb_free[-3]
                        p.wait(tn, fsem, ftok)
                    if kt == 0 and hh == 0:
                        p.wait(tn, evV, acc_free)
                        p.wait(tn, trc)
                    inst = tn.matmul(bank(bS),
                                     KTt[:, kpar * 512 + hh * 128:kpar * 512 + (hh + 1) * 128],
                                     QTs[:, h * 512:(h + 1) * 512],
                                     start=True, stop=True)
                    stoks.append(p.inc(inst, mm))
                    ppar = hh
                    if d0 <= 2048:
                        m0 = d0 + 384
                        p.wait(v, mm, stoks[-1])
                        if len(wtmp_free) >= 4:
                            p.wait(v, evS, wtmp_free[-4])
                        if len(pt_free) >= 4:
                            p.wait(v, mm, pt_free[-4])
                        stt = v.scalar_tensor_tensor(
                            wtmp[:, ppar * 512:(ppar + 1) * 512], bank(bS), INV_SQRT_HD,
                            tabsb[:, h * 2944 + m0:h * 2944 + m0 + 512],
                            ALU.mult, ALU.add)
                        sb_free.append((p.sem("evV"), p.inc(stt, evV)))
                        p.wait(s, evV)
                        if len(pt_free) >= 4:
                            p.wait(s, mm, pt_free[-4])
                        ac = s.activation(PTb[:, ppar * 512:(ppar + 1) * 512],
                                          wtmp[:, ppar * 512:(ppar + 1) * 512],
                                          AF.Silu, bias=0.0, scale=1.0)
                        wtmp_free.append(p.inc(ac, evS))
                    else:
                        p.wait(s, mm, stoks[-1])
                        if len(pt_free) >= 4:
                            p.wait(s, mm, pt_free[-4])
                        ac = s.activation(PTb[:, ppar * 512:(ppar + 1) * 512],
                                          bank(bS), AF.Silu,
                                          bias=rbsb[:, h:h + 1], scale=INV_SQRT_HD)
                        tk = p.inc(ac, evS)
                        wtmp_free.append(tk)
                        sb_free.append((p.sem("evS"), tk))
                        p.inc(nc.vector.engine_nop(), evV)
                last_av = None
                for hh in range(4):
                    p.wait(tn, evS, wtmp_free[-4 + hh])
                    inst = tn.matmul(bank(hh),
                                     Vt[:, kpar * 512 + hh * 128:kpar * 512 + (hh + 1) * 128],
                                     PTb[:, hh * 512:(hh + 1) * 512],
                                     start=(kt == 0), stop=(kt == nkt - 1))
                    last_av = p.inc(inst, mm)
                    pt_free.append(last_av)
                kt_free.append(last_av)
            p.wait(v, utlS[qpar], utl_cnt[qpar] * 16)
            p.wait(v, mm)
            p.wait(v, mm, outT_free_mm)
            for hh in range(4):
                h = hg * 4 + hh
                p.inc(v.tensor_tensor(outT[:, h * 512:(h + 1) * 512], bank(hh),
                                      UTs[:, h * 512:(h + 1) * 512], ALU.mult), evV)
            acc_free = evV.n
        out_ready = evV.n
        def xr_load(kk, ttk):
            if len(xr_guard) >= 1:
                p.wait(s, trc, xr_guard[-1])
            p.dma(s, xrS[kk % 2], xr[:, (kk % 2) * 1024:(kk % 2 + 1) * 1024],
                  x_full[q0 + ttk * 128:q0 + (ttk + 1) * 128, :])
        for tt in range(4):
            k = qs * 4 + tt
            if tt == 0:
                xr_load(k, tt)
            if tt + 1 < 4:
                xr_load(k + 1, tt + 1)
            p.wait(tn, evV, out_ready)
            for d in range(8):
                bt = (6, 5, 4)[len(eptr_free) % 3]
                if len(eptr_free) >= 3:
                    p.wait(tn, trc, eptr_free[-3])
                inst = tn.transpose(bank(bt, 128),
                                    outT[:, d * 512 + tt * 128:d * 512 + (tt + 1) * 128],
                                    idfs[:, :])
                ttok = p.inc(inst, mm)
                p.wait(v, mm, ttok)
                p.wait(v, xrS[k % 2], None)
                if yd_cnt[k % 2] >= 1 and d == 0:
                    p.wait(v, ydS[k % 2], yd_cnt[k % 2] * 16)
                add = v.tensor_tensor(
                    yt[:, (k % 2) * 1024 + d * 128:(k % 2) * 1024 + (d + 1) * 128],
                    bank(bt, 128),
                    xr[:, (k % 2) * 1024 + d * 128:(k % 2) * 1024 + (d + 1) * 128],
                    ALU.add)
                eptr_free.append(p.inc(add, trc))
            xr_guard.append(trc.n)
            p.wait(v, trc)
            p.wait(s, trc)
            ytile = yt[:, (k % 2) * 1024:(k % 2 + 1) * 1024]
            p.wait(s, sV)
            ln_tile(p, ytile, ytile, sq[:, :], st, sV, sS,
                    cpool[:, 0:1], cpool[:, 1:2])
            p.wait(g, sS)
            p.dma(g, ydS[k % 2], y[k * 128:(k + 1) * 128, :], ytile)
            yd_cnt[k % 2] += 1
            p.wait(v, sS)
            if ybf_cnt[0] >= 1:
                p.wait(v, ydbS)
            p.inc(v.tensor_copy(ybf[:, (k % 2) * 1024:(k % 2 + 1) * 1024], ytile), trc)
            p.wait(sy, trc)
            p.dma(sy, ydbS, ybf_dram[k * 128:(k + 1) * 128, :],
                  ybf[:, (k % 2) * 1024:(k % 2 + 1) * 1024])
            ybf_cnt[0] += 1
        outT_free_mm = mm.n
        qts_free_mm = mm.n
        uts_free = mm.n


def prep_host_l1a(inputs):
    h = {}
    h["ident_f32"] = np.eye(128, dtype=np.float32)
    h["ident_bf16"] = np.eye(128, dtype=np.float32).astype(BF)

    def blk(w):
        return np.ascontiguousarray(np.asarray(w).reshape(8, 128, 1024).transpose(1, 0, 2))

    h["l1a_Wb"] = np.ascontiguousarray(np.stack([
        blk(inputs["L1_attn_Wq"]), blk(inputs["L1_attn_Wk"]),
        blk(inputs["L1_attn_Wv"]), blk(inputs["L1_attn_Wu"])]))
    rb = np.asarray(inputs["L1_attn_rel_bias"])
    NEGBIG = np.float32(-1e30 * np.sqrt(128.0))
    tabv = np.empty((8, 128, 2944), dtype=np.float32)
    jj = np.arange(128)[:, None]
    mm_ = np.arange(2944)[None, :]
    d = mm_ - 384 - jj
    idx = np.minimum(d + 2047, 4094)
    idx_c = np.clip(idx, 0, 4094)
    for hh in range(8):
        vals = rb[idx_c, hh]
        tabv[hh] = np.where(d < 0, NEGBIG, vals)
    h["l1a_tab"] = tabv
    h["l1a_rblast"] = np.broadcast_to(rb[4094, :], (128, 8)).copy()
    return h


def declare_l1a(nc, kind_act="ExternalInput"):
    io = {}
    io["l1a_x_full"] = nc.dram_tensor("l1a_x_full", [T, 1024], F32, kind=kind_act)
    io["l1a_y"] = nc.dram_tensor("l1a_y", [HTOK, 1024], F32, kind="ExternalOutput")
    io["l1a_ybf"] = nc.dram_tensor("l1a_ybf", [HTOK, 1024], BF16, kind="ExternalOutput")
    io["l1a_xbf"] = nc.dram_tensor("l1a_xbf", [T, 1024], BF16, kind="ExternalInput")
    io["l1a_Wb"] = nc.dram_tensor("l1a_Wb", [4, 128, 8, 1024], F32, kind="ExternalInput")
    io["l1a_tab"] = nc.dram_tensor("l1a_tab", [8, 128, 2944], F32, kind="ExternalInput")
    io["l1a_rblast"] = nc.dram_tensor("l1a_rblast", [128, 8], F32, kind="ExternalInput")
    io["ident_f32"] = nc.dram_tensor("ident_f32", [128, 128], F32, kind="ExternalInput")
    io["ident_bf16"] = nc.dram_tensor("ident_bf16", [128, 128], BF16, kind="ExternalInput")
    io["l1a_KT"] = nc.dram_tensor("l1a_KT", [8 * 128 * 8192], BF16, kind="Internal")
    io["l1a_km"] = nc.dram_tensor("l1a_km", [8 * 8192], F32, kind="Internal")
    io["l1a_V"] = nc.dram_tensor("l1a_V", [8192 * 1024], BF16, kind="Internal")
    io["l1a_QT"] = nc.dram_tensor("l1a_QT", [8 * 128 * 8192], BF16, kind="Internal")
    io["l1a_UT"] = nc.dram_tensor("l1a_UT", [8 * 128 * 8192], BF16, kind="Internal")
    return io


def build_l1a_program():
    nc = bass.Bass("TRN2", target_bir_lowering=False)
    io = declare_l1a(nc)
    p = Prog(nc)
    emit_l1_attn(p, io)
    return nc


def emit_l0_attn(p: Prog, io):
    nc = p.nc
    tn, v, s, g, sy = nc.tensor, nc.vector, nc.scalar, nc.gpsimd, nc.sync
    x_full = io["l0a_x_full"].ap()
    y = io["l0a_y"].ap()
    ybf_dram = io["l0a_ybf"].ap()
    Wqkrb = io["l0a_Wqkr"]
    Wvb = io["l0a_Wv"]
    Wgb = io["l0a_Wg"]
    Wob = io["l0a_Wo"]
    glnr = io["l0a_glnrep"]
    uvc = io["l0a_uv"]
    maskc = io["l0a_mask"]
    STt = io["l0a_sin"]
    idf = io["ident_f32"]
    idb = io["ident_bf16"]
    v_scr = io["l0a_v"]
    gT_scr = io["l0a_gT"]
    c2_scr = io["l0a_c2"]
    ptp_scr = io["l0a_pt"]
    s_scr = io["l0a_s"]

    mm = p.sem("mm"); evS = p.sem("evS"); evV = p.sem("evV")
    trc = p.sem("trc")
    sV = p.sem("lnV"); sS = p.sem("lnS")
    preS = p.sem("a0pre"); preG = p.sem("a0preG")
    stS = [p.sem("a0st0"), p.sem("a0st1")]
    xtS = [p.sem("a0xt0"), p.sem("a0xt1")]
    vstS = [p.sem("a0vst0"), p.sem("a0vst1")]
    gstS = [p.sem("a0gst0"), p.sem("a0gst1")]
    c2S = [p.sem("a0c20"), p.sem("a0c21"), p.sem("a0c22"), p.sem("a0c23")]
    bdS = [p.sem("a0bd0"), p.sem("a0bd1")]
    vldS = [p.sem("a0vld0"), p.sem("a0vld1")]
    gldS = [p.sem("a0gld0"), p.sem("a0gld1")]
    xrS = [p.sem("a0xr0"), p.sem("a0xr1")]
    ydS = [p.sem("a0yd0"), p.sem("a0yd1")]

    with ExitStack() as stk:
        def SB(*a):
            return stk.enter_context(nc.sbuf_tensor(*a))
        cpool = SB("a0cp", [128, 4], F32)
        idbs = SB("a0idb", [128, 128], BF16)
        idfs = SB("a0idf", [128, 128], F32)
        onescol = SB("a0on1", [128, 1], BF16)
        onesrow = SB("a0on2", [1, 128], BF16)
        uvs = SB("a0uv", [128, 2], F32)
        masks = SB("a0msk", [128, 4 * 512], F32)
        wqkr = SB("a0wqkr", [128, 3 * 1024], BF16)
        kT = SB("a0kT", [128, 8192], BF16)
        quT = SB("a0quT", [128, 8192], BF16)
        qvT = SB("a0qvT", [128, 8192], BF16)
        rT = SB("a0rT", [128, 9216], BF16)
        PS = stk.enter_context(nc.psum_tensor("a0ps", [128, 7 * 512], F32))
        stkP = stk.enter_context(ExitStack())

        def SBP(*a):
            return stkP.enter_context(nc.sbuf_tensor(*a))
        glns = SBP("a0gln", [128, 1024], F32)
        wvs = SBP("a0wv", [128, 8 * 2048], BF16)
        wgs = SBP("a0wg", [128, 8 * 2048], BF16)
        qT = SBP("a0qT", [128, 8192], BF16)

        def bank(i, w=512):
            return PS[:, i * 512:i * 512 + w]

        def bankbf(i):
            return PS[:, i * 512:(i + 1) * 512].bitcast(BF16)

        p.inc(v.memset(cpool[:, 0:1], LN_EPS), sV)
        p.inc(v.memset(cpool[:, 1:2], 1.0 / 1024.0), sV)
        p.inc(v.memset(cpool[:, 2:3], 1e-6), sV)
        p.inc(v.memset(onescol[:, :], 1.0), sV)
        p.inc(v.memset(onesrow[:, :], 1.0), sV)
        p.inc(v.memset(rT[:, 8192:9216], 0.0), sV)
        p.dma(sy, preS, idbs[:, :], idb.ap())
        p.dma(sy, preS, idfs[:, :], idf.ap())
        p.dma(sy, preS, glns[:, :], glnr.ap())
        p.dma(sy, preS, uvs[:, :], uvc.ap())
        for qt in range(4):
            p.dma(sy, preS, masks[:, qt * 512:(qt + 1) * 512],
                  bass.AP(maskc, qt * 128 * 512, [[512, 128], [1, 512]]))
        p.dma(g, preG, wqkr[:, :], bass.AP(Wqkrb, 0, [[1024, 128], [128 * 1024, 3], [1, 1024]]))
        p.dma(g, preG, wvs[:, :], bass.AP(Wvb, 0, [[8 * 2048, 128], [1, 8 * 2048]]))
        p.dma(g, preG, wgs[:, :], bass.AP(Wgb, 0, [[8 * 2048, 128], [1, 8 * 2048]]))
        pre_ready = preS.n
        preg_ready = preG.n

        with nc.sbuf_tensor("a0stb", [128, 2 * 8 * 512], BF16) as stb:
            st_cnt = [0, 0]
            for mc in range(16):
                par = mc % 2
                if st_cnt[par] >= 8:
                    p.wait(g, trc)
                for c in range(8):
                    p.dma(g, stS[par], stb[:, par * 4096 + c * 512:par * 4096 + (c + 1) * 512],
                          STt.ap()[c * 128:(c + 1) * 128, mc * 512:(mc + 1) * 512])
                st_cnt[par] += 8
                p.wait(tn, stS[par], st_cnt[par] * 16)
                p.wait(tn, preG, preg_ready)
                bR = mc % 2
                if mc >= 2:
                    p.wait(tn, trc)
                for c in range(8):
                    inst = tn.matmul(bank(bR), wqkr[:, 2 * 1024 + c * 128:2 * 1024 + (c + 1) * 128],
                                     stb[:, par * 4096 + c * 512:par * 4096 + (c + 1) * 512],
                                     start=(c == 0), stop=(c == 7))
                rtok = p.inc(inst, mm)
                p.wait(v, mm, rtok)
                p.inc(v.tensor_copy(rT[:, mc * 512:(mc + 1) * 512], bank(bR)), trc)

        p.fence_all([sy, g, tn, v, s])
        xt = SBP("a0xtb", [128, 2 * 1024], BF16)
        xfb = SBP("a0xf", [128, 2 * 1024], F32)
        xls = SBP("a0xls", [128, 2 * 8 * 512], BF16)
        rmss = SBP("a0rms", [128, 4], F32)
        vbuf = SBP("a0vb", [128, 2 * 2048], BF16)
        gbuf = SBP("a0gb", [128, 2 * 512], BF16)
        sqs = SBP("a0sq", [128, 1024], F32)

        xt_cnt = 0
        tr_free = []
        vst_cnt = [0, 0]
        gst_cnt = [0, 0]

        pend_xn = [None]
        xn_tok = []

        def flush_xnorm(p_):
            if pend_xn[0] is None:
                return
            kp, par, k, xtok = pend_xn[0]
            pend_xn[0] = None
            p_.wait(tn, evV, xtok)
            for c in range(8):
                if len(tr_free) >= 2:
                    p_.wait(tn, trc, tr_free[-2])
                bv = bankbf(2 + (c % 2))
                inst = tn.transpose(bv[:, 0:128],
                                    xt[:, kp * 1024 + c * 128:kp * 1024 + (c + 1) * 128],
                                    idbs[:, :])
                tok = p_.inc(inst, mm)
                p_.wait(v, mm, tok)
                cp = v.tensor_copy(
                    xls[:, par * 4096 + c * 512 + (k % 4) * 128:par * 4096 + c * 512 + (k % 4 + 1) * 128],
                    bv[:, 0:128])
                tr_free.append(p_.inc(cp, trc))

        def build_xnorm(src_ap, k, par):
            nonlocal xt_cnt
            kp = k % 2
            if len(xn_tok) >= 2:
                p.wait(sy, evV, xn_tok[-2])
            p.dma(sy, xtS[kp], xfb[:, kp * 1024:(kp + 1) * 1024],
                  src_ap[k * 128:(k + 1) * 128, :])
            xt_cnt += 1
            p.wait(s, xtS[kp], None)
            p.wait(s, preS, pre_ready)
            p.wait(s, sV)
            p.wait(s, evV, evV.n)
            p.inc(s.activation(sqs[:, :], xfb[:, kp * 1024:(kp + 1) * 1024],
                               AF.Square, bias=0.0, scale=1.0), evS)
            p.wait(v, evS)
            p.inc(v.reduce_sum(rmss[:, 0:1], sqs[:, :], axis=AX.X), evV)
            p.wait(s, evV)
            p.inc(s.activation(rmss[:, 1:2], rmss[:, 0:1], AF.Sqrt,
                               bias=cpool[:, 2:3], scale=cpool[:, 1:2]), evS)
            p.wait(v, evS)
            p.inc(v.reciprocal(rmss[:, 2:3], rmss[:, 1:2]), evV)
            p.wait(v, evV)
            stt = v.scalar_tensor_tensor(xt[:, kp * 1024:(kp + 1) * 1024],
                                         xfb[:, kp * 1024:(kp + 1) * 1024],
                                         rmss[:, 2:3], glns[:, :],
                                         ALU.mult, ALU.mult)
            xtok = p.inc(stt, evV)
            xn_tok.append(xtok)
            flush_xnorm(p)
            pend_xn[0] = (kp, par, k, xtok)

        for fs in range(16):
            par = fs % 2
            if fs >= 2:
                p.wait(v, mm)
            for tt in range(4):
                build_xnorm(x_full, fs * 4 + tt, par)
            flush_xnorm(p)
            p.wait(tn, trc)
            bK = fs % 2
            p.wait(tn, evV, evV.n)
            for c in range(8):
                inst = tn.matmul(bank(bK), wqkr[:, 1 * 1024 + c * 128:1 * 1024 + (c + 1) * 128],
                                 xls[:, par * 4096 + c * 512:par * 4096 + (c + 1) * 512],
                                 start=(c == 0), stop=(c == 7))
            ktok = p.inc(inst, mm)
            p.wait(v, mm, ktok)
            p.inc(v.tensor_copy(kT[:, fs * 512:(fs + 1) * 512], bank(bK)), trc)
            bQ = 1 - (fs % 2)
            for c in range(8):
                inst = tn.matmul(bank(bQ), wqkr[:, 0 * 1024 + c * 128:0 * 1024 + (c + 1) * 128],
                                 xls[:, par * 4096 + c * 512:par * 4096 + (c + 1) * 512],
                                 start=(c == 0), stop=(c == 7))
            qtok = p.inc(inst, mm)
            p.wait(v, mm, qtok)
            p.inc(v.tensor_copy(qT[:, fs * 512:(fs + 1) * 512], bank(bQ)), trc)
            for vc in range(16):
                bG = 4 + (vc % 2)
                p.wait(tn, evS, evS.n)
                for c in range(8):
                    inst = tn.matmul(bank(bG),
                                     wgs[:, c * 2048 + vc * 128:c * 2048 + (vc + 1) * 128],
                                     xls[:, par * 4096 + c * 512:par * 4096 + (c + 1) * 512],
                                     start=(c == 0), stop=(c == 7))
                gtok = p.inc(inst, mm)
                p.wait(s, mm, gtok)
                gp = vc % 2
                if gst_cnt[gp] >= 1:
                    p.wait(s, gstS[gp], gst_cnt[gp] * 16)
                p.inc(s.activation(gbuf[:, gp * 512:(gp + 1) * 512], bank(bG),
                                   AF.Silu, bias=0.0, scale=1.0), evS)
                p.wait(g, evS)
                p.dma(g, gstS[gp],
                      bass.AP(gT_scr, vc * 128 * T + fs * 512, [[T, 128], [1, 512]]),
                      gbuf[:, gp * 512:(gp + 1) * 512])
                gst_cnt[gp] += 1
            for tt in range(4):
                tp = tt % 2
                p.wait(tn, evS, evS.n)
                for half in range(4):
                    bV = (4, 5, 4, 5)[half]
                    if half >= 2:
                        p.wait(tn, evS, evS.n)
                    for c in range(8):
                        inst = tn.matmul(bank(bV),
                                         xls[:, par * 4096 + c * 512 + tt * 128:par * 4096 + c * 512 + (tt + 1) * 128],
                                         wvs[:, c * 2048 + half * 512:c * 2048 + (half + 1) * 512],
                                         start=(c == 0), stop=(c == 7))
                    vtok = p.inc(inst, mm)
                    p.wait(s, mm, vtok)
                    if vst_cnt[tp] >= 1 and half == 0:
                        p.wait(s, vstS[tp], vst_cnt[tp] * 16)
                    p.inc(s.activation(vbuf[:, tp * 2048 + half * 512:tp * 2048 + (half + 1) * 512],
                                       bank(bV), AF.Identity, bias=0.0, scale=1.0), evS)
                p.wait(g, evS)
                p.dma(g, vstS[tp],
                      bass.AP(v_scr, (fs * 4 + tt) * 128 * 2048, [[2048, 128], [1, 2048]]),
                      vbuf[:, tp * 2048:(tp + 1) * 2048])
                vst_cnt[tp] += 1
        p.wait(s, trc)
        p.wait(s, preS, pre_ready)
        p.inc(s.activation(quT[:, :], qT[:, :], AF.Identity,
                           bias=uvs[:, 0:1], scale=INV_SQRT_HD), evS)
        p.inc(s.activation(qvT[:, :], qT[:, :], AF.Identity,
                           bias=uvs[:, 1:2], scale=INV_SQRT_HD), evS)
        qready = evS.n

        p.fence_all([sy, g, tn, v, s])
        stkP.close()
        wos = SB("a0wo", [128, 16 * 1024], BF16)
        p.dma(g, preG, wos[:, :], bass.AP(Wob, 0, [[16 * 1024, 128], [1, 16 * 1024]]))
        wos_ready = preG.n
        PTkc = SB("a0PT", [128, 4 * 512], BF16)
        ptv = SB("a0ptv", [128, 4 * 512], BF16)
        cbuf = SB("a0cb", [128, 4 * 512], F32)
        bdb = SB("a0bd", [128, 4 * 512], F32)
        sld = SB("a0sld", [128, 4 * 512], F32)
        stile = SB("a0stile", [128, 2 * 1024], F32)
        Pbuf = SB("a0P", [128, 2 * 512], BF16)
        rmb = SB("a0rm", [128, 12], F32)
        lrow = SB("a0lr", [1, 512], F32)
        lrbf = SB("a0lrb", [1, 512], BF16)
        lacc = SB("a0lacc", [128, 4], F32)
        lpart = SB("a0lp", [128, 2], F32)
        lr4 = SB("a0lr4", [4, 128], F32)
        lr4b = SB("a0lr4b", [4, 128], BF16)
        linv = SB("a0li", [128, 512], F32)
        vtile = SB("a0vt", [128, 4 * 512], BF16)
        gts = SB("a0gts", [128, 2 * 512], BF16)
        gated = SB("a0gated", [128, 16 * 512], BF16)
        tmpf = SB("a0tmpf", [128, 512], F32)
        yt = SB("a0yt", [128, 2 * 1024], F32)
        xr = SB("a0xr", [128, 2 * 1024], F32)
        sq2 = SB("a0sq2", [128, 1024], F32)
        st2 = SB("a0st2", [128, 4], F32)
        ybf = SB("a0ybf", [128, 1024], BF16)

        env = dict(locals())
        emit_l0a_body(p, list(range(16)), env)
        p.wait(g, ydS[0])
        p.wait(g, ydS[1])


def emit_l0a_body(p: Prog, SUPS, env):
    nc = p.nc
    tn, v, s, g, sy = nc.tensor, nc.vector, nc.scalar, nc.gpsimd, nc.sync
    mm = p.sem("mm"); evS = p.sem("evS"); evV = p.sem("evV")
    trc = p.sem("trc")
    sV = p.sem("lnV"); sS = p.sem("lnS")
    c2S = [p.sem("a0c20"), p.sem("a0c21"), p.sem("a0c22"), p.sem("a0c23")]
    bdS = [p.sem("a0bd0"), p.sem("a0bd1")]
    vldS = [p.sem("a0vld0"), p.sem("a0vld1")]
    gldS = [p.sem("a0gld0"), p.sem("a0gld1")]
    xrS = [p.sem("a0xr0"), p.sem("a0xr1")]
    ydS = [p.sem("a0yd0"), p.sem("a0yd1")]
    E = env
    bank = E["bank"]; bankbf = E["bankbf"]
    quT = E["quT"]; qvT = E["qvT"]; kT = E["kT"]; rT = E["rT"]
    PTkc = E["PTkc"]; ptv = E["ptv"]; ptp_scr = E["ptp_scr"]; s_scr = E["s_scr"]
    sld = E["sld"]; lacc = E["lacc"]; lpart = E["lpart"]; lr4 = E["lr4"]; lr4b = E["lr4b"]
    cbuf = E["cbuf"]; bdb = E["bdb"]; stile = E["stile"]
    Pbuf = E["Pbuf"]; rmb = E["rmb"]; lrow = E["lrow"]; lrbf = E["lrbf"]
    linv = E["linv"]; vtile = E["vtile"]; gts = E["gts"]; gated = E["gated"]
    tmpf = E["tmpf"]; yt = E["yt"]; xr = E["xr"]; sq2 = E["sq2"]; st2 = E["st2"]
    masks = E["masks"]; onescol = E["onescol"]; onesrow = E["onesrow"]
    idfs = E["idfs"]; idbs = E["idbs"]; cpool = E["cpool"]; wos = E["wos"]
    v_scr = E["v_scr"]; gT_scr = E["gT_scr"]; c2_scr = E["c2_scr"]
    x_own = E["x_full"]; y = E["y"]; qready = E["qready"]; pre_ready = E["pre_ready"]
    ybf = E["ybf"]; ybf_dram = E["ybf_dram"]
    ydbS = p.sem("ydb")
    ybf_cnt = [0]

    p.wait(tn, evS, qready)
    p.wait(v, p.sem("a0pre"), pre_ready)
    scp = p.sem("a0scp")
    c2_cnt = [0, 0, 0, 0]
    ss_cnt = [0, 0]
    bd_cnt = [0, 0]
    ibB = 0
    lp_free = {}
    lr_cnt = [0]
    exp_free = []
    sld_cnt = [0, 0, 0, 0]
    pexp_free = []
    ptcopy = []
    ptst_cnt = [0, 0]
    ptld_cnt = [0, 0]
    vld_cnt = [0, 0]
    ptld_cnt4 = [0, 0]
    gld_cnt = [0, 0]
    xr_guard = []
    yd_cnt = [0, 0]
    cb_free = []
    bd_done_evV = 0
    pt_free_mm = 0

    for qs, sup in enumerate(SUPS):
        q0 = sup * 512
        nkc = sup + 1
        m_lo = 8192 - 512 - q0

        p.wait(tn, evV, bd_done_evV)
        for qq in range(4):
            for mc in range(sup + 2):
                i = qq * (sup + 2) + mc
                b = i % 4
                if len(cb_free) >= 4:
                    fs_, ft_ = cb_free[-4]
                    p.wait(tn, fs_, ft_)
                inst = tn.matmul(bank(b),
                                 qvT[:, qs * 512 + qq * 128:qs * 512 + (qq + 1) * 128],
                                 rT[:, m_lo + mc * 512:m_lo + (mc + 1) * 512],
                                 start=True, stop=True)
                ctok = p.inc(inst, mm)
                ceng = v if b % 2 == 0 else s
                csem = evV if b % 2 == 0 else scp
                p.wait(ceng, mm, ctok)
                p.wait(ceng, evV, bd_done_evV)
                if c2_cnt[b] >= 1:
                    p.wait(ceng, c2S[b], c2_cnt[b] * 16)
                cp = ceng.tensor_copy(cbuf[:, b * 512:(b + 1) * 512], bank(b)) if b % 2 == 0 \
                    else s.activation(cbuf[:, b * 512:(b + 1) * 512], bank(b), AF.Identity,
                                      bias=0.0, scale=1.0)
                cb_free.append((csem, p.inc(cp, csem)))
                p.wait(g, csem)
                p.dma(g, c2S[b],
                      bass.AP(c2_scr, (qq * 128) * 8704 + mc * 512, [[8704, 128], [1, 512]]),
                      cbuf[:, b * 512:(b + 1) * 512])
                c2_cnt[b] += 1
        strip_done = [c2S[0].n, c2S[1].n, c2S[2].n, c2S[3].n]

        p.wait(tn, evV)
        p.wait(tn, scp)
        p.wait(v, evS)
        p.inc(v.memset(rmb[:, 0:8], -3.0e38), sV)
        p.wait(v, sV)
        ttr_free = []
        ssS = [p.sem("a0ss0"), p.sem("a0ss1")]
        groups = []
        for qt in range(4):
            kc0 = 0
            while kc0 < nkc:
                w = 2 if kc0 + 1 < nkc else 1
                groups.append((qt, kc0, w))
                kc0 += w
        if True:
            for gi, (qt, kc0, w) in enumerate(groups):
                bp = (gi % 2) * 2
                if len(ttr_free) >= 2:
                    p.wait(tn, evV, ttr_free[-2])
                atok = None
                for j in range(w):
                    inst = tn.matmul(bank(bp + j),
                                     quT[:, qs * 512 + qt * 128:qs * 512 + (qt + 1) * 128],
                                     kT[:, (kc0 + j) * 512:(kc0 + j + 1) * 512],
                                     start=True, stop=True)
                    atok = p.inc(inst, mm)
                p.wait(sy, c2S[0], strip_done[0])
                p.wait(sy, c2S[1], strip_done[1])
                p.wait(sy, c2S[2], strip_done[2])
                p.wait(sy, c2S[3], strip_done[3])
                if len(ttr_free) >= 2:
                    p.wait(sy, evV, ttr_free[-2])
                for j in range(w):
                    p.dma(sy, bdS[gi % 2],
                          bdb[:, (bp + j) * 512:(bp + j + 1) * 512],
                          bass.AP(c2_scr, (qt * 128) * 8704 + (511 - 128 * qt) + (kc0 + j) * 512,
                                  [[8703, 128], [1, 512]]))
                    bd_cnt[gi % 2] += 1
                p.wait(v, bdS[gi % 2], bd_cnt[gi % 2] * 16)
                p.wait(v, mm, atok)
                if kc0 + w == nkc:
                    dslot = bp + w - 1
                    p.inc(v.tensor_tensor(bdb[:, dslot * 512:(dslot + 1) * 512],
                                          bdb[:, dslot * 512:(dslot + 1) * 512],
                                          masks[:, qt * 512:(qt + 1) * 512], ALU.add), evV)
                    p.wait(v, evV)
                else:
                    p.inc(nc.vector.engine_nop(), evV)
                W = w * 512
                spar = gi % 2
                p.wait(v, ssS[spar], ss_cnt[spar] * 16)
                tta = v.tensor_tensor(stile[:, 0:W] if spar == 0 else stile[:, 1024:1024 + W],
                                      bank(bp, W), bdb[:, bp * 512:bp * 512 + W], ALU.add)
                ttr_free.append(p.inc(tta, evV))
                p.wait(g, evV)
                p.dma(g, ssS[spar],
                      bass.AP(s_scr, (qt * 128) * 8192 + kc0 * 512, [[8192, 128], [1, W]]),
                      stile[:, 0:W] if spar == 0 else stile[:, 1024:1024 + W])
                ss_cnt[spar] += 1
                p.wait(v, evV)
                p.inc(v.reduce_max(rmb[:, 4:5], stile[:, 0:W] if spar == 0 else stile[:, 1024:1024 + W],
                                   axis=AX.X), evV)
                p.wait(v, evV)
                p.inc(v.tensor_tensor(rmb[:, qt:qt + 1], rmb[:, qt:qt + 1],
                                      rmb[:, 4:5], ALU.max), evV)
                p.wait(v, evV)
        for qt in range(4):
            p.inc(v.tensor_scalar_mul(rmb[:, 8 + qt:9 + qt],
                                      rmb[:, qt:qt + 1], -1.0), evV)
        p.wait(v, evV)
        negm_ready = evV.n

        p.wait(v, mm)
        p.inc(v.memset(lacc[:, 0:4], 0.0), sV)
        p.wait(v, sV)

        ptS = [p.sem("a0pt0"), p.sem("a0pt1")]
        lmm_last = None
        pendingB = None

        def do_transposes(j, bj, qtj, exptok):
            p.wait(tn, evS, exptok)
            for kb in range(4):
                jj = j * 4 + kb
                bt = (2, 3, 0, 1)[jj % 4]
                if len(ptcopy) >= 4:
                    fs_, ft_ = ptcopy[-4]
                    p.wait(tn, fs_, ft_)
                inst2 = tn.transpose(bankbf(bt)[:, 0:128],
                                     Pbuf[:, bj * 512 + kb * 128:bj * 512 + (kb + 1) * 128],
                                     idbs[:, :])
                ttok2 = p.inc(inst2, mm)
                if kb == 3:
                    pexp_free.append(ttok2)
                ceng = v if kb % 2 == 0 else s
                csem = trc if kb % 2 == 0 else scp
                p.wait(ceng, mm, ttok2)
                p.wait(ceng, ptS[kb % 2], ptst_cnt[kb % 2] * 16)
                if kb % 2 == 0:
                    cp = v.tensor_copy(
                        PTkc[:, kb * 512 + qtj * 128:kb * 512 + (qtj + 1) * 128],
                        bankbf(bt)[:, 0:128])
                else:
                    cp = s.activation(
                        PTkc[:, kb * 512 + qtj * 128:kb * 512 + (qtj + 1) * 128],
                        bankbf(bt)[:, 0:128], AF.Identity, bias=0.0, scale=1.0)
                ptcopy.append((csem, p.inc(cp, csem)))

        p.wait(tn, evV)
        sldS = [p.sem("a0sl0"), p.sem("a0sl1"), p.sem("a0sl2"), p.sem("a0sl3")]
        for kc in range(nkc):
            for qt in range(4):
                i = kc * 4 + qt
                b = ibB % 4
                bb = ibB % 2
                ibB += 1
                if len(exp_free) >= 4:
                    p.wait(sy, evS, exp_free[-4])
                p.wait(sy, ssS[0], ss_cnt[0] * 16)
                p.wait(sy, ssS[1], ss_cnt[1] * 16)
                p.dma(sy, sldS[b], sld[:, b * 512:(b + 1) * 512],
                      bass.AP(s_scr, (qt * 128) * 8192 + kc * 512, [[8192, 128], [1, 512]]))
                sld_cnt[b] += 1
                p.wait(s, sldS[b], sld_cnt[b] * 16)
                p.wait(s, evV, negm_ready)
                if len(pexp_free) >= 2:
                    p.wait(s, mm, pexp_free[-2])
                p.wait(s, evV, lp_free.get(bb, 0))
                ex = s.activation(Pbuf[:, bb * 512:(bb + 1) * 512],
                                  sld[:, b * 512:(b + 1) * 512], AF.Exp,
                                  bias=rmb[:, 8 + qt:9 + qt], scale=1.0,
                                  accum_out=lpart[:, bb:bb + 1])
                exp_free.append(p.inc(ex, evS))
                p.wait(v, evS, exp_free[-1])
                p.wait(v, sV)
                lp_free[bb] = p.inc(v.tensor_tensor(lacc[:, qt:qt + 1], lacc[:, qt:qt + 1],
                                                    lpart[:, bb:bb + 1], ALU.add), evV)
                p.wait(v, evV)
                if pendingB is not None:
                    do_transposes(*pendingB)
                pendingB = (i, bb, qt, exp_free[-1])
            if pendingB is not None:
                do_transposes(*pendingB)
                pendingB = None
            for kb in range(4):
                fs_, ft_ = ptcopy[-4 + kb]
                p.wait(g, fs_, ft_)
                p.dma(g, ptS[kb % 2],
                      bass.AP(ptp_scr, (kc * 4 + kb) * 128 * 512, [[512, 128], [1, 512]]),
                      PTkc[:, kb * 512:(kb + 1) * 512])
                ptst_cnt[kb % 2] += 1
        p.wait(tn, evV)
        inst = tn.transpose(bank(6)[0:4, 0:128], lacc[:, 0:4], idfs[:, :])
        btok = p.inc(inst, mm)
        p.wait(v, mm, btok)
        p.inc(v.reciprocal(lr4[0:4, :], bank(6)[0:4, 0:128]), evV)
        p.wait(v, evV)
        p.inc(v.tensor_copy(lr4b[0:4, :], lr4[0:4, :]), evV)
        p.wait(sy, evV)
        p.dma(sy, p.sem("a0lrS"), bass.AP(lrbf[:, :].tensor, 0, [[512, 1], [1, 512]]),
              bass.AP(lr4b[0:4, :].tensor, 0, [[128, 4], [1, 128]]))
        lr_cnt[0] += 1
        p.wait(tn, p.sem("a0lrS"), lr_cnt[0] * 16)
        inst = tn.matmul(bank(6), onesrow[:, :], lrbf[:, :], start=True, stop=True)
        btok = p.inc(inst, mm)
        p.wait(v, mm, btok)
        p.inc(v.tensor_copy(linv[:, :], bank(6)), evV)
        linv_ready = evV.n

        pv_free = []
        gate_done = []
        tsl_tok = {0: 0, 1: 0}
        ptlS = [p.sem("a0ptl0"), p.sem("a0ptl1")]
        nkt = nkc * 4
        for vq in range(4):
            p.wait(tn, evV)
            for kt2 in range(0, nkt, 2):
                kp = (kt2 // 2) % 2
                if len(pv_free) >= 2:
                    p.wait(g, mm, pv_free[-2])
                p.dma(g, vldS[kp],
                      vtile[:, kp * 1024:(kp + 1) * 1024],
                      bass.AP(v_scr, kt2 * 128 * 2048 + vq * 512,
                              [[2048, 128], [262144, 2], [1, 512]]))
                vld_cnt[kp] += 1
                if len(pv_free) >= 2:
                    p.wait(s, mm, pv_free[-2])
                p.wait(s, ptS[0]); p.wait(s, ptS[1])
                p.dma(s, ptlS[kp],
                      ptv[:, kp * 1024:(kp + 1) * 1024],
                      bass.AP(ptp_scr, kt2 * 128 * 512, [[512, 128], [65536, 2], [1, 512]]))
                ptld_cnt4[kp] += 1
                p.wait(tn, vldS[kp], vld_cnt[kp] * 16)
                p.wait(tn, ptlS[kp], ptld_cnt4[kp] * 16)
                inst = None
                for j in range(2):
                    kt = kt2 + j
                    for vc in range(4):
                        inst = tn.matmul(bank(vc),
                                         vtile[:, kp * 1024 + j * 512 + vc * 128:kp * 1024 + j * 512 + (vc + 1) * 128],
                                         ptv[:, kp * 1024 + j * 512:kp * 1024 + (j + 1) * 512],
                                         start=(kt == 0), stop=(kt == nkt - 1))
                pv_free.append(p.inc(inst, mm))
            pv_done = mm.n
            for vc in range(4):
                vcg = vq * 4 + vc
                gp = vcg % 2
                tp_ = vc % 2
                tslot = sld[:, tp_ * 512:(tp_ + 1) * 512]
                if len(gate_done) >= 2:
                    p.wait(sy, scp, gate_done[-2])
                p.dma(sy, gldS[gp],
                      gts[:, gp * 512:(gp + 1) * 512],
                      bass.AP(gT_scr, vcg * 128 * T + qs * 512, [[T, 128], [1, 512]]))
                gld_cnt[gp] += 1
                p.wait(v, mm, pv_done)
                p.wait(v, evV, linv_ready)
                if tsl_tok[tp_]:
                    p.wait(v, scp, tsl_tok[tp_])
                tm = v.tensor_tensor(tslot, bank(vc), linv[:, :], ALU.mult)
                ttok = p.inc(tm, evV)
                p.wait(g, evV, ttok)
                p.wait(g, gldS[gp], gld_cnt[gp] * 16)
                mul2 = g.tensor_tensor(gated[:, vcg * 512:(vcg + 1) * 512],
                                       tslot, gts[:, gp * 512:(gp + 1) * 512], ALU.mult)
                tsl_tok[tp_] = p.inc(mul2, scp)
                gate_done.append(tsl_tok[tp_])
        gated_ready = scp.n

        wo_evac = []
        def xr_load(kk):
            if len(xr_guard) >= 1:
                p.wait(s, trc, xr_guard[-1])
            p.dma(s, xrS[kk % 2], xr[:, (kk % 2) * 1024:(kk % 2 + 1) * 1024],
                  x_own[kk * 128:(kk + 1) * 128, :])
        for qt in range(4):
            k = qs * 4 + qt
            if qt == 0:
                xr_load(k)
            if qt + 1 < 4:
                xr_load(k + 1)
            p.wait(tn, scp, gated_ready)
            p.wait(tn, p.sem("a0preG"), E["wos_ready"])
            for dh in range(2):
                b = 4 + dh
                if len(wo_evac) >= 2:
                    p.wait(tn, trc, wo_evac[-2])
                for vc in range(16):
                    inst = tn.matmul(bank(b),
                                     gated[:, vc * 512 + qt * 128:vc * 512 + (qt + 1) * 128],
                                     wos[:, vc * 1024 + dh * 512:vc * 1024 + (dh + 1) * 512],
                                     start=(vc == 0), stop=(vc == 15))
                otok = p.inc(inst, mm)
                p.wait(v, mm, otok)
                p.wait(v, xrS[k % 2], None)
                if yd_cnt[k % 2] >= 1 and dh == 0:
                    p.wait(v, ydS[k % 2], yd_cnt[k % 2] * 16)
                add = v.tensor_tensor(
                    yt[:, (k % 2) * 1024 + dh * 512:(k % 2) * 1024 + (dh + 1) * 512],
                    bank(b),
                    xr[:, (k % 2) * 1024 + dh * 512:(k % 2) * 1024 + (dh + 1) * 512],
                    ALU.add)
                wo_evac.append(p.inc(add, trc))
            xr_guard.append(trc.n)
            p.wait(v, trc)
            p.wait(s, trc)
            ytile = yt[:, (k % 2) * 1024:(k % 2 + 1) * 1024]
            p.wait(s, sV)
            ln_tile(p, ytile, ytile, sq2[:, :], st2, sV, sS,
                    cpool[:, 0:1], cpool[:, 1:2])
            p.wait(g, sS)
            p.dma(g, ydS[k % 2], y[k * 128:(k + 1) * 128, :], ytile)
            yd_cnt[k % 2] += 1
            p.wait(v, sS)
            if ybf_cnt[0] >= 1:
                p.wait(v, ydbS)
            p.inc(v.tensor_copy(ybf[:, :], ytile), trc)
            p.wait(sy, trc)
            p.dma(sy, ydbS, ybf_dram[k * 128:(k + 1) * 128, :], ybf[:, :])
            ybf_cnt[0] += 1
        bd_done_evV = evV.n
        pt_free_mm = mm.n


def prep_host_l0a(inputs):
    h = {}
    h["ident_f32"] = np.eye(128, dtype=np.float32)
    h["ident_bf16"] = np.eye(128, dtype=np.float32).astype(BF)
    Wq = np.asarray(inputs["L0_attn_W_q"])
    Wk = np.asarray(inputs["L0_attn_W_k"])
    Wr = np.asarray(inputs["L0_attn_W_r"])

    def blk128(w):
        return np.ascontiguousarray(w.reshape(8, 128, 128).transpose(1, 0, 2))

    h["l0a_Wqkr"] = np.ascontiguousarray(
        np.stack([blk128(Wq), blk128(Wk), blk128(Wr)]))
    h["l0a_Wv"] = np.ascontiguousarray(
        np.asarray(inputs["L0_attn_W_v"]).reshape(8, 128, 2048).transpose(1, 0, 2))
    h["l0a_Wg"] = np.ascontiguousarray(
        np.asarray(inputs["L0_attn_W_g"]).reshape(8, 128, 2048).transpose(1, 0, 2))
    h["l0a_Wo"] = np.ascontiguousarray(
        np.asarray(inputs["L0_attn_W_o"]).reshape(16, 128, 1024).transpose(1, 0, 2))
    h["l0a_glnrep"] = np.broadcast_to(np.asarray(inputs["L0_attn_g_ln"]), (128, 1024)).copy()
    uv = np.stack([np.asarray(inputs["L0_attn_xl_u"]),
                   np.asarray(inputs["L0_attn_xl_v"])], axis=1)
    h["l0a_uv"] = np.ascontiguousarray(uv.astype(np.float32))
    mask = np.zeros((4, 128, 512), dtype=np.float32)
    for qt in range(4):
        a = np.arange(128)[:, None]
        jj = np.arange(512)[None, :]
        mask[qt] = np.where(jj <= qt * 128 + a, 0.0, -1e30)
    h["l0a_mask"] = mask
    pos = np.arange(T, dtype=np.float64)
    inv_lams = 1.0 / (10000.0 ** (np.arange(0, 1024, 2, dtype=np.float64) / 1024.0))
    pre = pos[:, None] * inv_lams[None, :]
    emb = np.concatenate([np.sin(pre), np.cos(pre)], axis=1)
    emb = emb[::-1]
    h["l0a_sin"] = np.ascontiguousarray(emb.T.astype(np.float32))
    return h


def declare_l0a(nc, kind_act="ExternalInput"):
    io = {}
    io["l0a_x_full"] = nc.dram_tensor("l0a_x_full", [T, 1024], F32, kind=kind_act)
    io["l0a_y"] = nc.dram_tensor("l0a_y", [T, 1024], F32, kind="ExternalOutput")
    io["l0a_ybf"] = nc.dram_tensor("l0a_ybf", [T, 1024], BF16, kind="ExternalOutput")
    io["l0a_Wqkr"] = nc.dram_tensor("l0a_Wqkr", [3, 128, 8, 128], F32, kind="ExternalInput")
    io["l0a_Wv"] = nc.dram_tensor("l0a_Wv", [128, 8, 2048], F32, kind="ExternalInput")
    io["l0a_Wg"] = nc.dram_tensor("l0a_Wg", [128, 8, 2048], F32, kind="ExternalInput")
    io["l0a_Wo"] = nc.dram_tensor("l0a_Wo", [128, 16, 1024], F32, kind="ExternalInput")
    io["l0a_glnrep"] = nc.dram_tensor("l0a_glnrep", [128, 1024], F32, kind="ExternalInput")
    io["l0a_uv"] = nc.dram_tensor("l0a_uv", [128, 2], F32, kind="ExternalInput")
    io["l0a_mask"] = nc.dram_tensor("l0a_mask", [4, 128, 512], F32, kind="ExternalInput")
    io["l0a_sin"] = nc.dram_tensor("l0a_sin", [1024, T], F32, kind="ExternalInput")
    io["ident_f32"] = nc.dram_tensor("ident_f32", [128, 128], F32, kind="ExternalInput")
    io["ident_bf16"] = nc.dram_tensor("ident_bf16", [128, 128], BF16, kind="ExternalInput")
    io["l0a_v"] = nc.dram_tensor("l0a_v", [T * 2048], BF16, kind="Internal")
    io["l0a_gT"] = nc.dram_tensor("l0a_gT", [2048 * T], BF16, kind="Internal")
    io["l0a_c2"] = nc.dram_tensor("l0a_c2", [512 * 8704], F32, kind="Internal")
    io["l0a_pt"] = nc.dram_tensor("l0a_pt", [64 * 128 * 512], BF16, kind="Internal")
    io["l0a_s"] = nc.dram_tensor("l0a_s", [512 * 8192], F32, kind="Internal")
    return io


def build_l0a_program():
    nc = bass.Bass("TRN2", target_bir_lowering=False)
    io = declare_l0a(nc)
    p = Prog(nc)
    emit_l0_attn(p, io)
    return nc


def build_fused_program():
    nc = bass.Bass("TRN2", target_bir_lowering=False)
    io = {}
    io["x_full"] = nc.dram_tensor("x_full", [T, 1024], F32, kind="ExternalInput")
    io["out_y"] = nc.dram_tensor("out_y", [HTOK, 1024], F32, kind="ExternalOutput")
    x1f = nc.dram_tensor("act_x1", [T, 1024], F32, kind="Internal")
    x2f = nc.dram_tensor("act_x2", [T, 1024], F32, kind="Internal")
    x3 = nc.dram_tensor("act_x3", [HTOK, 1024], F32, kind="Internal")
    x1b = nc.dram_tensor("act_x1b", [T, 1024], BF16, kind="Internal")
    x2b = nc.dram_tensor("act_x2b", [T, 1024], BF16, kind="Internal")
    x3b = nc.dram_tensor("act_x3b", [HTOK, 1024], BF16, kind="Internal")

    io["ident_f32"] = nc.dram_tensor("ident_f32", [128, 128], F32, kind="ExternalInput")
    io["ident_bf16"] = nc.dram_tensor("ident_bf16", [128, 128], BF16, kind="ExternalInput")
    io["l0a_Wqkr"] = nc.dram_tensor("l0a_Wqkr", [3, 128, 8, 128], F32, kind="ExternalInput")
    io["l0a_Wv"] = nc.dram_tensor("l0a_Wv", [128, 8, 2048], F32, kind="ExternalInput")
    io["l0a_Wg"] = nc.dram_tensor("l0a_Wg", [128, 8, 2048], F32, kind="ExternalInput")
    io["l0a_Wo"] = nc.dram_tensor("l0a_Wo", [128, 16, 1024], F32, kind="ExternalInput")
    io["l0a_glnrep"] = nc.dram_tensor("l0a_glnrep", [128, 1024], F32, kind="ExternalInput")
    io["l0a_uv"] = nc.dram_tensor("l0a_uv", [128, 2], F32, kind="ExternalInput")
    io["l0a_mask"] = nc.dram_tensor("l0a_mask", [4, 128, 512], F32, kind="ExternalInput")
    io["l0a_sin"] = nc.dram_tensor("l0a_sin", [1024, T], F32, kind="ExternalInput")
    io["l0a_v"] = nc.dram_tensor("l0a_v", [T * 2048], BF16, kind="Internal")
    io["l0a_gT"] = nc.dram_tensor("l0a_gT", [2048 * T], BF16, kind="Internal")
    io["l0a_c2"] = nc.dram_tensor("l0a_c2", [512 * 8704], F32, kind="Internal")
    io["l0a_pt"] = nc.dram_tensor("l0a_pt", [64 * 128 * 512], BF16, kind="Internal")
    io["l0a_s"] = nc.dram_tensor("l0a_s", [512 * 8192], F32, kind="Internal")
    io["l0a_x_full"] = io["x_full"]
    io["l0a_y"] = x1f
    io["l0a_ybf"] = x1b
    io["l0f_We1b"] = nc.dram_tensor("l0f_We1b", [16, 128, 8, 512], F32, kind="ExternalInput")
    io["l0f_We3b"] = nc.dram_tensor("l0f_We3b", [16, 128, 8, 512], F32, kind="ExternalInput")
    io["l0f_We2b"] = nc.dram_tensor("l0f_We2b", [16, 128, 4, 1024], F32, kind="ExternalInput")
    io["l0f_ws1b"] = nc.dram_tensor("l0f_ws1b", [128, 8, 1024], F32, kind="ExternalInput")
    io["l0f_ws3b"] = nc.dram_tensor("l0f_ws3b", [128, 8, 1024], F32, kind="ExternalInput")
    io["l0f_ws2b"] = nc.dram_tensor("l0f_ws2b", [128, 8, 1024], F32, kind="ExternalInput")
    io["l0f_gwb"] = nc.dram_tensor("l0f_gwb", [128, 8, 16], F32, kind="ExternalInput")
    io["l0f_gbrep"] = nc.dram_tensor("l0f_gbrep", [128, 16], F32, kind="ExternalInput")
    io["l0f_be1"] = nc.dram_tensor("l0f_be1", [16, 512], F32, kind="ExternalInput")
    io["l0f_be3"] = nc.dram_tensor("l0f_be3", [16, 512], F32, kind="ExternalInput")
    io["l0f_be2"] = nc.dram_tensor("l0f_be2", [16, 1024], F32, kind="ExternalInput")
    io["l0f_bs1"] = nc.dram_tensor("l0f_bs1", [1024], F32, kind="ExternalInput")
    io["l0f_bs3"] = nc.dram_tensor("l0f_bs3", [1024], F32, kind="ExternalInput")
    io["l0f_bs2"] = nc.dram_tensor("l0f_bs2", [1024], F32, kind="ExternalInput")
    io["l0f_onehot"] = nc.dram_tensor("l0f_onehot", [16, 16 * 128], BF16, kind="ExternalInput")
    io["l0f_x"] = x1f
    io["l0f_xbf"] = x1b
    io["l0f_y"] = x2f
    io["l0f_ybf"] = x2b
    io["l1a_Wb"] = nc.dram_tensor("l1a_Wb", [4, 128, 8, 1024], F32, kind="ExternalInput")
    io["l1a_tab"] = nc.dram_tensor("l1a_tab", [8, 128, 2944], F32, kind="ExternalInput")
    io["l1a_rblast"] = nc.dram_tensor("l1a_rblast", [128, 8], F32, kind="ExternalInput")
    io["l1a_KT"] = nc.dram_tensor("l1a_KT", [8 * 128 * 8192], BF16, kind="Internal")
    io["l1a_km"] = nc.dram_tensor("l1a_km", [8 * 8192], F32, kind="Internal")
    io["l1a_V"] = nc.dram_tensor("l1a_V", [8192 * 1024], BF16, kind="Internal")
    io["l1a_QT"] = nc.dram_tensor("l1a_QT", [8 * 128 * 8192], BF16, kind="Internal")
    io["l1a_UT"] = nc.dram_tensor("l1a_UT", [8 * 128 * 8192], BF16, kind="Internal")
    io["l1a_x_full"] = x2f
    io["l1a_xbf"] = x2b
    io["l1a_y"] = x3
    io["l1a_ybf"] = x3b
    io["l1f_w1T"] = nc.dram_tensor("l1f_w1T", [1024, 4096], F32, kind="ExternalInput")
    io["l1f_w2T"] = nc.dram_tensor("l1f_w2T", [1024, 4096], F32, kind="ExternalInput")
    io["l1f_h1Tb"] = nc.dram_tensor("l1f_h1Tb", [32, 128, 32, 128], F32, kind="ExternalInput")
    io["l1f_h2Tb"] = nc.dram_tensor("l1f_h2Tb", [32, 128, 32, 128], F32, kind="ExternalInput")
    io["l1f_w3Tb"] = nc.dram_tensor("l1f_w3Tb", [8, 128, 32, 128], F32, kind="ExternalInput")
    io["l1f_scales"] = nc.dram_tensor("l1f_scales", [5, 4096], F32, kind="ExternalInput")
    io["scr_a"] = nc.dram_tensor("scr_a", [4096 * 4096], BF16, kind="Internal")
    io["scr_b"] = nc.dram_tensor("scr_b", [4096 * 4096], BF16, kind="Internal")
    io["l1f_x"] = x3
    io["l1f_xbf"] = x3b

    ALLE = [nc.sync, nc.gpsimd, nc.tensor, nc.vector, nc.scalar]
    p = Prog(nc)
    emit_l0_attn(p, io)
    p.fence_all(ALLE)
    emit_l0_ffn(p, io)
    p.fence_all(ALLE)
    emit_l1_attn(p, io)
    p.fence_all(ALLE)
    emit_l1_ffn(p, io)
    return nc


def prep_host_all(inputs):
    h = {}
    h.update(prep_host_l0a(inputs))
    h.update(prep_host_l0f(inputs))
    h.update(prep_host_l1a(inputs))
    h.update(prep_host_l1f(inputs))
    return h


FUSED = True


def kernel_fused(inputs):
    x = np.asarray(inputs["L0_attn_x"])
    h = prep_host_all(inputs)
    if "f" not in _PROG_CACHE:
        _PROG_CACHE["f"] = build_fused_program()
    nc = _PROG_CACHE["f"]
    rows = [own_rows_for(c) for c in range(8)]
    cores = list(range(8))
    im = []
    for c in cores:
        b = c // 2
        d = dict(h)
        d["x_full"] = np.ascontiguousarray(x[b])
        im.append(d)
    r = run_bass_kernel_spmd(nc, im, core_ids=cores)
    out = np.empty((4, T, 1024), dtype=np.float32)
    for c in cores:
        out[c // 2][rows[c]] = r.results[c]["out_y"]
    return out


def own_rows_for(core):
    sups = SUP_EVEN if core % 2 == 0 else SUP_ODD
    return np.concatenate([np.arange(sp * 512, (sp + 1) * 512) for sp in sups])


_PROG_CACHE = {}


def _progs():
    if "p" not in _PROG_CACHE:
        _PROG_CACHE["p"] = (build_l0a_program(), build_l0f_program(),
                            build_l1a_program(), build_l1f_program())
    return _PROG_CACHE["p"]


def kernel(**inputs):
    if FUSED:
        return kernel_fused(inputs)
    x = np.asarray(inputs["L0_attn_x"])
    h0a = prep_host_l0a(inputs)
    h0f = prep_host_l0f(inputs)
    h1a = prep_host_l1a(inputs)
    h1f = prep_host_l1f(inputs)
    nc0a, nc0f, nc1a, nc1f = _progs()
    rows = [own_rows_for(c) for c in range(8)]
    cores = list(range(8))

    im = []
    for c in cores:
        b = c // 2
        d = dict(h0a)
        d["l0a_x_own"] = np.ascontiguousarray(x[b][rows[c]])
        d["l0a_x_full"] = np.ascontiguousarray(x[b])
        im.append(d)
    r = run_bass_kernel_spmd(nc0a, im, core_ids=cores)
    x1 = [r.results[c]["l0a_y"] for c in cores]

    im = []
    for c in cores:
        d = dict(h0f)
        d["l0f_x"] = x1[c]
        im.append(d)
    r = run_bass_kernel_spmd(nc0f, im, core_ids=cores)
    x2 = [r.results[c]["l0f_y"] for c in cores]

    x2_full = []
    for b in range(4):
        xb = np.empty((T, 1024), dtype=np.float32)
        xb[rows[2 * b]] = x2[2 * b]
        xb[rows[2 * b + 1]] = x2[2 * b + 1]
        x2_full.append(xb)
    im = []
    for c in cores:
        b = c // 2
        d = dict(h1a)
        d["l1a_x_own"] = x2[c]
        d["l1a_x_full"] = x2_full[b]
        im.append(d)
    r = run_bass_kernel_spmd(nc1a, im, core_ids=cores)
    x3 = [r.results[c]["l1a_y"] for c in cores]

    im = []
    for c in cores:
        d = dict(h1f)
        d["l1f_x"] = x3[c]
        im.append(d)
    r = run_bass_kernel_spmd(nc1f, im, core_ids=cores)
    y = [r.results[c]["out_y"] for c in cores]

    out = np.empty((4, T, 1024), dtype=np.float32)
    for c in cores:
        out[c // 2][rows[c]] = y[c]
    return out


# version 17
# speedup vs baseline: 1.0434x; 1.0434x over previous
import numpy as np
import ml_dtypes
from contextlib import ExitStack
import concourse.bass as bass
import concourse.mybir as mybir
from concourse.bass_utils import run_bass_kernel_spmd

F32 = mybir.dt.float32
BF16 = mybir.dt.bfloat16
AF = mybir.ActivationFunctionType
ALU = mybir.AluOpType
AX = mybir.AxisListType

B, T, D = 4, 8192, 1024
HTOK = 4096
SUP_EVEN = [0, 1, 2, 3, 12, 13, 14, 15]
SUP_ODD = [4, 5, 6, 7, 8, 9, 10, 11]
LN_EPS = 1e-5
BF = ml_dtypes.bfloat16


class Sem:
    __slots__ = ("h", "n", "name")

    def __init__(self, nc, name):
        self.h = nc.alloc_semaphore(name)
        self.n = 0
        self.name = name


class Prog:
    def __init__(self, nc):
        self.nc = nc
        self.sems = {}
        self._waited = {}

    def sem(self, name) -> Sem:
        if name not in self.sems:
            self.sems[name] = Sem(self.nc, name)
        return self.sems[name]

    def inc(self, inst, sem: Sem, k=1):
        inst.then_inc(sem.h, k)
        sem.n += k
        return sem.n

    def dma(self, eng, sem: Sem, out, in_):
        inst = eng.dma_start(out=out, in_=in_)
        self.inc(inst, sem, 16)
        return sem.n

    def wait(self, eng, sem: Sem, val=None):
        if val is None:
            val = sem.n
        if val <= 0:
            return
        key = (str(eng.engine), sem.name)
        if self._waited.get(key, -1) >= val:
            return
        eng.wait_ge(sem.h, val)
        self._waited[key] = val

    def fence(self, engines, sems):
        for e in engines:
            for s in sems:
                self.wait(e, s)

    def fence_all(self, engines):
        self.fence(engines, list(self.sems.values()))

    def snap(self):
        return ({k: s.n for k, s in self.sems.items()}, dict(self._waited))

    def restore(self, sn):
        counts, waited = sn
        for k, v in counts.items():
            self.sems[k].n = v
        self._waited = dict(waited)

    def assert_counts(self, sn_counts, tag=""):
        for k, v in sn_counts.items():
            assert self.sems[k].n == v, \
                f"switch body sem mismatch {tag}: {k} {self.sems[k].n} != {v}"


def ln_tile(p: Prog, src_f32, out_f32, sq_tmp, st_tmp, sV: Sem, sS: Sem,
            eps_ap, inv1024_ap):
    nc = p.nc
    v, s = nc.vector, nc.scalar
    p.wait(v, sS)
    p.inc(v.reduce_sum(st_tmp[:, 0:1], src_f32, axis=AX.X, negate=True), sV)
    p.wait(v, sV)
    p.inc(v.tensor_scalar_mul(st_tmp[:, 1:2], st_tmp[:, 0:1], 1.0 / 1024.0), sV)
    p.wait(s, sV)
    p.inc(s.activation(sq_tmp, src_f32, AF.Square, bias=st_tmp[:, 1:2], scale=1.0), sS)
    p.wait(v, sS)
    p.inc(v.reduce_sum(st_tmp[:, 2:3], sq_tmp, axis=AX.X), sV)
    p.wait(s, sV)
    p.inc(s.activation(st_tmp[:, 2:3], st_tmp[:, 2:3], AF.Sqrt,
                       bias=eps_ap, scale=inv1024_ap), sS)
    p.wait(v, sS)
    p.inc(v.reciprocal(st_tmp[:, 3:4], st_tmp[:, 2:3]), sV)
    p.wait(v, sV)
    p.inc(v.tensor_mul(st_tmp[:, 1:2], st_tmp[:, 1:2], st_tmp[:, 3:4]), sV)
    p.wait(s, sV)
    p.inc(s.activation(out_f32, src_f32, AF.Identity,
                       bias=st_tmp[:, 1:2], scale=st_tmp[:, 3:4]), sS)
    return sS.n


def emit_l1_ffn(p: Prog, io, stages="ABCD"):
    nc = p.nc
    tn, v, s, g, sy = nc.tensor, nc.vector, nc.scalar, nc.gpsimd, nc.sync
    x3 = io["l1f_x"].ap()
    x3bf = io["l1f_xbf"].ap()
    y = io["out_y"].ap()
    w1T, w2T = io["l1f_w1T"], io["l1f_w2T"]
    h1Tb, h2Tb = io["l1f_h1Tb"], io["l1f_h2Tb"]
    w3Tb = io["l1f_w3Tb"]
    sc = io["l1f_scales"].ap()
    idf = io["ident_f32"]
    idb = io["ident_bf16"]
    h_scr = io["scr_a"]
    g_scr = io["scr_b"]

    mm = p.sem("mm"); evS = p.sem("evS"); evV = p.sem("evV")
    trc = p.sem("trc")
    sV = p.sem("lnV"); sS = p.sem("lnS")
    preS = p.sem("f1pre"); wS = p.sem("f1wS")
    xtS = [p.sem("f1xt0"), p.sem("f1xt1")]
    hbS = [p.sem("f1hb0"), p.sem("f1hb1")]

    with (
        nc.sbuf_tensor("f1cp", [128, 4], F32) as cpool,
        nc.sbuf_tensor("f1sc", [128, 5 * 32], F32) as sc32,
        nc.sbuf_tensor("f1idb", [128, 128], BF16) as idbs,
        nc.sbuf_tensor("f1idf", [128, 128], F32) as idfs,
        nc.psum_tensor("f1ps", [128, 4 * 512], F32) as ps,
        nc.psum_tensor("f1pt", [128, 2 * 1024], BF16) as pt,
        nc.psum_tensor("f1pf", [128, 512], F32) as pf,
    ):
        p.inc(nc.vector.memset(cpool[:, 0:1], LN_EPS), sV)
        p.inc(nc.vector.memset(cpool[:, 1:2], 1.0 / 1024.0), sV)
        p.dma(sy, preS, idbs[:, :], idb.ap())
        p.dma(sy, preS, idfs[:, :], idf.ap())
        for r in range(5):
            for oc in range(32 if r < 4 else 8):
                p.dma(sy, preS, sc32[:, r * 32 + oc:r * 32 + oc + 1],
                      sc[r, oc * 128:(oc + 1) * 128].unsqueeze(1))
        pre_ready = preS.n

        with (
            nc.sbuf_tensor("f1w12", [128, 2 * 8 * 4096], BF16) as w12,
            nc.sbuf_tensor("f1xt", [128, 2 * 1024], BF16) as xt,
            nc.sbuf_tensor("f1x3T", [128, 2 * 8 * 512], BF16) as x3T,
            nc.sbuf_tensor("f1tmp", [128, 4 * 512], F32) as tmp,
            nc.sbuf_tensor("f1hb", [128, 2 * 512], BF16) as hb,
        ):
            for wi, wsrc in enumerate((w1T, w2T) if "A" in stages else ()):
                for c in range(8):
                    p.dma(g, wS, w12[:, (wi * 8 + c) * 4096:(wi * 8 + c + 1) * 4096],
                          wsrc.ap()[c * 128:(c + 1) * 128, :])
            w_ready = wS.n

            xt_free = []; x3T_free = []; tr_free = []; ps_free = []; hb_cnt = [0, 0]
            for ts in range(HTOK // 512 if "A" in stages else 0):
                for tt in range(4):
                    k = ts * 4 + tt
                    if len(xt_free) >= 2:
                        p.wait(sy, mm, xt_free[-2])
                    p.dma(sy, xtS[k % 2], xt[:, (k % 2) * 1024:(k % 2 + 1) * 1024],
                          x3bf[k * 128:(k + 1) * 128, :])
                    p.wait(tn, xtS[k % 2])
                    p.wait(tn, preS, pre_ready)
                    if len(x3T_free) >= 2:
                        p.wait(v, mm, x3T_free[-2])
                    for c in range(8):
                        i = k * 8 + c
                        if len(tr_free) >= 2:
                            p.wait(tn, trc, tr_free[-2])
                        inst = tn.transpose(
                            pt[:, (i % 2) * 1024:(i % 2) * 1024 + 128],
                            xt[:, (k % 2) * 1024 + c * 128:(k % 2) * 1024 + (c + 1) * 128],
                            idbs[:, :])
                        tok = p.inc(inst, mm)
                        if c == 7:
                            xt_free.append(tok)
                        p.wait(v, mm, tok)
                        cp = v.tensor_copy(
                            x3T[:, (ts % 2) * 4096 + c * 512 + tt * 128:
                                (ts % 2) * 4096 + c * 512 + (tt + 1) * 128],
                            pt[:, (i % 2) * 1024:(i % 2) * 1024 + 128])
                        tr_free.append(p.inc(cp, trc))
                slice_ready = trc.n
                p.wait(tn, wS, w_ready)
                p.wait(tn, trc, slice_ready)
                last_mm = None
                for oc in range(32):
                    pr = oc % 2
                    if len(ps_free) >= 2:
                        p.wait(tn, evV, ps_free[-2])
                    for wi in range(2):
                        psl = ps[:, (2 * pr + wi) * 512:(2 * pr + wi + 1) * 512]
                        for c in range(8):
                            inst = tn.matmul(
                                psl,
                                w12[:, (wi * 8 + c) * 4096 + oc * 128:(wi * 8 + c) * 4096 + (oc + 1) * 128],
                                x3T[:, (ts % 2) * 4096 + c * 512:(ts % 2) * 4096 + (c + 1) * 512],
                                start=(c == 0), stop=(c == 7))
                        tok = p.inc(inst, mm)
                    last_mm = tok
                    t1f = tmp[:, pr * 512:(pr + 1) * 512]
                    t2f = tmp[:, (2 + pr) * 512:(3 + pr) * 512]
                    p.wait(s, mm, tok - 1)
                    p.wait(s, preS, pre_ready)
                    if len(ps_free) >= 2:
                        p.wait(s, evV, ps_free[-2])
                    p.inc(s.activation(t1f, ps[:, 2 * pr * 512:(2 * pr + 1) * 512],
                                       AF.Silu, bias=0.0,
                                       scale=sc32[:, oc:oc + 1]), evS)
                    p.wait(v, mm, tok)
                    p.wait(v, preS, pre_ready)
                    p.inc(v.tensor_scalar(t2f, ps[:, (2 * pr + 1) * 512:(2 * pr + 2) * 512],
                                          sc32[:, 32 + oc:32 + oc + 1], None, ALU.mult), evV)
                    p.wait(v, evV)
                    p.wait(v, evS)
                    if hb_cnt[pr] >= 1:
                        p.wait(v, hbS[pr], hb_cnt[pr] * 16)
                    mul = v.tensor_tensor(hb[:, pr * 512:(pr + 1) * 512], t1f, t2f, ALU.mult)
                    ps_free.append(p.inc(mul, evV))
                    p.wait(g, evV)
                    p.dma(g, hbS[pr],
                          bass.AP(h_scr, oc * 128 * 4096 + ts * 512, [[4096, 128], [1, 512]]),
                          hb[:, pr * 512:(pr + 1) * 512])
                    hb_cnt[pr] += 1
                x3T_free.append(last_mm)

        def dense4k(src_scr, dst_scr, wTb, sc_row):
            p.fence_all([sy, g, nc.tensor, nc.vector, nc.scalar])
            evS_base = evS.n
            hsS = p.sem("f1hs")
            wbS = [p.sem("f1wb0"), p.sem("f1wb1")]
            gbS = [p.sem("f1gb0"), p.sem("f1gb1")]
            with (
                nc.sbuf_tensor(f"f1hs{sc_row}", [128, 32 * 2048], BF16) as hs,
                nc.sbuf_tensor(f"f1wbt{sc_row}", [128, 2 * 32 * 128], BF16) as wb,
                nc.sbuf_tensor(f"f1gbt{sc_row}", [128, 2 * 4 * 512], BF16) as gb,
            ):
                hs_free_mm = 0
                wb_free = []
                gb_cnt = [gbS[0].n // 16, gbS[1].n // 16]

                def wb_load(kw):
                    if len(wb_free) >= 1:
                        p.wait(g, mm, wb_free[-1])
                    p.dma(g, wbS[kw % 2], wb[:, (kw % 2) * 4096:(kw % 2 + 1) * 4096],
                          bass.AP(wTb, (kw % 32) * 128 * 4096, [[4096, 128], [1, 4096]]))

                wb_load(0)
                ko = 0
                for mt in range(2):
                    p.wait(sy, mm, hs_free_mm)
                    for c in range(32):
                        p.dma(sy, hsS, hs[:, c * 2048:(c + 1) * 2048],
                              bass.AP(src_scr, c * 128 * 4096 + mt * 2048, [[4096, 128], [1, 2048]]))
                    hs_ready = hsS.n
                    for oc in range(32):
                        pr = ko % 2
                        if ko + 1 < 64:
                            wb_load(ko + 1)
                        p.wait(tn, wbS[pr])
                        p.wait(tn, hsS, hs_ready)
                        last = None
                        for sub in range(4):
                            if ko > 0:
                                p.wait(tn, evS, evS_base + (ko - 1) * 4 + sub + 1)
                            psl = ps[:, sub * 512:(sub + 1) * 512]
                            for c in range(32):
                                inst = tn.matmul(
                                    psl, wb[:, pr * 4096 + c * 128:pr * 4096 + (c + 1) * 128],
                                    hs[:, c * 2048 + sub * 512:c * 2048 + (sub + 1) * 512],
                                    start=(c == 0), stop=(c == 31))
                            last = p.inc(inst, mm)
                            p.wait(s, mm, last)
                            if gb_cnt[pr] >= 1 and sub == 0:
                                p.wait(s, gbS[pr], gb_cnt[pr] * 16)
                            p.inc(s.activation(gb[:, pr * 2048 + sub * 512:pr * 2048 + (sub + 1) * 512],
                                               psl, AF.Gelu, bias=0.0,
                                               scale=sc32[:, sc_row * 32 + oc:sc_row * 32 + oc + 1]), evS)
                        wb_free.append(last)
                        p.wait(g, evS)
                        p.dma(g, gbS[pr],
                              bass.AP(dst_scr, oc * 128 * 4096 + mt * 2048, [[4096, 128], [1, 2048]]),
                              gb[:, pr * 2048:(pr + 1) * 2048])
                        gb_cnt[pr] += 1
                        ko += 1
                    hs_free_mm = mm.n

        if "B" in stages:
            dense4k(h_scr, g_scr, h1Tb, 2)
        if "C" in stages:
            dense4k(g_scr, h_scr, h2Tb, 3)

        p.fence_all([sy, g, nc.tensor, nc.vector, nc.scalar])
        w3S = p.sem("f1w3S")
        g2S = [p.sem("f1g20"), p.sem("f1g21")]
        xrS = [p.sem("f1xr0"), p.sem("f1xr1")]
        ydS = [p.sem("f1yd0"), p.sem("f1yd1")]
        with (
            nc.sbuf_tensor("f1g2", [128, 2 * 32 * 512], BF16) as g2s,
            nc.sbuf_tensor("f1w3", [128, 8 * 4096], BF16) as w3s,
            nc.sbuf_tensor("f1y4T", [128, 8 * 512], F32) as y4T,
            nc.sbuf_tensor("f1yt", [128, 2 * 1024], F32) as yt,
            nc.sbuf_tensor("f1xr", [128, 2 * 1024], F32) as xr,
            nc.sbuf_tensor("f1sq", [128, 1024], F32) as sq,
            nc.sbuf_tensor("f1st", [128, 4], F32) as st,
        ):
            for oc in range(8 if "D" in stages else 0):
                p.dma(g, w3S, w3s[:, oc * 4096:(oc + 1) * 4096],
                      bass.AP(w3Tb, oc * 128 * 4096, [[4096, 128], [1, 4096]]))
            w3_ready = w3S.n

            g2_free = [0, 0]
            tr32_free = []; yd_cnt = [0, 0]; xr_guard = []
            y4_last_tr = 0
            for ts in range(HTOK // 512 if "D" in stages else 1):
                pr = ts % 2
                p.wait(sy, mm, g2_free[pr])
                for c in range(32):
                    p.dma(sy, g2S[pr], g2s[:, pr * 16384 + c * 512:pr * 16384 + (c + 1) * 512],
                          bass.AP(h_scr, c * 128 * 4096 + ts * 512, [[4096, 128], [1, 512]]))
                p.wait(tn, w3S, w3_ready)
                p.wait(tn, g2S[pr])
                last = None
                ev_base = evV.n
                for oc in range(8):
                    psl = ps[:, (oc % 4) * 512:(oc % 4 + 1) * 512]
                    if oc >= 4:
                        p.wait(tn, evV, ev_base + oc - 4 + 1)
                    for c in range(32):
                        inst = tn.matmul(psl,
                                         w3s[:, oc * 4096 + c * 128:oc * 4096 + (c + 1) * 128],
                                         g2s[:, pr * 16384 + c * 512:pr * 16384 + (c + 1) * 512],
                                         start=(c == 0), stop=(c == 31))
                    last = p.inc(inst, mm)
                    p.wait(v, mm, last)
                    p.wait(v, mm, y4_last_tr)
                    p.wait(v, preS, pre_ready)
                    p.inc(v.tensor_scalar(y4T[:, oc * 512:(oc + 1) * 512], psl,
                                          sc32[:, 4 * 32 + oc:4 * 32 + oc + 1], None, ALU.mult), evV)
                g2_free[pr] = last
                y4_ready = evV.n
                def xr_load(kk):
                    if len(xr_guard) >= 1:
                        p.wait(s, trc, xr_guard[-1])
                    p.dma(s, xrS[kk % 2], xr[:, (kk % 2) * 1024:(kk % 2 + 1) * 1024],
                          x3[kk * 128:(kk + 1) * 128, :])
                for tt in range(4):
                    k = ts * 4 + tt
                    if tt == 0:
                        xr_load(k)
                    if tt + 1 < 4:
                        xr_load(k + 1)
                    p.wait(tn, evV, y4_ready)
                    for oc in range(8):
                        i = k * 8 + oc
                        if len(tr32_free) >= 1:
                            p.wait(tn, trc, tr32_free[-1])
                        inst = tn.transpose(pf[:, 0:128],
                                            y4T[:, oc * 512 + tt * 128:oc * 512 + (tt + 1) * 128],
                                            idfs[:, :])
                        ttok = p.inc(inst, mm)
                        if tt == 3 and oc == 7:
                            y4_last_tr = ttok
                        p.wait(v, mm, ttok)
                        p.wait(v, xrS[k % 2])
                        if yd_cnt[k % 2] >= 1 and oc == 0:
                            p.wait(v, ydS[k % 2], yd_cnt[k % 2] * 16)
                        add = v.tensor_tensor(
                            yt[:, (k % 2) * 1024 + oc * 128:(k % 2) * 1024 + (oc + 1) * 128],
                            pf[:, 0:128],
                            xr[:, (k % 2) * 1024 + oc * 128:(k % 2) * 1024 + (oc + 1) * 128],
                            ALU.add)
                        tr32_free.append(p.inc(add, trc))
                    xr_guard.append(trc.n)
                    p.wait(v, trc)
                    p.wait(s, trc)
                    ytile = yt[:, (k % 2) * 1024:(k % 2 + 1) * 1024]
                    p.wait(s, sV)
                    ln_tile(p, ytile, ytile, sq[:, :], st, sV, sS,
                            cpool[:, 0:1], cpool[:, 1:2])
                    p.wait(g, sS)
                    p.dma(g, ydS[k % 2], y[k * 128:(k + 1) * 128, :], ytile)
                    yd_cnt[k % 2] += 1
            p.wait(g, ydS[0])
            p.wait(g, ydS[1])


def prep_host_l1f(inputs):
    h = {}
    h["ident_f32"] = np.eye(128, dtype=np.float32)
    h["ident_bf16"] = np.eye(128, dtype=np.float32).astype(BF)
    h["l1f_w1T"] = np.ascontiguousarray(inputs["L1_ffn_w1_w"].T)
    h["l1f_w2T"] = np.ascontiguousarray(inputs["L1_ffn_w2_w"].T)

    def blk(wT, noc):
        n_in = wT.shape[0]
        return np.ascontiguousarray(
            wT.reshape(n_in // 128, 128, noc, 128).transpose(2, 1, 0, 3))

    h["l1f_h1Tb"] = blk(np.ascontiguousarray(inputs["L1_ffn_h1_w"].T), 32)
    h["l1f_h2Tb"] = blk(np.ascontiguousarray(inputs["L1_ffn_h2_w"].T), 32)
    h["l1f_w3Tb"] = blk(np.ascontiguousarray(inputs["L1_ffn_w3_w"].T), 8)
    sc = np.zeros((5, 4096), dtype=np.float32)
    sc[0] = np.asarray(inputs["L1_ffn_w1_s"])
    sc[1] = np.asarray(inputs["L1_ffn_w2_s"])
    sc[2] = np.asarray(inputs["L1_ffn_h1_s"])
    sc[3] = np.asarray(inputs["L1_ffn_h2_s"])
    sc[4, :1024] = np.asarray(inputs["L1_ffn_w3_s"])
    h["l1f_scales"] = sc
    return h


def declare_l1f(nc, kind_act="ExternalInput"):
    io = {}
    io["l1f_x"] = nc.dram_tensor("l1f_x", [HTOK, 1024], F32, kind=kind_act)
    io["l1f_xbf"] = nc.dram_tensor("l1f_xbf", [HTOK, 1024], BF16, kind="ExternalInput")
    io["out_y"] = nc.dram_tensor("out_y", [HTOK, 1024], F32, kind="ExternalOutput")
    io["l1f_w1T"] = nc.dram_tensor("l1f_w1T", [1024, 4096], F32, kind="ExternalInput")
    io["l1f_w2T"] = nc.dram_tensor("l1f_w2T", [1024, 4096], F32, kind="ExternalInput")
    io["l1f_h1Tb"] = nc.dram_tensor("l1f_h1Tb", [32, 128, 32, 128], F32, kind="ExternalInput")
    io["l1f_h2Tb"] = nc.dram_tensor("l1f_h2Tb", [32, 128, 32, 128], F32, kind="ExternalInput")
    io["l1f_w3Tb"] = nc.dram_tensor("l1f_w3Tb", [8, 128, 32, 128], F32, kind="ExternalInput")
    io["l1f_scales"] = nc.dram_tensor("l1f_scales", [5, 4096], F32, kind="ExternalInput")
    io["ident_f32"] = nc.dram_tensor("ident_f32", [128, 128], F32, kind="ExternalInput")
    io["ident_bf16"] = nc.dram_tensor("ident_bf16", [128, 128], BF16, kind="ExternalInput")
    io["scr_a"] = nc.dram_tensor("scr_a", [4096 * 4096], BF16, kind="Internal")
    io["scr_b"] = nc.dram_tensor("scr_b", [4096 * 4096], BF16, kind="Internal")
    return io


def build_l1f_program(stages="ABCD"):
    nc = bass.Bass("TRN2", target_bir_lowering=False)
    io = declare_l1f(nc)
    p = Prog(nc)
    emit_l1_ffn(p, io, stages)
    return nc


def emit_l0_ffn(p: Prog, io):
    nc = p.nc
    tn, v, s, g, sy = nc.tensor, nc.vector, nc.scalar, nc.gpsimd, nc.sync
    x1 = io["l0f_x"].ap()
    x1bf = io["l0f_xbf"].ap()
    y = io["l0f_y"].ap()
    ybf_dram = io["l0f_ybf"].ap()
    We1b = io["l0f_We1b"]
    We3b = io["l0f_We3b"]
    We2b = io["l0f_We2b"]
    ws1b = io["l0f_ws1b"]
    ws3b = io["l0f_ws3b"]
    ws2b = io["l0f_ws2b"]
    gwb = io["l0f_gwb"]
    gbr = io["l0f_gbrep"]
    be1 = io["l0f_be1"].ap()
    be3 = io["l0f_be3"].ap()
    be2 = io["l0f_be2"]
    bs1 = io["l0f_bs1"].ap()
    bs3 = io["l0f_bs3"].ap()
    bs2 = io["l0f_bs2"].ap()
    oneh = io["l0f_onehot"]
    idf = io["ident_f32"]
    idb = io["ident_bf16"]

    mm = p.sem("mm"); evS = p.sem("evS"); evV = p.sem("evV")
    trc = p.sem("trc"); gsem = trc
    sV = p.sem("lnV"); sS = p.sem("lnS")
    preS = p.sem("f0pre"); wshS = p.sem("f0wsh")
    xtS = [p.sem("f0xt0"), p.sem("f0xt1")]
    we13S = [p.sem("f0we0"), p.sem("f0we1")]
    we2S = p.sem("f0we2")
    xrS = [p.sem("f0xr0"), p.sem("f0xr1")]
    ydS = [p.sem("f0yd0"), p.sem("f0yd1")]

    NMT = 8
    MT = 1024

    with ExitStack() as stk:
        def SB(*a):
            return stk.enter_context(nc.sbuf_tensor(*a))
        cpool = SB("f0cp", [128, 4], F32)
        idbs = SB("f0idb", [128, 128], BF16)
        idfs = SB("f0idf", [128, 128], F32)
        gwsb = SB("f0gw", [128, 8 * 16], BF16)
        gbsb = SB("f0gbr", [128, 16], F32)
        ohsb = SB("f0oh", [16, 16 * 128], BF16)
        be1sb = SB("f0be1", [128, 64], F32)
        be3sb = SB("f0be3", [128, 64], F32)
        be2sb = SB("f0be2", [16, 1024], BF16)
        bssb = SB("f0bs", [128, 3 * 8], F32)
        wsh = SB("f0wsh", [128, 3 * 8 * 1024], BF16)
        x1T = SB("f0x1T", [128, 8 * MT], BF16)
        zacc = SB("f0za", [128, 8 * MT], F32)
        we13 = SB("f0we13", [128, 2 * 8192], BF16)
        we2sb = SB("f0we2", [128, 4 * 1024], BF16)
        xt = SB("f0xt", [128, 2 * 1024], BF16)
        hbuf = SB("f0hb", [128, 8 * 512], BF16)
        tmp = SB("f0tmp", [128, 6 * 512], F32)
        shT = SB("f0sh", [128, 8 * 512], BF16)
        gt = SB("f0gt", [128, 6 * 16], F32)
        gsc = SB("f0gs", [128, 16], F32)
        cbf = SB("f0cb", [128, 32], BF16)
        cT16 = SB("f0cT", [16, MT], BF16)
        yt = SB("f0yt", [128, 2 * 1024], F32)
        ybf = SB("f0ybf", [128, 2 * 1024], BF16)
        xr = SB("f0xr", [128, 2 * 1024], F32)
        sq = SB("f0sq", [128, 1024], F32)
        st = SB("f0st", [128, 4], F32)
        PS = stk.enter_context(nc.psum_tensor("f0ps", [128, 7 * 512], F32))
        def bank(i, w=512):
            return PS[:, i * 512:i * 512 + w]

        b6bf = PS[:, 6 * 512:7 * 512].bitcast(BF16)
        b3bf = PS[:, 3 * 512:4 * 512].bitcast(BF16)

        p.inc(v.memset(cpool[:, 0:1], LN_EPS), sV)
        p.inc(v.memset(cpool[:, 1:2], 1.0 / 1024.0), sV)
        p.dma(sy, preS, idbs[:, :], idb.ap())
        p.dma(sy, preS, idfs[:, :], idf.ap())
        p.dma(sy, preS, gbsb[:, :], gbr.ap())
        for e in range(16):
            for ic in range(4):
                p.dma(sy, preS, be1sb[:, e * 4 + ic:e * 4 + ic + 1],
                      be1[e, ic * 128:(ic + 1) * 128].unsqueeze(1))
                p.dma(sy, preS, be3sb[:, e * 4 + ic:e * 4 + ic + 1],
                      be3[e, ic * 128:(ic + 1) * 128].unsqueeze(1))
        for r, src in enumerate((bs1, bs3, bs2)):
            for ic in range(8):
                p.dma(sy, preS, bssb[:, r * 8 + ic:r * 8 + ic + 1],
                      src[ic * 128:(ic + 1) * 128].unsqueeze(1))
        preG = p.sem("f0preG")
        p.dma(g, preG, gwsb[:, :], bass.AP(gwb, 0, [[8 * 16, 128], [1, 8 * 16]]))
        p.dma(g, preG, ohsb[:, :], bass.AP(oneh, 0, [[16 * 128, 16], [1, 16 * 128]]))
        p.dma(g, preG, be2sb[:, :], bass.AP(be2, 0, [[1024, 16], [1, 1024]]))
        for wi, wsrc in enumerate((ws1b, ws3b, ws2b)):
            p.dma(g, wshS, wsh[:, wi * 8192:(wi + 1) * 8192],
                  bass.AP(wsrc, 0, [[8 * 1024, 128], [1, 8 * 1024]]))
        pre_ready = preS.n
        preg_ready = preG.n
        wsh_ready = wshS.n

        xt_cnt = 0
        tr_free = []
        pend_gate = []
        pend_cbf_tok = [0]
        x1T_free_mm = 0
        cb_read_evV = 0
        we13_free = []
        we2_free_mm = 0
        zacc_free = 0
        xr_guard = []
        yd_cnt = [0, 0]
        ydbS = p.sem("ydb")
        ybf_cnt = [0]
        eptr_free = []
        gate_seq = 0

        pend_tile = [None]
        xtmm = []

        def l0f_tile_body(mt, tt):
            k = mt * 8 + tt
            if mt > 0 and tt == 0:
                p.wait(tn, trc)
            p.wait(tn, xtS[k % 2])
            p.wait(tn, preS, pre_ready)
            if mt > 0:
                p.wait(tn, mm, x1T_free_mm)
                p.wait(v, mm, x1T_free_mm)
            for c in range(8):
                i = k * 8 + c
                b = 4 + (i % 2)
                bview = PS[:, b * 512:(b + 1) * 512].bitcast(BF16)
                if len(tr_free) >= 2:
                    p.wait(tn, trc, tr_free[-2])
                inst = tn.transpose(bview[:, 0:128],
                                    xt[:, (k % 2) * 1024 + c * 128:(k % 2) * 1024 + (c + 1) * 128],
                                    idbs[:, :])
                tok = p.inc(inst, mm)
                p.wait(v, mm, tok)
                cp = v.tensor_copy(x1T[:, c * MT + tt * 128:c * MT + (tt + 1) * 128],
                                   bview[:, 0:128])
                tr_free.append(p.inc(cp, trc))
                if c == 7:
                    xtmm.append(tok)
            if len(pend_gate) >= 2:
                emit_gate_transpose(p, b3bf, idbs, cT16, cbf, *pend_gate.pop(0))
            p.wait(tn, trc)
            p.wait(tn, p.sem("f0preG"), preg_ready)
            for c in range(8):
                inst = tn.matmul(bank(6, 16),
                                 x1T[:, c * MT + tt * 128:c * MT + (tt + 1) * 128],
                                 gwsb[:, c * 16:(c + 1) * 16],
                                 start=(c == 0), stop=(c == 7))
            p.inc(inst, mm)
            emit_gate_tile(p, io, gt, gsc, gbsb, cbf, bank(6, 16), b6bf, idbs,
                           cT16, tt, mm, evV, evS, trc, gsem, pend_cbf_tok)
            pend_gate.append((pend_cbf_tok[0], tt))

        for mt in range(NMT):
            for tt in range(8):
                k = mt * 8 + tt
                if xtmm:
                    p.wait(sy, mm, xtmm[-1])
                p.dma(sy, xtS[k % 2], xt[:, (k % 2) * 1024:(k % 2 + 1) * 1024],
                      x1bf[k * 128:(k + 1) * 128, :])
                xt_cnt += 1
                if pend_tile[0] is not None:
                    l0f_tile_body(*pend_tile[0])
                pend_tile[0] = (mt, tt)
            if pend_tile[0] is not None:
                l0f_tile_body(*pend_tile[0])
                pend_tile[0] = None
            while pend_gate:
                emit_gate_transpose(p, b3bf, idbs, cT16, cbf, *pend_gate.pop(0))
            gate_done = trc.n
            x1T_ready = trc.n

            hA_free = {}
            hb_free = {}
            zo_free = {}
            tmp_free = {}
            for e in range(16):
                pr = e % 2
                if len(we13_free) >= 2:
                    p.wait(g, mm, we13_free[-2])
                p.dma(g, we13S[pr], we13[:, pr * 8192:pr * 8192 + 4096],
                      bass.AP(We1b, e * 128 * 4096, [[4096, 128], [1, 4096]]))
                p.dma(g, we13S[pr], we13[:, pr * 8192 + 4096:pr * 8192 + 8192],
                      bass.AP(We3b, e * 128 * 4096, [[4096, 128], [1, 4096]]))
                p.wait(g, mm, we2_free_mm)
                p.dma(g, we2S, we2sb[:, :],
                      bass.AP(We2b, e * 128 * 4096, [[4096, 128], [1, 4096]]))
                p.wait(tn, we13S[pr])
                p.wait(tn, trc, x1T_ready)
                hb_ready = [0, 0]
                for sub in range(2):
                    p.wait(tn, trc, gate_done)
                    p.wait(tn, evV, cb_read_evV)
                    inst = tn.matmul(bank(6),
                                     ohsb[:, e * 128:(e + 1) * 128],
                                     cT16[:, sub * 512:(sub + 1) * 512],
                                     start=True, stop=True)
                    cb_tok = p.inc(inst, mm)
                    for ic in range(4):
                        icp = ic % 2
                        bA = 2 * icp
                        bB = bA + 1
                        p.wait(tn, evV, hA_free.get(icp, 0))
                        for c in range(8):
                            tn.matmul(bank(bA),
                                      we13[:, pr * 8192 + c * 512 + ic * 128:pr * 8192 + c * 512 + (ic + 1) * 128],
                                      x1T[:, c * MT + sub * 512:c * MT + (sub + 1) * 512],
                                      start=(c == 0), stop=(c == 7))
                            inst = tn.matmul(bank(bB),
                                             we13[:, pr * 8192 + 4096 + c * 512 + ic * 128:pr * 8192 + 4096 + c * 512 + (ic + 1) * 128],
                                             x1T[:, c * MT + sub * 512:c * MT + (sub + 1) * 512],
                                             start=(c == 0), stop=(c == 7))
                        htok = p.inc(inst, mm)
                        t0 = icp * 1536
                        hslot = sub * 4 + ic
                        p.wait(s, mm, htok)
                        p.wait(s, evV, tmp_free.get(icp, 0))
                        p.inc(s.activation(tmp[:, t0:t0 + 512], bank(bA), AF.Silu,
                                           bias=be1sb[:, e * 4 + ic:e * 4 + ic + 1], scale=1.0), evS)
                        p.wait(v, mm, htok)
                        p.wait(v, evV, tmp_free.get(icp, 0))
                        p.inc(v.tensor_scalar(tmp[:, t0 + 512:t0 + 1024], bank(bB),
                                              be3sb[:, e * 4 + ic:e * 4 + ic + 1], None, ALU.add), evV)
                        p.wait(v, evV)
                        p.wait(v, evS)
                        p.wait(v, mm, hb_free.get(hslot, 0))
                        p.inc(v.tensor_tensor(tmp[:, t0 + 1024:t0 + 1536], tmp[:, t0:t0 + 512],
                                              tmp[:, t0 + 512:t0 + 1024], ALU.mult), evV)
                        p.wait(v, evV)
                        p.wait(v, mm, cb_tok)
                        tok = p.inc(v.tensor_tensor(hbuf[:, hslot * 512:(hslot + 1) * 512],
                                                    tmp[:, t0 + 1024:t0 + 1536], bank(6), ALU.mult), evV)
                        hA_free[icp] = tok
                        tmp_free[icp] = tok
                    cb_read_evV = evV.n
                    hb_ready[sub] = evV.n
                p.wait(tn, we2S)
                last_o = None
                for sub in range(2):
                    p.wait(tn, evV, hb_ready[sub])
                    for d in range(8):
                        dp = d % 2
                        bO = 4 + dp
                        p.wait(tn, evV, zo_free.get(dp, 0))
                        for ic in range(4):
                            inst = tn.matmul(bank(bO),
                                             we2sb[:, ic * 1024 + d * 128:ic * 1024 + (d + 1) * 128],
                                             hbuf[:, (sub * 4 + ic) * 512:(sub * 4 + ic + 1) * 512],
                                             start=(ic == 0), stop=(ic == 3))
                        otok = p.inc(inst, mm)
                        last_o = otok
                        p.wait(v, mm, otok)
                        if mt > 0:
                            p.wait(v, trc, zacc_free)
                        dst = zacc[:, d * MT + sub * 512:d * MT + (sub + 1) * 512]
                        if e == 0:
                            zo_free[dp] = p.inc(v.tensor_copy(dst, bank(bO)), evV)
                        else:
                            zo_free[dp] = p.inc(v.tensor_tensor(dst, bank(bO), dst, ALU.add), evV)
                    for ic in range(4):
                        hb_free[sub * 4 + ic] = last_o
                we2_free_mm = last_o
                we13_free.append(mm.n)
            p.wait(tn, wshS, wsh_ready)
            sh_free = {}
            for sub in range(2):
                for ic in range(8):
                    icp = ic % 2
                    bA = 2 * icp
                    bB = bA + 1
                    p.wait(tn, evV, hA_free.get(icp, 0))
                    for c in range(8):
                        tn.matmul(bank(bA),
                                  wsh[:, 0 * 8192 + c * 1024 + ic * 128:0 * 8192 + c * 1024 + (ic + 1) * 128],
                                  x1T[:, c * MT + sub * 512:c * MT + (sub + 1) * 512],
                                  start=(c == 0), stop=(c == 7))
                        inst = tn.matmul(bank(bB),
                                         wsh[:, 1 * 8192 + c * 1024 + ic * 128:1 * 8192 + c * 1024 + (ic + 1) * 128],
                                         x1T[:, c * MT + sub * 512:c * MT + (sub + 1) * 512],
                                         start=(c == 0), stop=(c == 7))
                    htok = p.inc(inst, mm)
                    t0 = icp * 1536
                    p.wait(s, mm, htok)
                    p.wait(s, evV, tmp_free.get(icp, 0))
                    p.inc(s.activation(tmp[:, t0:t0 + 512], bank(bA), AF.Silu,
                                       bias=bssb[:, ic:ic + 1], scale=1.0), evS)
                    p.wait(v, mm, htok)
                    p.wait(v, evV, tmp_free.get(icp, 0))
                    p.inc(v.tensor_scalar(tmp[:, t0 + 512:t0 + 1024], bank(bB),
                                          bssb[:, 8 + ic:8 + ic + 1], None, ALU.add), evV)
                    p.wait(v, evV)
                    p.wait(v, evS)
                    p.wait(v, mm, sh_free.get(ic, 0))
                    tok = p.inc(v.tensor_tensor(shT[:, ic * 512:(ic + 1) * 512],
                                                tmp[:, t0:t0 + 512], tmp[:, t0 + 512:t0 + 1024], ALU.mult), evV)
                    hA_free[icp] = tok
                    tmp_free[icp] = tok
                p.wait(tn, evV, evV.n)
                p.wait(tn, preS, pre_ready)
                last_o = None
                for d in range(8):
                    dp = d % 2
                    bO = 4 + dp
                    p.wait(tn, evV, zo_free.get(dp, 0))
                    for ic in range(8):
                        tn.matmul(bank(bO),
                                  wsh[:, 2 * 8192 + ic * 1024 + d * 128:2 * 8192 + ic * 1024 + (d + 1) * 128],
                                  shT[:, ic * 512:(ic + 1) * 512],
                                  start=(ic == 0), stop=False)
                    inst = tn.matmul(bank(bO),
                                     be2sb[0:16, d * 128:(d + 1) * 128],
                                     cT16[:, sub * 512:(sub + 1) * 512],
                                     start=False, stop=True)
                    otok = p.inc(inst, mm)
                    last_o = otok
                    p.wait(v, mm, otok)
                    dst = zacc[:, d * MT + sub * 512:d * MT + (sub + 1) * 512]
                    stt = v.scalar_tensor_tensor(dst, bank(bO),
                                                 bssb[:, 16 + d:16 + d + 1], dst,
                                                 ALU.add, ALU.add)
                    zo_free[dp] = p.inc(stt, evV)
                for ic in range(8):
                    sh_free[ic] = last_o
            z_ready = evV.n
            def xr_load(kk):
                if len(xr_guard) >= 1:
                    p.wait(s, trc, xr_guard[-1])
                p.dma(s, xrS[kk % 2], xr[:, (kk % 2) * 1024:(kk % 2 + 1) * 1024],
                      x1[kk * 128:(kk + 1) * 128, :])
            for tt in range(8):
                k = mt * 8 + tt
                if tt == 0:
                    xr_load(k)
                if tt + 1 < 8:
                    xr_load(k + 1)
                p.wait(tn, evV, z_ready)
                for d in range(8):
                    bt = (6, 5, 4, 3)[d % 4]
                    if len(eptr_free) >= 4:
                        p.wait(tn, trc, eptr_free[-4])
                    inst = tn.transpose(bank(bt, 128),
                                        zacc[:, d * MT + tt * 128:d * MT + (tt + 1) * 128],
                                        idfs[:, :])
                    ttok = p.inc(inst, mm)
                    p.wait(v, mm, ttok)
                    p.wait(v, xrS[k % 2])
                    if yd_cnt[k % 2] >= 1 and d == 0:
                        p.wait(v, ydS[k % 2], yd_cnt[k % 2] * 16)
                    add = v.tensor_tensor(
                        yt[:, (k % 2) * 1024 + d * 128:(k % 2) * 1024 + (d + 1) * 128],
                        bank(bt, 128),
                        xr[:, (k % 2) * 1024 + d * 128:(k % 2) * 1024 + (d + 1) * 128],
                        ALU.add)
                    eptr_free.append(p.inc(add, trc))
                xr_guard.append(trc.n)
                p.wait(v, trc)
                p.wait(s, trc)
                ytile = yt[:, (k % 2) * 1024:(k % 2 + 1) * 1024]
                p.wait(s, sV)
                ln_tile(p, ytile, ytile, sq[:, :], st, sV, sS,
                        cpool[:, 0:1], cpool[:, 1:2])
                p.wait(g, sS)
                p.dma(g, ydS[k % 2], y[k * 128:(k + 1) * 128, :], ytile)
                yd_cnt[k % 2] += 1
                p.wait(v, sS)
                if ybf_cnt[0] >= 1:
                    p.wait(v, ydbS)
                p.inc(v.tensor_copy(ybf[:, (k % 2) * 1024:(k % 2 + 1) * 1024], ytile), trc)
                p.wait(sy, trc)
                p.dma(sy, ydbS, ybf_dram[k * 128:(k + 1) * 128, :],
                      ybf[:, (k % 2) * 1024:(k % 2 + 1) * 1024])
                ybf_cnt[0] += 1
            zacc_free = trc.n
            x1T_free_mm = mm.n
        p.wait(g, ydS[0])
        p.wait(g, ydS[1])


def emit_gate_transpose(p: Prog, b6bf, idbs, cT16, cbf, cbtok, tt):
    nc = p.nc
    tn, v = nc.tensor, nc.vector
    mm = p.sem("mm"); evV = p.sem("evV"); trc = p.sem("trc")
    p.wait(tn, evV, cbtok)
    gg = getattr(p, "_gate_tr", None)
    if gg is None:
        gg = p._gate_tr = [0]
    if gg[0]:
        p.wait(tn, trc, gg[0])
    inst = tn.transpose(b6bf[0:16, 0:128], cbf[:, (tt % 2) * 16:(tt % 2 + 1) * 16],
                        idbs[:, :])
    tok = p.inc(inst, mm)
    p.wait(v, mm, tok)
    cp = v.tensor_copy(cT16[0:16, tt * 128:(tt + 1) * 128], b6bf[0:16, 0:128])
    gg[0] = p.inc(cp, trc)


def emit_gate_tile(p: Prog, io, gt, gsc, gbsb, cbf, s16psum, b6bf, idbs, cT16,
                   tt, mm, evV, evS, trc, gsem, pend_cbf_tok):
    nc = p.nc
    v, s, tn = nc.vector, nc.scalar, nc.tensor
    T = gt
    orig = T[:, 0:16]
    sgb = T[:, 16:32]
    stl = T[:, 32:48]
    scr = T[:, 48:64]
    scr2 = T[:, 64:80]
    mx = gsc[:, 0:1]
    sm = gsc[:, 1:2]
    gs4 = gsc[:, 4:8]
    t2g = gsc[:, 8:9]
    t2 = gsc[:, 9:10]

    p.wait(v, mm)
    p.wait(v, evS)
    p.inc(v.reduce_max(mx, s16psum, axis=AX.X, negate=True), evV)
    p.wait(s, evV)
    p.wait(s, mm)
    p.inc(s.activation(orig, s16psum, AF.Exp, bias=mx, scale=1.0), evS)
    p.wait(v, evS)
    p.inc(v.reduce_sum(sm, orig, axis=AX.X), evV)
    p.wait(v, evV)
    p.inc(v.reciprocal(sm, sm), evV)
    p.wait(v, evV)
    p.inc(v.tensor_scalar(orig, orig, sm, None, ALU.mult), evV)
    p.wait(v, evV)
    p.inc(v.tensor_tensor(sgb, orig, gbsb[:, :], ALU.add), evV)
    p.wait(v, evV)
    c0, c1, c2, c3 = (sgb[:, j:16:4] for j in range(4))
    p.inc(v.tensor_tensor(scr[:, 0:4], c0, c1, ALU.max), evV)
    p.inc(v.tensor_tensor(scr[:, 4:8], c0, c1, ALU.min), evV)
    p.inc(v.tensor_tensor(scr[:, 8:12], c2, c3, ALU.max), evV)
    p.inc(v.tensor_tensor(scr[:, 12:16], c2, c3, ALU.min), evV)
    p.wait(v, evV)
    p.inc(v.tensor_tensor(scr2[:, 0:4], scr[:, 0:4], scr[:, 8:12], ALU.max), evV)
    p.inc(v.tensor_tensor(scr2[:, 4:8], scr[:, 0:4], scr[:, 8:12], ALU.min), evV)
    p.inc(v.tensor_tensor(scr2[:, 8:12], scr[:, 4:8], scr[:, 12:16], ALU.max), evV)
    p.wait(v, evV)
    p.inc(v.tensor_tensor(scr2[:, 4:8], scr2[:, 4:8], scr2[:, 8:12], ALU.max), evV)
    p.wait(v, evV)
    p.inc(v.tensor_tensor(gs4, scr2[:, 0:4], scr2[:, 4:8], ALU.add), evV)
    p.wait(v, evV)
    g0, g1, g2, g3 = (gs4[:, j:j + 1] for j in range(4))
    p.inc(v.tensor_tensor(scr[:, 0:1], g0, g1, ALU.max), evV)
    p.inc(v.tensor_tensor(scr[:, 1:2], g0, g1, ALU.min), evV)
    p.inc(v.tensor_tensor(scr[:, 2:3], g2, g3, ALU.max), evV)
    p.inc(v.tensor_tensor(scr[:, 3:4], g2, g3, ALU.min), evV)
    p.wait(v, evV)
    p.inc(v.tensor_tensor(scr[:, 4:5], scr[:, 0:1], scr[:, 2:3], ALU.min), evV)
    p.inc(v.tensor_tensor(scr[:, 5:6], scr[:, 1:2], scr[:, 3:4], ALU.max), evV)
    p.wait(v, evV)
    p.inc(v.tensor_tensor(t2g, scr[:, 4:5], scr[:, 5:6], ALU.max), evV)
    p.wait(v, evV)
    p.inc(v.tensor_single_scalar(gs4, gs4, t2g, ALU.is_ge), evV)
    p.wait(v, evV)
    kb = bass.AP(gs4.tensor, gs4.offset, [gs4.ap[0], [1, 4], [0, 4]])
    p.inc(v.tensor_tensor(stl, sgb, kb, ALU.mult), evV)
    p.inc(v.tensor_scalar(scr, kb, 1e30, -1e30, ALU.mult, ALU.add), evV)
    p.wait(v, evV)
    p.inc(v.tensor_tensor(stl, stl, scr, ALU.add), evV)
    p.wait(v, evV)
    a0, a1 = stl[:, 0:16:2], stl[:, 1:16:2]
    p.inc(v.tensor_tensor(scr[:, 0:8], a0, a1, ALU.max), evV)
    p.inc(v.tensor_tensor(scr[:, 8:16], a0, a1, ALU.min), evV)
    p.wait(v, evV)
    for lvl, n in ((0, 4), (1, 2), (2, 1)):
        Ma, Mb = scr[:, 0:2 * n:2], scr[:, 1:2 * n:2]
        Sa, Sb = scr[:, 8:8 + 2 * n:2], scr[:, 9:8 + 2 * n:2]
        p.inc(v.tensor_tensor(scr2[:, 0:n], Ma, Mb, ALU.max), evV)
        p.inc(v.tensor_tensor(scr2[:, 8:8 + n], Ma, Mb, ALU.min), evV)
        p.inc(v.tensor_tensor(scr2[:, 4:4 + n], Sa, Sb, ALU.max), evV)
        p.wait(v, evV)
        p.inc(v.tensor_tensor(scr2[:, 8:8 + n], scr2[:, 8:8 + n], scr2[:, 4:4 + n], ALU.max), evV)
        p.wait(v, evV)
        p.inc(v.tensor_copy(scr[:, 0:n], scr2[:, 0:n]), evV)
        p.inc(v.tensor_copy(scr[:, 8:8 + n], scr2[:, 8:8 + n]), evV)
        p.wait(v, evV)
    p.inc(v.tensor_copy(t2, scr[:, 8:9]), evV)
    p.wait(v, evV)
    p.inc(v.tensor_single_scalar(scr[:, 0:16], stl, t2, ALU.is_ge), evV)
    p.wait(v, evV)
    p.inc(v.tensor_tensor(cbf[:, (tt % 2) * 16:(tt % 2 + 1) * 16], orig, scr[:, 0:16], ALU.mult), evV)
    p.wait(v, evV)
    pend_cbf_tok[0] = evV.n


def prep_host_l0f(inputs):
    h = {}
    h["ident_f32"] = np.eye(128, dtype=np.float32)
    h["ident_bf16"] = np.eye(128, dtype=np.float32).astype(BF)
    We1 = np.asarray(inputs["L0_ffn_We1"])
    We3 = np.asarray(inputs["L0_ffn_We3"])
    We2 = np.asarray(inputs["L0_ffn_We2"])
    h["l0f_We1b"] = np.ascontiguousarray(We1.reshape(16, 8, 128, 512).transpose(0, 2, 1, 3))
    h["l0f_We3b"] = np.ascontiguousarray(We3.reshape(16, 8, 128, 512).transpose(0, 2, 1, 3))
    h["l0f_We2b"] = np.ascontiguousarray(We2.reshape(16, 4, 128, 1024).transpose(0, 2, 1, 3))
    for nm, key in (("ws1b", "L0_ffn_ws1"), ("ws3b", "L0_ffn_ws3"), ("ws2b", "L0_ffn_ws2")):
        w = np.asarray(inputs[key])
        h["l0f_" + nm] = np.ascontiguousarray(w.reshape(8, 128, 1024).transpose(1, 0, 2))
    gw = np.asarray(inputs["L0_ffn_gate_w"])
    gwT = np.ascontiguousarray(gw.T)
    h["l0f_gwb"] = np.ascontiguousarray(gwT.reshape(8, 128, 16).transpose(1, 0, 2))
    h["l0f_gbrep"] = np.broadcast_to(np.asarray(inputs["L0_ffn_gate_b"]), (128, 16)).copy()
    h["l0f_be1"] = np.asarray(inputs["L0_ffn_be1"])
    h["l0f_be3"] = np.asarray(inputs["L0_ffn_be3"])
    h["l0f_be2"] = np.asarray(inputs["L0_ffn_be2"])
    h["l0f_bs1"] = np.asarray(inputs["L0_ffn_bs1"])
    h["l0f_bs3"] = np.asarray(inputs["L0_ffn_bs3"])
    h["l0f_bs2"] = np.asarray(inputs["L0_ffn_bs2"])
    onehot = np.zeros((16, 16 * 128), dtype=np.float32)
    for e in range(16):
        onehot[e, e * 128:(e + 1) * 128] = 1.0
    h["l0f_onehot"] = onehot.astype(BF)
    return h


def declare_l0f(nc, kind_act="ExternalInput"):
    io = {}
    io["l0f_x"] = nc.dram_tensor("l0f_x", [T, 1024], F32, kind=kind_act)
    io["l0f_xbf"] = nc.dram_tensor("l0f_xbf", [T, 1024], BF16, kind="ExternalInput")
    io["l0f_y"] = nc.dram_tensor("l0f_y", [T, 1024], F32, kind="ExternalOutput")
    io["l0f_ybf"] = nc.dram_tensor("l0f_ybf", [T, 1024], BF16, kind="ExternalOutput")
    io["l0f_We1b"] = nc.dram_tensor("l0f_We1b", [16, 128, 8, 512], F32, kind="ExternalInput")
    io["l0f_We3b"] = nc.dram_tensor("l0f_We3b", [16, 128, 8, 512], F32, kind="ExternalInput")
    io["l0f_We2b"] = nc.dram_tensor("l0f_We2b", [16, 128, 4, 1024], F32, kind="ExternalInput")
    io["l0f_ws1b"] = nc.dram_tensor("l0f_ws1b", [128, 8, 1024], F32, kind="ExternalInput")
    io["l0f_ws3b"] = nc.dram_tensor("l0f_ws3b", [128, 8, 1024], F32, kind="ExternalInput")
    io["l0f_ws2b"] = nc.dram_tensor("l0f_ws2b", [128, 8, 1024], F32, kind="ExternalInput")
    io["l0f_gwb"] = nc.dram_tensor("l0f_gwb", [128, 8, 16], F32, kind="ExternalInput")
    io["l0f_gbrep"] = nc.dram_tensor("l0f_gbrep", [128, 16], F32, kind="ExternalInput")
    io["l0f_be1"] = nc.dram_tensor("l0f_be1", [16, 512], F32, kind="ExternalInput")
    io["l0f_be3"] = nc.dram_tensor("l0f_be3", [16, 512], F32, kind="ExternalInput")
    io["l0f_be2"] = nc.dram_tensor("l0f_be2", [16, 1024], F32, kind="ExternalInput")
    io["l0f_bs1"] = nc.dram_tensor("l0f_bs1", [1024], F32, kind="ExternalInput")
    io["l0f_bs3"] = nc.dram_tensor("l0f_bs3", [1024], F32, kind="ExternalInput")
    io["l0f_bs2"] = nc.dram_tensor("l0f_bs2", [1024], F32, kind="ExternalInput")
    io["l0f_onehot"] = nc.dram_tensor("l0f_onehot", [16, 16 * 128], BF16, kind="ExternalInput")
    io["ident_f32"] = nc.dram_tensor("ident_f32", [128, 128], F32, kind="ExternalInput")
    io["ident_bf16"] = nc.dram_tensor("ident_bf16", [128, 128], BF16, kind="ExternalInput")
    return io


def build_l0f_program():
    nc = bass.Bass("TRN2", target_bir_lowering=False)
    io = declare_l0f(nc)
    p = Prog(nc)
    emit_l0_ffn(p, io)
    return nc


INV_SQRT_HD = 1.0 / float(np.sqrt(128.0))


def emit_l1_attn(p: Prog, io):
    nc = p.nc
    tn, v, s, g, sy = nc.tensor, nc.vector, nc.scalar, nc.gpsimd, nc.sync
    x_full = io["l1a_x_full"].ap()
    xbf_full = io["l1a_xbf"].ap()
    y = io["l1a_y"].ap()
    ybf_dram = io["l1a_ybf"].ap()
    Wb = io["l1a_Wb"]
    tab = io["l1a_tab"]
    rbl = io["l1a_rblast"]
    idf = io["ident_f32"]
    idb = io["ident_bf16"]
    KT_scr = io["l1a_KT"]
    km_scr = io["l1a_km"]
    V_scr = io["l1a_V"]
    QT_scr = io["l1a_QT"]
    UT_scr = io["l1a_UT"]

    mm = p.sem("mm"); evS = p.sem("evS"); evV = p.sem("evV")
    trc = p.sem("trc")
    sV = p.sem("lnV"); sS = p.sem("lnS")
    preS = p.sem("a1pre"); preG = p.sem("a1preG")
    xtS = [p.sem("a1xt0"), p.sem("a1xt1")]
    kstS = [p.sem("a1kst0"), p.sem("a1kst1")]
    vstS = [p.sem("a1vst0"), p.sem("a1vst1")]
    ktlS = [p.sem("a1ktl0"), p.sem("a1ktl1"), p.sem("a1ktl2"), p.sem("a1ktl3")]
    vtlS = [p.sem("a1vtl0"), p.sem("a1vtl1"), p.sem("a1vtl2"), p.sem("a1vtl3")]
    qtlS = [p.sem("a1qtl0"), p.sem("a1qtl1")]
    utlS = [p.sem("a1utl0"), p.sem("a1utl1")]
    xrS = [p.sem("a1xr0"), p.sem("a1xr1")]
    ydS = [p.sem("a1yd0"), p.sem("a1yd1")]

    with ExitStack() as stk:
        def SB(*a):
            return stk.enter_context(nc.sbuf_tensor(*a))
        cpool = SB("a1cp", [128, 4], F32)
        idbs = SB("a1idb", [128, 128], BF16)
        idfs = SB("a1idf", [128, 128], F32)
        rbsb = SB("a1rbl", [128, 8], F32)
        onesb = SB("a1ones", [128, 1], BF16)
        stkP1 = stk.enter_context(ExitStack())

        def SBP1(*a):
            return stkP1.enter_context(nc.sbuf_tensor(*a))
        Wsb = SBP1("a1W", [128, 4 * 8 * 1024], BF16)
        xt = SBP1("a1xt", [128, 2 * 1024], BF16)
        xTs = SBP1("a1xT", [128, 2 * 8 * 512], BF16)
        kbuf = SBP1("a1kb", [128, 2 * 512], BF16)
        absb = SBP1("a1ab", [128, 512], BF16)
        kmrow = SBP1("a1kmr", [1, 512], F32)
        kmcols = SBP1("a1km", [128, 32], F32)
        vbuf = SBP1("a1vb", [128, 2 * 1024], BF16)
        qub = SBP1("a1qu", [128, 2 * 512], BF16)
        PS = stk.enter_context(nc.psum_tensor("a1ps", [128, 7 * 512], F32))

        def bank(i, w=512):
            return PS[:, i * 512:i * 512 + w]

        def bankbf(i):
            return PS[:, i * 512:(i + 1) * 512].bitcast(BF16)

        p.inc(v.memset(cpool[:, 0:1], LN_EPS), sV)
        p.inc(v.memset(cpool[:, 1:2], 1.0 / 1024.0), sV)
        p.inc(v.memset(onesb[:, :], 1.0), sV)
        p.dma(sy, preS, idbs[:, :], idb.ap())
        p.dma(sy, preS, idfs[:, :], idf.ap())
        p.dma(sy, preS, rbsb[:, :], rbl.ap())
        for wi in range(4):
            p.dma(g, preG, Wsb[:, wi * 8192:(wi + 1) * 8192],
                  bass.AP(Wb, wi * 128 * 8192, [[8192, 128], [1, 8192]]))
        pre_ready = preS.n
        preg_ready = preG.n
        p.wait(v, preS, pre_ready)
        p.inc(v.tensor_scalar_mul(rbsb[:, :], rbsb[:, :], INV_SQRT_HD), evV)

        xt_cnt = 0
        tr_free = []
        kst_cnt = [0, 0]
        vst_cnt = [0, 0]
        kml_cnt = [0, 0]
        kmcol_free = [0, 0]
        kq_free = []
        v_free = []

        pend_xt = [None]
        xtmm = []

        def flush_xT(p_):
            if pend_xt[0] is None:
                return
            k, ts_par = pend_xt[0]
            pend_xt[0] = None
            p_.wait(tn, xtS[k % 2])
            p_.wait(tn, preS, pre_ready)
            for c in range(8):
                i = k * 8 + c
                if len(tr_free) >= 2:
                    p_.wait(tn, trc, tr_free[-2])
                bv = bankbf(i % 2)
                inst = tn.transpose(bv[:, 0:128],
                                    xt[:, (k % 2) * 1024 + c * 128:(k % 2) * 1024 + (c + 1) * 128],
                                    idbs[:, :])
                tok = p_.inc(inst, mm)
                p_.wait(v, mm, tok)
                cp = v.tensor_copy(
                    xTs[:, ts_par * 4096 + c * 512 + (k % 4) * 128:ts_par * 4096 + c * 512 + (k % 4 + 1) * 128],
                    bv[:, 0:128])
                tr_free.append(p_.inc(cp, trc))
                if c == 7:
                    xtmm.append(tok)

        def build_xT(src_ap, k, ts_par):
            nonlocal xt_cnt
            if xtmm:
                p.wait(sy, mm, xtmm[-1])
            p.dma(sy, xtS[k % 2], xt[:, (k % 2) * 1024:(k % 2 + 1) * 1024],
                  src_ap[(k % 32768) * 128:(k % 32768 + 1) * 128, :])
            xt_cnt += 1
            flush_xT(p)
            pend_xt[0] = (k, ts_par)

        for fs in range(16):
            par = fs % 2
            if fs >= 2:
                p.wait(v, mm)
            for tt in range(4):
                build_xT(xbf_full, fs * 4 + tt, par)
            flush_xT(p)
            p.wait(tn, trc)
            p.wait(tn, preG, preg_ready)
            for h in range(8):
                bK = 2 + (h % 2)
                if h >= 2 or fs > 0:
                    p.wait(tn, evS, evS.n)
                for c in range(8):
                    inst = tn.matmul(bank(bK),
                                     Wsb[:, 1 * 8192 + c * 1024 + h * 128:1 * 8192 + c * 1024 + (h + 1) * 128],
                                     xTs[:, par * 4096 + c * 512:par * 4096 + (c + 1) * 512],
                                     start=(c == 0), stop=(c == 7))
                ktok = p.inc(inst, mm)
                p.wait(s, mm, ktok)
                if kst_cnt[h % 2] >= 1:
                    p.wait(s, kstS[h % 2], kst_cnt[h % 2] * 16)
                p.inc(s.activation(kbuf[:, (h % 2) * 512:(h % 2 + 1) * 512], bank(bK),
                                   AF.Silu, bias=0.0, scale=1.0), evS)
                p.wait(s, evS)
                p.inc(s.activation(absb[:, :], kbuf[:, (h % 2) * 512:(h % 2 + 1) * 512],
                                   AF.Abs, bias=0.0, scale=1.0), evS)
                p.inc(nc.vector.engine_nop(), evV)
                p.wait(tn, evS, evS.n)
                inst = tn.matmul(bank(6)[0:1, :], onesb[:, :], absb[:, :],
                                 start=True, stop=True)
                kmtok = p.inc(inst, mm)
                p.wait(s, mm, kmtok)
                p.wait(s, p.sem("a1kmS"))
                p.inc(s.activation(kmrow[:, :], bank(6)[0:1, :], AF.Sign,
                                   bias=0.0, scale=1.0), evS)
                p.wait(sy, evS)
                p.dma(sy, p.sem("a1kmS"),
                      bass.AP(km_scr, h * 8192 + fs * 512, [[512, 1], [1, 512]]),
                      kmrow[:, :])
                p.wait(g, evS)
                p.dma(g, kstS[h % 2],
                      bass.AP(KT_scr, h * 128 * 8192 + fs * 512, [[8192, 128], [1, 512]]),
                      kbuf[:, (h % 2) * 512:(h % 2 + 1) * 512])
                kst_cnt[h % 2] += 1
            for tt in range(4):
                bV = 2 + (tt % 2)
                p.wait(tn, evV, evV.n)
                p.wait(tn, evS, evS.n)
                for half in range(2):
                    for c in range(8):
                        inst = tn.matmul(bank(bV) if half == 0 else bank(4 + (tt % 2)),
                                         xTs[:, par * 4096 + c * 512 + tt * 128:par * 4096 + c * 512 + (tt + 1) * 128],
                                         Wsb[:, 2 * 8192 + c * 1024 + half * 512:2 * 8192 + c * 1024 + (half + 1) * 512],
                                         start=(c == 0), stop=(c == 7))
                vtok = p.inc(inst, mm)
                p.wait(s, mm, vtok)
                p.wait(s, evS)
                if vst_cnt[tt % 2] >= 1:
                    p.wait(s, vstS[tt % 2], vst_cnt[tt % 2] * 16)
                p.inc(s.activation(vbuf[:, (tt % 2) * 1024:(tt % 2) * 1024 + 512], bank(bV),
                                   AF.Silu, bias=0.0, scale=1.0), evS)
                p.inc(s.activation(vbuf[:, (tt % 2) * 1024 + 512:(tt % 2 + 1) * 1024],
                                   bank(4 + (tt % 2)), AF.Silu, bias=0.0, scale=1.0), evS)
                kmldS = p.sem("a1kml0") if tt % 2 == 0 else p.sem("a1kml1")
                kml_cnt[tt % 2] += 8
                p.wait(sy, p.sem("a1kmS"))
                p.wait(sy, evV, kmcol_free[tt % 2])
                for h in range(8):
                    p.dma(sy, kmldS, kmcols[:, (tt % 2) * 16 + h:(tt % 2) * 16 + h + 1],
                          bass.AP(km_scr, h * 8192 + (fs * 4 + tt) * 128, [[1, 128], [1, 1]]))
                p.wait(v, evS)
                p.wait(v, kmldS, kml_cnt[tt % 2] * 16)
                kmlast = None
                for h in range(8):
                    kmlast = p.inc(v.tensor_scalar(
                        vbuf[:, (tt % 2) * 1024 + h * 128:(tt % 2) * 1024 + (h + 1) * 128],
                        vbuf[:, (tt % 2) * 1024 + h * 128:(tt % 2) * 1024 + (h + 1) * 128],
                        kmcols[:, (tt % 2) * 16 + h:(tt % 2) * 16 + h + 1], None, ALU.mult), evV)
                kmcol_free[tt % 2] = kmlast
                p.wait(g, evV)
                p.dma(g, vstS[tt % 2],
                      bass.AP(V_scr, (fs * 4 + tt) * 128 * 1024, [[1024, 128], [1, 1024]]),
                      vbuf[:, (tt % 2) * 1024:(tt % 2 + 1) * 1024])
                vst_cnt[tt % 2] += 1
        for os_ in range(16):
            par = os_ % 2
            p.wait(v, mm)
            for tt in range(4):
                build_xT(xbf_full, os_ * 4 + tt, par)
            flush_xT(p)
            p.wait(tn, trc)
            for h in range(8):
                bQ = 2 + (h % 2)
                p.wait(tn, evS, evS.n)
                for c in range(8):
                    inst = tn.matmul(bank(bQ),
                                     Wsb[:, 0 * 8192 + c * 1024 + h * 128:0 * 8192 + c * 1024 + (h + 1) * 128],
                                     xTs[:, par * 4096 + c * 512:par * 4096 + (c + 1) * 512],
                                     start=(c == 0), stop=(c == 7))
                qtok = p.inc(inst, mm)
                p.wait(s, mm, qtok)
                if kst_cnt[h % 2] >= 1:
                    p.wait(s, kstS[h % 2], kst_cnt[h % 2] * 16)
                p.inc(s.activation(qub[:, (h % 2) * 512:(h % 2 + 1) * 512], bank(bQ),
                                   AF.Silu, bias=0.0, scale=1.0), evS)
                p.wait(g, evS)
                p.dma(g, kstS[h % 2],
                      bass.AP(QT_scr, h * 128 * 8192 + os_ * 512, [[8192, 128], [1, 512]]),
                      qub[:, (h % 2) * 512:(h % 2 + 1) * 512])
                kst_cnt[h % 2] += 1
            for h in range(8):
                bU = 2 + (h % 2)
                p.wait(tn, evS, evS.n)
                for c in range(8):
                    inst = tn.matmul(bank(bU),
                                     Wsb[:, 3 * 8192 + c * 1024 + h * 128:3 * 8192 + c * 1024 + (h + 1) * 128],
                                     xTs[:, par * 4096 + c * 512:par * 4096 + (c + 1) * 512],
                                     start=(c == 0), stop=(c == 7))
                utok = p.inc(inst, mm)
                p.wait(s, mm, utok)
                if kst_cnt[h % 2] >= 1:
                    p.wait(s, kstS[h % 2], kst_cnt[h % 2] * 16)
                p.inc(s.activation(qub[:, (h % 2) * 512:(h % 2 + 1) * 512], bank(bU),
                                   AF.Silu, bias=0.0, scale=1.0), evS)
                p.wait(g, evS)
                p.dma(g, kstS[h % 2],
                      bass.AP(UT_scr, h * 128 * 8192 + os_ * 512, [[8192, 128], [1, 512]]),
                      qub[:, (h % 2) * 512:(h % 2 + 1) * 512])
                kst_cnt[h % 2] += 1

        p.fence_all([sy, g, tn, v, s])
        stkP1.close()
        QTs = SB("a1QT", [128, 8 * 512], BF16)
        UTs = SB("a1UT", [128, 8 * 512], BF16)
        KTt = SB("a1KTt", [128, 4 * 4 * 128], BF16)
        Vt = SB("a1Vt", [128, 4 * 512], BF16)
        PTb = SB("a1PT", [128, 4 * 512], BF16)
        wtmp = SB("a1wt", [128, 4 * 512], BF16)
        outT = SB("a1oT", [128, 8 * 512], F32)
        yt = SB("a1yt", [128, 2 * 1024], F32)
        xr = SB("a1xr", [128, 2 * 1024], F32)
        sq = SB("a1sq", [128, 1024], F32)
        st = SB("a1st", [128, 4], F32)
        ybf = SB("a1ybf", [128, 2 * 1024], BF16)
        tabsb = SB("a1tab", [128, 8 * 2944], BF16)
        for h in range(8):
            p.dma(g, preG, tabsb[:, h * 2944:(h + 1) * 2944],
                  bass.AP(tab, h * 128 * 2944, [[2944, 128], [1, 2944]]))
        p.wait(v, preG)
        for h in range(8):
            p.inc(v.tensor_scalar_mul(tabsb[:, h * 2944:(h + 1) * 2944],
                                      tabsb[:, h * 2944:(h + 1) * 2944], INV_SQRT_HD), evV)
        tab_ready = evV.n
        engs = [tn, v, s, g, sy]
        pid = nc.partition_id(engines=[e.engine for e in engs])
        paridx = pid & 1
        idxs = [paridx] * len(engs)
        base_snap = p.snap()
        body_counts = []
        for bi in nc.Switch(engines=engs, index=idxs, n=2):
            p.restore(base_snap)
            p._waited = {}
            SUPS = SUP_EVEN if bi == 0 else SUP_ODD
            emit_l1a_body(p, SUPS, locals())
            body_counts.append({k: sm.n for k, sm in p.sems.items()})
        assert body_counts[0] == body_counts[1], "l1a switch body sem mismatch"
        p._waited = {}
        p.wait(g, ydS[0])
        p.wait(g, ydS[1])


def emit_l1a_body(p: Prog, SUPS, env):
    nc = p.nc
    tn, v, s, g, sy = nc.tensor, nc.vector, nc.scalar, nc.gpsimd, nc.sync
    mm = p.sem("mm"); evS = p.sem("evS"); evV = p.sem("evV")
    trc = p.sem("trc")
    sV = p.sem("lnV"); sS = p.sem("lnS")
    ktlS = [p.sem("a1ktl0"), p.sem("a1ktl1"), p.sem("a1ktl2"), p.sem("a1ktl3")]
    vtlS = [p.sem("a1vtl0"), p.sem("a1vtl1"), p.sem("a1vtl2"), p.sem("a1vtl3")]
    qtlS = [p.sem("a1qtl0"), p.sem("a1qtl1")]
    utlS = [p.sem("a1utl0"), p.sem("a1utl1")]
    xrS = [p.sem("a1xr0"), p.sem("a1xr1")]
    ydS = [p.sem("a1yd0"), p.sem("a1yd1")]
    kstS = [p.sem("a1kst0"), p.sem("a1kst1")]
    vstS = [p.sem("a1vst0"), p.sem("a1vst1")]
    (bank, bankbf, KTt, Vt, PTb, wtmp, outT, QTs, UTs, tabsb, rbsb, yt, xr,
     sq, st, cpool, idfs, KT_scr, V_scr, QT_scr, UT_scr, x_full, y, tab_ready) = (
        env["bank"], env["bankbf"], env["KTt"], env["Vt"], env["PTb"], env["wtmp"],
        env["outT"], env["QTs"], env["UTs"], env["tabsb"], env["rbsb"], env["yt"],
        env["xr"], env["sq"], env["st"], env["cpool"], env["idfs"], env["KT_scr"],
        env["V_scr"], env["QT_scr"], env["UT_scr"], env["x_full"], env["y"],
        env["tab_ready"])
    ybf = env["ybf"]; ybf_dram = env["ybf_dram"]
    ydbS = p.sem("ydb")
    ybf_cnt = [0]

    p.wait(sy, kstS[0]); p.wait(sy, kstS[1])
    p.wait(sy, vstS[0]); p.wait(sy, vstS[1])
    p.wait(v, p.sem("evV"), tab_ready)

    ktl_cnt = [0, 0, 0, 0]
    vtl_cnt = [0, 0, 0, 0]
    qtl_cnt = [0, 0]
    utl_cnt = [0, 0]
    qts_free_mm = 0
    uts_free = 0
    kt_free = []
    pt_free = []
    sb_free = []
    wtmp_free = []
    acc_free = 0
    xr_guard = []
    yd_cnt = [0, 0]
    eptr_free = []
    outT_free_mm = 0
    isc = [0]

    for qs in range(8):
        sup = SUPS[qs]
        q0 = sup * 512
        qpar = qs % 2
        nkt = 4 * (sup + 1)
        p.wait(sy, mm, qts_free_mm)
        for h in range(8):
            p.dma(sy, qtlS[qpar], QTs[:, h * 512:(h + 1) * 512],
                  bass.AP(QT_scr, h * 128 * 8192 + q0, [[8192, 128], [1, 512]]))
            p.dma(sy, utlS[qpar], UTs[:, h * 512:(h + 1) * 512],
                  bass.AP(UT_scr, h * 128 * 8192 + q0, [[8192, 128], [1, 512]]))
        qtl_cnt[qpar] += 8
        utl_cnt[qpar] += 8
        for hg in range(2):
            for kt in range(nkt):
                kpar = kt % 4
                d0 = q0 - kt * 128
                if len(kt_free) >= 4:
                    p.wait(g, mm, kt_free[-4])
                for hh in range(4):
                    h = hg * 4 + hh
                    p.dma(g, ktlS[kpar], KTt[:, kpar * 512 + hh * 128:kpar * 512 + (hh + 1) * 128],
                          bass.AP(KT_scr, h * 128 * 8192 + kt * 128, [[8192, 128], [1, 128]]))
                ktl_cnt[kpar] += 4
                p.dma(g, vtlS[kpar], Vt[:, kpar * 512:(kpar + 1) * 512],
                      bass.AP(V_scr, (kt * 128) * 1024 + hg * 512, [[1024, 128], [1, 512]]))
                vtl_cnt[kpar] += 1
                p.wait(tn, ktlS[kpar], ktl_cnt[kpar] * 16)
                p.wait(tn, vtlS[kpar], vtl_cnt[kpar] * 16)
                p.wait(tn, qtlS[qpar], qtl_cnt[qpar] * 16)
                stoks = []
                for hh in range(4):
                    h = hg * 4 + hh
                    bS = 4 + (isc[0] % 3)
                    isc[0] += 1
                    if len(sb_free) >= 3:
                        fsem, ftok = sb_free[-3]
                        p.wait(tn, fsem, ftok)
                    if kt == 0 and hh == 0:
                        p.wait(tn, evV, acc_free)
                        p.wait(tn, trc)
                    inst = tn.matmul(bank(bS),
                                     KTt[:, kpar * 512 + hh * 128:kpar * 512 + (hh + 1) * 128],
                                     QTs[:, h * 512:(h + 1) * 512],
                                     start=True, stop=True)
                    stoks.append(p.inc(inst, mm))
                    ppar = hh
                    if d0 <= 2048:
                        m0 = d0 + 384
                        p.wait(v, mm, stoks[-1])
                        if len(wtmp_free) >= 4:
                            p.wait(v, evS, wtmp_free[-4])
                        if len(pt_free) >= 4:
                            p.wait(v, mm, pt_free[-4])
                        stt = v.scalar_tensor_tensor(
                            wtmp[:, ppar * 512:(ppar + 1) * 512], bank(bS), INV_SQRT_HD,
                            tabsb[:, h * 2944 + m0:h * 2944 + m0 + 512],
                            ALU.mult, ALU.add)
                        sb_free.append((p.sem("evV"), p.inc(stt, evV)))
                        p.wait(s, evV)
                        if len(pt_free) >= 4:
                            p.wait(s, mm, pt_free[-4])
                        ac = s.activation(PTb[:, ppar * 512:(ppar + 1) * 512],
                                          wtmp[:, ppar * 512:(ppar + 1) * 512],
                                          AF.Silu, bias=0.0, scale=1.0)
                        wtmp_free.append(p.inc(ac, evS))
                    else:
                        p.wait(s, mm, stoks[-1])
                        if len(pt_free) >= 4:
                            p.wait(s, mm, pt_free[-4])
                        ac = s.activation(PTb[:, ppar * 512:(ppar + 1) * 512],
                                          bank(bS), AF.Silu,
                                          bias=rbsb[:, h:h + 1], scale=INV_SQRT_HD)
                        tk = p.inc(ac, evS)
                        wtmp_free.append(tk)
                        sb_free.append((p.sem("evS"), tk))
                        p.inc(nc.vector.engine_nop(), evV)
                last_av = None
                for hh in range(4):
                    p.wait(tn, evS, wtmp_free[-4 + hh])
                    inst = tn.matmul(bank(hh),
                                     Vt[:, kpar * 512 + hh * 128:kpar * 512 + (hh + 1) * 128],
                                     PTb[:, hh * 512:(hh + 1) * 512],
                                     start=(kt == 0), stop=(kt == nkt - 1))
                    last_av = p.inc(inst, mm)
                    pt_free.append(last_av)
                kt_free.append(last_av)
            p.wait(v, utlS[qpar], utl_cnt[qpar] * 16)
            p.wait(v, mm)
            p.wait(v, mm, outT_free_mm)
            for hh in range(4):
                h = hg * 4 + hh
                p.inc(v.tensor_tensor(outT[:, h * 512:(h + 1) * 512], bank(hh),
                                      UTs[:, h * 512:(h + 1) * 512], ALU.mult), evV)
            acc_free = evV.n
        out_ready = evV.n
        def xr_load(kk, ttk):
            if len(xr_guard) >= 1:
                p.wait(s, trc, xr_guard[-1])
            p.dma(s, xrS[kk % 2], xr[:, (kk % 2) * 1024:(kk % 2 + 1) * 1024],
                  x_full[q0 + ttk * 128:q0 + (ttk + 1) * 128, :])
        for tt in range(4):
            k = qs * 4 + tt
            if tt == 0:
                xr_load(k, tt)
            if tt + 1 < 4:
                xr_load(k + 1, tt + 1)
            p.wait(tn, evV, out_ready)
            for d in range(8):
                bt = (6, 5, 4)[len(eptr_free) % 3]
                if len(eptr_free) >= 3:
                    p.wait(tn, trc, eptr_free[-3])
                inst = tn.transpose(bank(bt, 128),
                                    outT[:, d * 512 + tt * 128:d * 512 + (tt + 1) * 128],
                                    idfs[:, :])
                ttok = p.inc(inst, mm)
                p.wait(v, mm, ttok)
                p.wait(v, xrS[k % 2], None)
                if yd_cnt[k % 2] >= 1 and d == 0:
                    p.wait(v, ydS[k % 2], yd_cnt[k % 2] * 16)
                add = v.tensor_tensor(
                    yt[:, (k % 2) * 1024 + d * 128:(k % 2) * 1024 + (d + 1) * 128],
                    bank(bt, 128),
                    xr[:, (k % 2) * 1024 + d * 128:(k % 2) * 1024 + (d + 1) * 128],
                    ALU.add)
                eptr_free.append(p.inc(add, trc))
            xr_guard.append(trc.n)
            p.wait(v, trc)
            p.wait(s, trc)
            ytile = yt[:, (k % 2) * 1024:(k % 2 + 1) * 1024]
            p.wait(s, sV)
            ln_tile(p, ytile, ytile, sq[:, :], st, sV, sS,
                    cpool[:, 0:1], cpool[:, 1:2])
            p.wait(g, sS)
            p.dma(g, ydS[k % 2], y[k * 128:(k + 1) * 128, :], ytile)
            yd_cnt[k % 2] += 1
            p.wait(v, sS)
            if ybf_cnt[0] >= 1:
                p.wait(v, ydbS)
            p.inc(v.tensor_copy(ybf[:, (k % 2) * 1024:(k % 2 + 1) * 1024], ytile), trc)
            p.wait(sy, trc)
            p.dma(sy, ydbS, ybf_dram[k * 128:(k + 1) * 128, :],
                  ybf[:, (k % 2) * 1024:(k % 2 + 1) * 1024])
            ybf_cnt[0] += 1
        outT_free_mm = mm.n
        qts_free_mm = mm.n
        uts_free = mm.n


def prep_host_l1a(inputs):
    h = {}
    h["ident_f32"] = np.eye(128, dtype=np.float32)
    h["ident_bf16"] = np.eye(128, dtype=np.float32).astype(BF)

    def blk(w):
        return np.ascontiguousarray(np.asarray(w).reshape(8, 128, 1024).transpose(1, 0, 2))

    h["l1a_Wb"] = np.ascontiguousarray(np.stack([
        blk(inputs["L1_attn_Wq"]), blk(inputs["L1_attn_Wk"]),
        blk(inputs["L1_attn_Wv"]), blk(inputs["L1_attn_Wu"])]))
    rb = np.asarray(inputs["L1_attn_rel_bias"])
    NEGBIG = np.float32(-1e30 * np.sqrt(128.0))
    tabv = np.empty((8, 128, 2944), dtype=np.float32)
    jj = np.arange(128)[:, None]
    mm_ = np.arange(2944)[None, :]
    d = mm_ - 384 - jj
    idx = np.minimum(d + 2047, 4094)
    idx_c = np.clip(idx, 0, 4094)
    for hh in range(8):
        vals = rb[idx_c, hh]
        tabv[hh] = np.where(d < 0, NEGBIG, vals)
    h["l1a_tab"] = tabv
    h["l1a_rblast"] = np.broadcast_to(rb[4094, :], (128, 8)).copy()
    return h


def declare_l1a(nc, kind_act="ExternalInput"):
    io = {}
    io["l1a_x_full"] = nc.dram_tensor("l1a_x_full", [T, 1024], F32, kind=kind_act)
    io["l1a_y"] = nc.dram_tensor("l1a_y", [HTOK, 1024], F32, kind="ExternalOutput")
    io["l1a_ybf"] = nc.dram_tensor("l1a_ybf", [HTOK, 1024], BF16, kind="ExternalOutput")
    io["l1a_xbf"] = nc.dram_tensor("l1a_xbf", [T, 1024], BF16, kind="ExternalInput")
    io["l1a_Wb"] = nc.dram_tensor("l1a_Wb", [4, 128, 8, 1024], F32, kind="ExternalInput")
    io["l1a_tab"] = nc.dram_tensor("l1a_tab", [8, 128, 2944], F32, kind="ExternalInput")
    io["l1a_rblast"] = nc.dram_tensor("l1a_rblast", [128, 8], F32, kind="ExternalInput")
    io["ident_f32"] = nc.dram_tensor("ident_f32", [128, 128], F32, kind="ExternalInput")
    io["ident_bf16"] = nc.dram_tensor("ident_bf16", [128, 128], BF16, kind="ExternalInput")
    io["l1a_KT"] = nc.dram_tensor("l1a_KT", [8 * 128 * 8192], BF16, kind="Internal")
    io["l1a_km"] = nc.dram_tensor("l1a_km", [8 * 8192], F32, kind="Internal")
    io["l1a_V"] = nc.dram_tensor("l1a_V", [8192 * 1024], BF16, kind="Internal")
    io["l1a_QT"] = nc.dram_tensor("l1a_QT", [8 * 128 * 8192], BF16, kind="Internal")
    io["l1a_UT"] = nc.dram_tensor("l1a_UT", [8 * 128 * 8192], BF16, kind="Internal")
    return io


def build_l1a_program():
    nc = bass.Bass("TRN2", target_bir_lowering=False)
    io = declare_l1a(nc)
    p = Prog(nc)
    emit_l1_attn(p, io)
    return nc


def emit_l0_attn(p: Prog, io):
    nc = p.nc
    tn, v, s, g, sy = nc.tensor, nc.vector, nc.scalar, nc.gpsimd, nc.sync
    x_full = io["l0a_x_full"].ap()
    y = io["l0a_y"].ap()
    ybf_dram = io["l0a_ybf"].ap()
    Wqkrb = io["l0a_Wqkr"]
    Wvb = io["l0a_Wv"]
    Wgb = io["l0a_Wg"]
    Wob = io["l0a_Wo"]
    glnr = io["l0a_glnrep"]
    uvc = io["l0a_uv"]
    maskc = io["l0a_mask"]
    STt = io["l0a_sin"]
    idf = io["ident_f32"]
    idb = io["ident_bf16"]
    v_scr = io["l0a_v"]
    gT_scr = io["l0a_gT"]
    c2_scr = io["l0a_c2"]
    ptp_scr = io["l0a_pt"]
    s_scr = io["l0a_s"]

    mm = p.sem("mm"); evS = p.sem("evS"); evV = p.sem("evV")
    trc = p.sem("trc")
    sV = p.sem("lnV"); sS = p.sem("lnS")
    preS = p.sem("a0pre"); preG = p.sem("a0preG")
    stS = [p.sem("a0st0"), p.sem("a0st1")]
    xtS = [p.sem("a0xt0"), p.sem("a0xt1")]
    vstS = [p.sem("a0vst0"), p.sem("a0vst1")]
    gstS = [p.sem("a0gst0"), p.sem("a0gst1")]
    c2S = [p.sem("a0c20"), p.sem("a0c21"), p.sem("a0c22"), p.sem("a0c23")]
    bdS = [p.sem("a0bd0"), p.sem("a0bd1")]
    vldS = [p.sem("a0vld0"), p.sem("a0vld1")]
    gldS = [p.sem("a0gld0"), p.sem("a0gld1")]
    xrS = [p.sem("a0xr0"), p.sem("a0xr1")]
    ydS = [p.sem("a0yd0"), p.sem("a0yd1")]

    with ExitStack() as stk:
        def SB(*a):
            return stk.enter_context(nc.sbuf_tensor(*a))
        cpool = SB("a0cp", [128, 4], F32)
        idbs = SB("a0idb", [128, 128], BF16)
        idfs = SB("a0idf", [128, 128], F32)
        onescol = SB("a0on1", [128, 1], BF16)
        onesrow = SB("a0on2", [1, 128], BF16)
        uvs = SB("a0uv", [128, 2], F32)
        masks = SB("a0msk", [128, 4 * 512], F32)
        wqkr = SB("a0wqkr", [128, 3 * 1024], BF16)
        kT = SB("a0kT", [128, 8192], BF16)
        quT = SB("a0quT", [128, 8192], BF16)
        qvT = SB("a0qvT", [128, 8192], BF16)
        rT = SB("a0rT", [128, 9216], BF16)
        PS = stk.enter_context(nc.psum_tensor("a0ps", [128, 7 * 512], F32))
        stkP = stk.enter_context(ExitStack())

        def SBP(*a):
            return stkP.enter_context(nc.sbuf_tensor(*a))
        glns = SBP("a0gln", [128, 1024], F32)
        wvs = SBP("a0wv", [128, 8 * 2048], BF16)
        wgs = SBP("a0wg", [128, 8 * 2048], BF16)
        qT = SBP("a0qT", [128, 8192], BF16)

        def bank(i, w=512):
            return PS[:, i * 512:i * 512 + w]

        def bankbf(i):
            return PS[:, i * 512:(i + 1) * 512].bitcast(BF16)

        p.inc(v.memset(cpool[:, 0:1], LN_EPS), sV)
        p.inc(v.memset(cpool[:, 1:2], 1.0 / 1024.0), sV)
        p.inc(v.memset(cpool[:, 2:3], 1e-6), sV)
        p.inc(v.memset(onescol[:, :], 1.0), sV)
        p.inc(v.memset(onesrow[:, :], 1.0), sV)
        p.inc(v.memset(rT[:, 8192:9216], 0.0), sV)
        p.dma(sy, preS, idbs[:, :], idb.ap())
        p.dma(sy, preS, idfs[:, :], idf.ap())
        p.dma(sy, preS, glns[:, :], glnr.ap())
        p.dma(sy, preS, uvs[:, :], uvc.ap())
        for qt in range(4):
            p.dma(sy, preS, masks[:, qt * 512:(qt + 1) * 512],
                  bass.AP(maskc, qt * 128 * 512, [[512, 128], [1, 512]]))
        p.dma(g, preG, wqkr[:, :], bass.AP(Wqkrb, 0, [[1024, 128], [128 * 1024, 3], [1, 1024]]))
        p.dma(g, preG, wvs[:, :], bass.AP(Wvb, 0, [[8 * 2048, 128], [1, 8 * 2048]]))
        p.dma(g, preG, wgs[:, :], bass.AP(Wgb, 0, [[8 * 2048, 128], [1, 8 * 2048]]))
        pre_ready = preS.n
        preg_ready = preG.n

        with nc.sbuf_tensor("a0stb", [128, 2 * 8 * 512], BF16) as stb:
            st_cnt = [0, 0]
            for mc in range(16):
                par = mc % 2
                if st_cnt[par] >= 8:
                    p.wait(g, trc)
                for c in range(8):
                    p.dma(g, stS[par], stb[:, par * 4096 + c * 512:par * 4096 + (c + 1) * 512],
                          STt.ap()[c * 128:(c + 1) * 128, mc * 512:(mc + 1) * 512])
                st_cnt[par] += 8
                p.wait(tn, stS[par], st_cnt[par] * 16)
                p.wait(tn, preG, preg_ready)
                bR = mc % 2
                if mc >= 2:
                    p.wait(tn, trc)
                for c in range(8):
                    inst = tn.matmul(bank(bR), wqkr[:, 2 * 1024 + c * 128:2 * 1024 + (c + 1) * 128],
                                     stb[:, par * 4096 + c * 512:par * 4096 + (c + 1) * 512],
                                     start=(c == 0), stop=(c == 7))
                rtok = p.inc(inst, mm)
                p.wait(v, mm, rtok)
                p.inc(v.tensor_copy(rT[:, mc * 512:(mc + 1) * 512], bank(bR)), trc)

        p.fence_all([sy, g, tn, v, s])
        xt = SBP("a0xtb", [128, 2 * 1024], BF16)
        xfb = SBP("a0xf", [128, 2 * 1024], F32)
        xls = SBP("a0xls", [128, 2 * 8 * 512], BF16)
        rmss = SBP("a0rms", [128, 4], F32)
        vbuf = SBP("a0vb", [128, 2 * 2048], BF16)
        gbuf = SBP("a0gb", [128, 2 * 512], BF16)
        sqs = SBP("a0sq", [128, 1024], F32)

        xt_cnt = 0
        tr_free = []
        vst_cnt = [0, 0]
        gst_cnt = [0, 0]

        pend_xn = [None]
        xn_tok = []

        def flush_xnorm(p_):
            if pend_xn[0] is None:
                return
            kp, par, k, xtok = pend_xn[0]
            pend_xn[0] = None
            p_.wait(tn, evV, xtok)
            for c in range(8):
                if len(tr_free) >= 2:
                    p_.wait(tn, trc, tr_free[-2])
                bv = bankbf(2 + (c % 2))
                inst = tn.transpose(bv[:, 0:128],
                                    xt[:, kp * 1024 + c * 128:kp * 1024 + (c + 1) * 128],
                                    idbs[:, :])
                tok = p_.inc(inst, mm)
                p_.wait(v, mm, tok)
                cp = v.tensor_copy(
                    xls[:, par * 4096 + c * 512 + (k % 4) * 128:par * 4096 + c * 512 + (k % 4 + 1) * 128],
                    bv[:, 0:128])
                tr_free.append(p_.inc(cp, trc))

        def build_xnorm(src_ap, k, par):
            nonlocal xt_cnt
            kp = k % 2
            if len(xn_tok) >= 2:
                p.wait(sy, evV, xn_tok[-2])
            p.dma(sy, xtS[kp], xfb[:, kp * 1024:(kp + 1) * 1024],
                  src_ap[k * 128:(k + 1) * 128, :])
            xt_cnt += 1
            p.wait(s, xtS[kp], None)
            p.wait(s, preS, pre_ready)
            p.wait(s, sV)
            p.wait(s, evV, evV.n)
            p.inc(s.activation(sqs[:, :], xfb[:, kp * 1024:(kp + 1) * 1024],
                               AF.Square, bias=0.0, scale=1.0), evS)
            p.wait(v, evS)
            p.inc(v.reduce_sum(rmss[:, 0:1], sqs[:, :], axis=AX.X), evV)
            p.wait(s, evV)
            p.inc(s.activation(rmss[:, 1:2], rmss[:, 0:1], AF.Sqrt,
                               bias=cpool[:, 2:3], scale=cpool[:, 1:2]), evS)
            p.wait(v, evS)
            p.inc(v.reciprocal(rmss[:, 2:3], rmss[:, 1:2]), evV)
            p.wait(v, evV)
            stt = v.scalar_tensor_tensor(xt[:, kp * 1024:(kp + 1) * 1024],
                                         xfb[:, kp * 1024:(kp + 1) * 1024],
                                         rmss[:, 2:3], glns[:, :],
                                         ALU.mult, ALU.mult)
            xtok = p.inc(stt, evV)
            xn_tok.append(xtok)
            flush_xnorm(p)
            pend_xn[0] = (kp, par, k, xtok)

        for fs in range(16):
            par = fs % 2
            if fs >= 2:
                p.wait(v, mm)
            for tt in range(4):
                build_xnorm(x_full, fs * 4 + tt, par)
            flush_xnorm(p)
            p.wait(tn, trc)
            bK = fs % 2
            p.wait(tn, evV, evV.n)
            for c in range(8):
                inst = tn.matmul(bank(bK), wqkr[:, 1 * 1024 + c * 128:1 * 1024 + (c + 1) * 128],
                                 xls[:, par * 4096 + c * 512:par * 4096 + (c + 1) * 512],
                                 start=(c == 0), stop=(c == 7))
            ktok = p.inc(inst, mm)
            p.wait(v, mm, ktok)
            p.inc(v.tensor_copy(kT[:, fs * 512:(fs + 1) * 512], bank(bK)), trc)
            bQ = 1 - (fs % 2)
            for c in range(8):
                inst = tn.matmul(bank(bQ), wqkr[:, 0 * 1024 + c * 128:0 * 1024 + (c + 1) * 128],
                                 xls[:, par * 4096 + c * 512:par * 4096 + (c + 1) * 512],
                                 start=(c == 0), stop=(c == 7))
            qtok = p.inc(inst, mm)
            p.wait(v, mm, qtok)
            p.inc(v.tensor_copy(qT[:, fs * 512:(fs + 1) * 512], bank(bQ)), trc)
            for vc in range(16):
                bG = 4 + (vc % 2)
                p.wait(tn, evS, evS.n)
                for c in range(8):
                    inst = tn.matmul(bank(bG),
                                     wgs[:, c * 2048 + vc * 128:c * 2048 + (vc + 1) * 128],
                                     xls[:, par * 4096 + c * 512:par * 4096 + (c + 1) * 512],
                                     start=(c == 0), stop=(c == 7))
                gtok = p.inc(inst, mm)
                p.wait(s, mm, gtok)
                gp = vc % 2
                if gst_cnt[gp] >= 1:
                    p.wait(s, gstS[gp], gst_cnt[gp] * 16)
                p.inc(s.activation(gbuf[:, gp * 512:(gp + 1) * 512], bank(bG),
                                   AF.Silu, bias=0.0, scale=1.0), evS)
                p.wait(g, evS)
                p.dma(g, gstS[gp],
                      bass.AP(gT_scr, vc * 128 * T + fs * 512, [[T, 128], [1, 512]]),
                      gbuf[:, gp * 512:(gp + 1) * 512])
                gst_cnt[gp] += 1
            for tt in range(4):
                tp = tt % 2
                p.wait(tn, evS, evS.n)
                for half in range(4):
                    bV = (4, 5, 4, 5)[half]
                    if half >= 2:
                        p.wait(tn, evS, evS.n)
                    for c in range(8):
                        inst = tn.matmul(bank(bV),
                                         xls[:, par * 4096 + c * 512 + tt * 128:par * 4096 + c * 512 + (tt + 1) * 128],
                                         wvs[:, c * 2048 + half * 512:c * 2048 + (half + 1) * 512],
                                         start=(c == 0), stop=(c == 7))
                    vtok = p.inc(inst, mm)
                    p.wait(s, mm, vtok)
                    if vst_cnt[tp] >= 1 and half == 0:
                        p.wait(s, vstS[tp], vst_cnt[tp] * 16)
                    p.inc(s.activation(vbuf[:, tp * 2048 + half * 512:tp * 2048 + (half + 1) * 512],
                                       bank(bV), AF.Identity, bias=0.0, scale=1.0), evS)
                p.wait(g, evS)
                p.dma(g, vstS[tp],
                      bass.AP(v_scr, (fs * 4 + tt) * 128 * 2048, [[2048, 128], [1, 2048]]),
                      vbuf[:, tp * 2048:(tp + 1) * 2048])
                vst_cnt[tp] += 1
        p.wait(s, trc)
        p.wait(s, preS, pre_ready)
        p.inc(s.activation(quT[:, :], qT[:, :], AF.Identity,
                           bias=uvs[:, 0:1], scale=INV_SQRT_HD), evS)
        p.inc(s.activation(qvT[:, :], qT[:, :], AF.Identity,
                           bias=uvs[:, 1:2], scale=INV_SQRT_HD), evS)
        qready = evS.n

        p.fence_all([sy, g, tn, v, s])
        stkP.close()
        wos = SB("a0wo", [128, 16 * 1024], BF16)
        p.dma(g, preG, wos[:, :], bass.AP(Wob, 0, [[16 * 1024, 128], [1, 16 * 1024]]))
        wos_ready = preG.n
        PTkc = SB("a0PT", [128, 4 * 512], BF16)
        ptv = SB("a0ptv", [128, 4 * 512], BF16)
        cbuf = SB("a0cb", [128, 4 * 512], F32)
        bdb = SB("a0bd", [128, 4 * 512], F32)
        sld = SB("a0sld", [128, 4 * 512], F32)
        stile = SB("a0stile", [128, 2 * 1024], F32)
        Pbuf = SB("a0P", [128, 2 * 512], BF16)
        rmb = SB("a0rm", [128, 12], F32)
        lrow = SB("a0lr", [1, 512], F32)
        lrbf = SB("a0lrb", [1, 512], BF16)
        lacc = SB("a0lacc", [128, 4], F32)
        lpart = SB("a0lp", [128, 2], F32)
        lr4 = SB("a0lr4", [4, 128], F32)
        lr4b = SB("a0lr4b", [4, 128], BF16)
        linv = SB("a0li", [128, 512], F32)
        vtile = SB("a0vt", [128, 4 * 512], BF16)
        gts = SB("a0gts", [128, 2 * 512], BF16)
        gated = SB("a0gated", [128, 16 * 512], BF16)
        tmpf = SB("a0tmpf", [128, 512], F32)
        yt = SB("a0yt", [128, 2 * 1024], F32)
        xr = SB("a0xr", [128, 2 * 1024], F32)
        sq2 = SB("a0sq2", [128, 1024], F32)
        st2 = SB("a0st2", [128, 4], F32)
        ybf = SB("a0ybf", [128, 1024], BF16)

        env = dict(locals())
        emit_l0a_body(p, list(range(16)), env)
        p.wait(g, ydS[0])
        p.wait(g, ydS[1])


def emit_l0a_body(p: Prog, SUPS, env):
    nc = p.nc
    tn, v, s, g, sy = nc.tensor, nc.vector, nc.scalar, nc.gpsimd, nc.sync
    mm = p.sem("mm"); evS = p.sem("evS"); evV = p.sem("evV")
    trc = p.sem("trc")
    sV = p.sem("lnV"); sS = p.sem("lnS")
    c2S = [p.sem("a0c20"), p.sem("a0c21"), p.sem("a0c22"), p.sem("a0c23")]
    bdS = [p.sem("a0bd0"), p.sem("a0bd1")]
    vldS = [p.sem("a0vld0"), p.sem("a0vld1")]
    gldS = [p.sem("a0gld0"), p.sem("a0gld1")]
    xrS = [p.sem("a0xr0"), p.sem("a0xr1")]
    ydS = [p.sem("a0yd0"), p.sem("a0yd1")]
    E = env
    bank = E["bank"]; bankbf = E["bankbf"]
    quT = E["quT"]; qvT = E["qvT"]; kT = E["kT"]; rT = E["rT"]
    PTkc = E["PTkc"]; ptv = E["ptv"]; ptp_scr = E["ptp_scr"]; s_scr = E["s_scr"]
    sld = E["sld"]; lacc = E["lacc"]; lpart = E["lpart"]; lr4 = E["lr4"]; lr4b = E["lr4b"]
    cbuf = E["cbuf"]; bdb = E["bdb"]; stile = E["stile"]
    Pbuf = E["Pbuf"]; rmb = E["rmb"]; lrow = E["lrow"]; lrbf = E["lrbf"]
    linv = E["linv"]; vtile = E["vtile"]; gts = E["gts"]; gated = E["gated"]
    tmpf = E["tmpf"]; yt = E["yt"]; xr = E["xr"]; sq2 = E["sq2"]; st2 = E["st2"]
    masks = E["masks"]; onescol = E["onescol"]; onesrow = E["onesrow"]
    idfs = E["idfs"]; idbs = E["idbs"]; cpool = E["cpool"]; wos = E["wos"]
    v_scr = E["v_scr"]; gT_scr = E["gT_scr"]; c2_scr = E["c2_scr"]
    x_own = E["x_full"]; y = E["y"]; qready = E["qready"]; pre_ready = E["pre_ready"]
    ybf = E["ybf"]; ybf_dram = E["ybf_dram"]
    ydbS = p.sem("ydb")
    ybf_cnt = [0]

    p.wait(tn, evS, qready)
    p.wait(v, p.sem("a0pre"), pre_ready)
    scp = p.sem("a0scp")
    c2_cnt = [0, 0, 0, 0]
    ss_cnt = [0, 0]
    bd_cnt = [0, 0]
    ibB = 0
    lp_free = {}
    lr_cnt = [0]
    exp_free = []
    sld_cnt = [0, 0, 0, 0]
    pexp_free = []
    ptcopy = []
    ptst_cnt = [0, 0]
    ptld_cnt = [0, 0]
    vld_cnt = [0, 0]
    ptld_cnt4 = [0, 0]
    gld_cnt = [0, 0]
    xr_guard = []
    yd_cnt = [0, 0]
    cb_free = []
    bd_done_evV = 0
    pt_free_mm = 0

    for qs, sup in enumerate(SUPS):
        q0 = sup * 512
        nkc = sup + 1
        m_lo = 8192 - 512 - q0

        p.wait(tn, evV, bd_done_evV)
        for qq in range(4):
            for mc in range(sup + 2):
                i = qq * (sup + 2) + mc
                b = i % 4
                if len(cb_free) >= 4:
                    fs_, ft_ = cb_free[-4]
                    p.wait(tn, fs_, ft_)
                inst = tn.matmul(bank(b),
                                 qvT[:, qs * 512 + qq * 128:qs * 512 + (qq + 1) * 128],
                                 rT[:, m_lo + mc * 512:m_lo + (mc + 1) * 512],
                                 start=True, stop=True)
                ctok = p.inc(inst, mm)
                ceng = v if b % 2 == 0 else s
                csem = evV if b % 2 == 0 else scp
                p.wait(ceng, mm, ctok)
                p.wait(ceng, evV, bd_done_evV)
                if c2_cnt[b] >= 1:
                    p.wait(ceng, c2S[b], c2_cnt[b] * 16)
                cp = ceng.tensor_copy(cbuf[:, b * 512:(b + 1) * 512], bank(b)) if b % 2 == 0 \
                    else s.activation(cbuf[:, b * 512:(b + 1) * 512], bank(b), AF.Identity,
                                      bias=0.0, scale=1.0)
                cb_free.append((csem, p.inc(cp, csem)))
                p.wait(g, csem)
                p.dma(g, c2S[b],
                      bass.AP(c2_scr, (qq * 128) * 8704 + mc * 512, [[8704, 128], [1, 512]]),
                      cbuf[:, b * 512:(b + 1) * 512])
                c2_cnt[b] += 1
        strip_done = [c2S[0].n, c2S[1].n, c2S[2].n, c2S[3].n]

        p.wait(tn, evV)
        p.wait(tn, scp)
        p.wait(v, evS)
        p.inc(v.memset(rmb[:, 0:8], -3.0e38), sV)
        p.wait(v, sV)
        ttr_free = []
        ssS = [p.sem("a0ss0"), p.sem("a0ss1")]
        groups = []
        for qt in range(4):
            kc0 = 0
            while kc0 < nkc:
                w = 2 if kc0 + 1 < nkc else 1
                groups.append((qt, kc0, w))
                kc0 += w
        if True:
            for gi, (qt, kc0, w) in enumerate(groups):
                bp = (gi % 2) * 2
                if len(ttr_free) >= 2:
                    p.wait(tn, evV, ttr_free[-2])
                atok = None
                for j in range(w):
                    inst = tn.matmul(bank(bp + j),
                                     quT[:, qs * 512 + qt * 128:qs * 512 + (qt + 1) * 128],
                                     kT[:, (kc0 + j) * 512:(kc0 + j + 1) * 512],
                                     start=True, stop=True)
                    atok = p.inc(inst, mm)
                p.wait(sy, c2S[0], strip_done[0])
                p.wait(sy, c2S[1], strip_done[1])
                p.wait(sy, c2S[2], strip_done[2])
                p.wait(sy, c2S[3], strip_done[3])
                if len(ttr_free) >= 2:
                    p.wait(sy, evV, ttr_free[-2])
                for j in range(w):
                    p.dma(sy, bdS[gi % 2],
                          bdb[:, (bp + j) * 512:(bp + j + 1) * 512],
                          bass.AP(c2_scr, (qt * 128) * 8704 + (511 - 128 * qt) + (kc0 + j) * 512,
                                  [[8703, 128], [1, 512]]))
                    bd_cnt[gi % 2] += 1
                p.wait(v, bdS[gi % 2], bd_cnt[gi % 2] * 16)
                p.wait(v, mm, atok)
                if kc0 + w == nkc:
                    dslot = bp + w - 1
                    p.inc(v.tensor_tensor(bdb[:, dslot * 512:(dslot + 1) * 512],
                                          bdb[:, dslot * 512:(dslot + 1) * 512],
                                          masks[:, qt * 512:(qt + 1) * 512], ALU.add), evV)
                    p.wait(v, evV)
                else:
                    p.inc(nc.vector.engine_nop(), evV)
                W = w * 512
                spar = gi % 2
                p.wait(v, ssS[spar], ss_cnt[spar] * 16)
                tta = v.tensor_tensor(stile[:, 0:W] if spar == 0 else stile[:, 1024:1024 + W],
                                      bank(bp, W), bdb[:, bp * 512:bp * 512 + W], ALU.add)
                ttr_free.append(p.inc(tta, evV))
                p.wait(g, evV)
                p.dma(g, ssS[spar],
                      bass.AP(s_scr, (qt * 128) * 8192 + kc0 * 512, [[8192, 128], [1, W]]),
                      stile[:, 0:W] if spar == 0 else stile[:, 1024:1024 + W])
                ss_cnt[spar] += 1
                p.wait(v, evV)
                p.inc(v.reduce_max(rmb[:, 4:5], stile[:, 0:W] if spar == 0 else stile[:, 1024:1024 + W],
                                   axis=AX.X), evV)
                p.wait(v, evV)
                p.inc(v.tensor_tensor(rmb[:, qt:qt + 1], rmb[:, qt:qt + 1],
                                      rmb[:, 4:5], ALU.max), evV)
                p.wait(v, evV)
        for qt in range(4):
            p.inc(v.tensor_scalar_mul(rmb[:, 8 + qt:9 + qt],
                                      rmb[:, qt:qt + 1], -1.0), evV)
        p.wait(v, evV)
        negm_ready = evV.n

        p.wait(v, mm)
        p.inc(v.memset(lacc[:, 0:4], 0.0), sV)
        p.wait(v, sV)

        ptS = [p.sem("a0pt0"), p.sem("a0pt1")]
        lmm_last = None
        pendingB = None

        def do_transposes(j, bj, qtj, exptok):
            p.wait(tn, evS, exptok)
            for kb in range(4):
                jj = j * 4 + kb
                bt = (2, 3, 0, 1)[jj % 4]
                if len(ptcopy) >= 4:
                    fs_, ft_ = ptcopy[-4]
                    p.wait(tn, fs_, ft_)
                inst2 = tn.transpose(bankbf(bt)[:, 0:128],
                                     Pbuf[:, bj * 512 + kb * 128:bj * 512 + (kb + 1) * 128],
                                     idbs[:, :])
                ttok2 = p.inc(inst2, mm)
                if kb == 3:
                    pexp_free.append(ttok2)
                ceng = v if kb % 2 == 0 else s
                csem = trc if kb % 2 == 0 else scp
                p.wait(ceng, mm, ttok2)
                p.wait(ceng, ptS[kb % 2], ptst_cnt[kb % 2] * 16)
                if kb % 2 == 0:
                    cp = v.tensor_copy(
                        PTkc[:, kb * 512 + qtj * 128:kb * 512 + (qtj + 1) * 128],
                        bankbf(bt)[:, 0:128])
                else:
                    cp = s.activation(
                        PTkc[:, kb * 512 + qtj * 128:kb * 512 + (qtj + 1) * 128],
                        bankbf(bt)[:, 0:128], AF.Identity, bias=0.0, scale=1.0)
                ptcopy.append((csem, p.inc(cp, csem)))

        p.wait(tn, evV)
        sldS = [p.sem("a0sl0"), p.sem("a0sl1"), p.sem("a0sl2"), p.sem("a0sl3")]
        for kc in range(nkc):
            for qt in range(4):
                i = kc * 4 + qt
                b = ibB % 4
                bb = ibB % 2
                ibB += 1
                if len(exp_free) >= 4:
                    p.wait(sy, evS, exp_free[-4])
                p.wait(sy, ssS[0], ss_cnt[0] * 16)
                p.wait(sy, ssS[1], ss_cnt[1] * 16)
                p.dma(sy, sldS[b], sld[:, b * 512:(b + 1) * 512],
                      bass.AP(s_scr, (qt * 128) * 8192 + kc * 512, [[8192, 128], [1, 512]]))
                sld_cnt[b] += 1
                p.wait(s, sldS[b], sld_cnt[b] * 16)
                p.wait(s, evV, negm_ready)
                if len(pexp_free) >= 2:
                    p.wait(s, mm, pexp_free[-2])
                p.wait(s, evV, lp_free.get(bb, 0))
                ex = s.activation(Pbuf[:, bb * 512:(bb + 1) * 512],
                                  sld[:, b * 512:(b + 1) * 512], AF.Exp,
                                  bias=rmb[:, 8 + qt:9 + qt], scale=1.0,
                                  accum_out=lpart[:, bb:bb + 1])
                exp_free.append(p.inc(ex, evS))
                p.wait(v, evS, exp_free[-1])
                p.wait(v, sV)
                lp_free[bb] = p.inc(v.tensor_tensor(lacc[:, qt:qt + 1], lacc[:, qt:qt + 1],
                                                    lpart[:, bb:bb + 1], ALU.add), evV)
                p.wait(v, evV)
                if pendingB is not None:
                    do_transposes(*pendingB)
                pendingB = (i, bb, qt, exp_free[-1])
            if pendingB is not None:
                do_transposes(*pendingB)
                pendingB = None
            for kb in range(4):
                fs_, ft_ = ptcopy[-4 + kb]
                p.wait(g, fs_, ft_)
                p.dma(g, ptS[kb % 2],
                      bass.AP(ptp_scr, (kc * 4 + kb) * 128 * 512, [[512, 128], [1, 512]]),
                      PTkc[:, kb * 512:(kb + 1) * 512])
                ptst_cnt[kb % 2] += 1
        p.wait(tn, evV)
        inst = tn.transpose(bank(6)[0:4, 0:128], lacc[:, 0:4], idfs[:, :])
        btok = p.inc(inst, mm)
        p.wait(v, mm, btok)
        p.inc(v.reciprocal(lr4[0:4, :], bank(6)[0:4, 0:128]), evV)
        p.wait(v, evV)
        p.inc(v.tensor_copy(lr4b[0:4, :], lr4[0:4, :]), evV)
        p.wait(sy, evV)
        p.dma(sy, p.sem("a0lrS"), bass.AP(lrbf[:, :].tensor, 0, [[512, 1], [1, 512]]),
              bass.AP(lr4b[0:4, :].tensor, 0, [[128, 4], [1, 128]]))
        lr_cnt[0] += 1
        p.wait(tn, p.sem("a0lrS"), lr_cnt[0] * 16)
        inst = tn.matmul(bank(6), onesrow[:, :], lrbf[:, :], start=True, stop=True)
        btok = p.inc(inst, mm)
        p.wait(v, mm, btok)
        p.inc(v.tensor_copy(linv[:, :], bank(6)), evV)
        linv_ready = evV.n

        pv_free = []
        gate_done = []
        tsl_tok = {0: 0, 1: 0}
        ptlS = [p.sem("a0ptl0"), p.sem("a0ptl1")]
        nkt = nkc * 4
        for vq in range(4):
            p.wait(tn, evV)
            for kt2 in range(0, nkt, 2):
                kp = (kt2 // 2) % 2
                if len(pv_free) >= 2:
                    p.wait(g, mm, pv_free[-2])
                p.dma(g, vldS[kp],
                      vtile[:, kp * 1024:(kp + 1) * 1024],
                      bass.AP(v_scr, kt2 * 128 * 2048 + vq * 512,
                              [[2048, 128], [262144, 2], [1, 512]]))
                vld_cnt[kp] += 1
                if len(pv_free) >= 2:
                    p.wait(s, mm, pv_free[-2])
                    p.wait(sy, mm, pv_free[-2])
                p.wait(s, ptS[0]); p.wait(s, ptS[1])
                p.wait(sy, ptS[0]); p.wait(sy, ptS[1])
                p.dma(sy, ptlS[kp],
                      ptv[:, kp * 1024:kp * 1024 + 512],
                      bass.AP(ptp_scr, kt2 * 128 * 512, [[512, 128], [1, 512]]))
                p.dma(s, ptlS[kp],
                      ptv[:, kp * 1024 + 512:(kp + 1) * 1024],
                      bass.AP(ptp_scr, (kt2 + 1) * 128 * 512, [[512, 128], [1, 512]]))
                ptld_cnt4[kp] += 2
                p.wait(tn, vldS[kp], vld_cnt[kp] * 16)
                p.wait(tn, ptlS[kp], ptld_cnt4[kp] * 16)
                inst = None
                for j in range(2):
                    kt = kt2 + j
                    for vc in range(4):
                        inst = tn.matmul(bank(vc),
                                         vtile[:, kp * 1024 + j * 512 + vc * 128:kp * 1024 + j * 512 + (vc + 1) * 128],
                                         ptv[:, kp * 1024 + j * 512:kp * 1024 + (j + 1) * 512],
                                         start=(kt == 0), stop=(kt == nkt - 1))
                pv_free.append(p.inc(inst, mm))
            pv_done = mm.n
            for vc in range(4):
                vcg = vq * 4 + vc
                gp = vcg % 2
                tp_ = vc % 2
                tslot = sld[:, tp_ * 512:(tp_ + 1) * 512]
                if len(gate_done) >= 2:
                    p.wait(sy, scp, gate_done[-2])
                p.dma(sy, gldS[gp],
                      gts[:, gp * 512:(gp + 1) * 512],
                      bass.AP(gT_scr, vcg * 128 * T + qs * 512, [[T, 128], [1, 512]]))
                gld_cnt[gp] += 1
                p.wait(v, mm, pv_done)
                p.wait(v, evV, linv_ready)
                if tsl_tok[tp_]:
                    p.wait(v, scp, tsl_tok[tp_])
                tm = v.tensor_tensor(tslot, bank(vc), linv[:, :], ALU.mult)
                ttok = p.inc(tm, evV)
                p.wait(g, evV, ttok)
                p.wait(g, gldS[gp], gld_cnt[gp] * 16)
                mul2 = g.tensor_tensor(gated[:, vcg * 512:(vcg + 1) * 512],
                                       tslot, gts[:, gp * 512:(gp + 1) * 512], ALU.mult)
                tsl_tok[tp_] = p.inc(mul2, scp)
                gate_done.append(tsl_tok[tp_])
        gated_ready = scp.n

        wo_evac = []
        def xr_load(kk):
            if len(xr_guard) >= 1:
                p.wait(s, trc, xr_guard[-1])
            p.dma(s, xrS[kk % 2], xr[:, (kk % 2) * 1024:(kk % 2 + 1) * 1024],
                  x_own[kk * 128:(kk + 1) * 128, :])
        for qt in range(4):
            k = qs * 4 + qt
            if qt == 0:
                xr_load(k)
            if qt + 1 < 4:
                xr_load(k + 1)
            p.wait(tn, scp, gated_ready)
            p.wait(tn, p.sem("a0preG"), E["wos_ready"])
            for dh in range(2):
                b = 4 + dh
                if len(wo_evac) >= 2:
                    p.wait(tn, trc, wo_evac[-2])
                for vc in range(16):
                    inst = tn.matmul(bank(b),
                                     gated[:, vc * 512 + qt * 128:vc * 512 + (qt + 1) * 128],
                                     wos[:, vc * 1024 + dh * 512:vc * 1024 + (dh + 1) * 512],
                                     start=(vc == 0), stop=(vc == 15))
                otok = p.inc(inst, mm)
                p.wait(v, mm, otok)
                p.wait(v, xrS[k % 2], None)
                if yd_cnt[k % 2] >= 1 and dh == 0:
                    p.wait(v, ydS[k % 2], yd_cnt[k % 2] * 16)
                add = v.tensor_tensor(
                    yt[:, (k % 2) * 1024 + dh * 512:(k % 2) * 1024 + (dh + 1) * 512],
                    bank(b),
                    xr[:, (k % 2) * 1024 + dh * 512:(k % 2) * 1024 + (dh + 1) * 512],
                    ALU.add)
                wo_evac.append(p.inc(add, trc))
            xr_guard.append(trc.n)
            p.wait(v, trc)
            p.wait(s, trc)
            ytile = yt[:, (k % 2) * 1024:(k % 2 + 1) * 1024]
            p.wait(s, sV)
            ln_tile(p, ytile, ytile, sq2[:, :], st2, sV, sS,
                    cpool[:, 0:1], cpool[:, 1:2])
            p.wait(g, sS)
            p.dma(g, ydS[k % 2], y[k * 128:(k + 1) * 128, :], ytile)
            yd_cnt[k % 2] += 1
            p.wait(v, sS)
            if ybf_cnt[0] >= 1:
                p.wait(v, ydbS)
            p.inc(v.tensor_copy(ybf[:, :], ytile), trc)
            p.wait(sy, trc)
            p.dma(sy, ydbS, ybf_dram[k * 128:(k + 1) * 128, :], ybf[:, :])
            ybf_cnt[0] += 1
        bd_done_evV = evV.n
        pt_free_mm = mm.n


def prep_host_l0a(inputs):
    h = {}
    h["ident_f32"] = np.eye(128, dtype=np.float32)
    h["ident_bf16"] = np.eye(128, dtype=np.float32).astype(BF)
    Wq = np.asarray(inputs["L0_attn_W_q"])
    Wk = np.asarray(inputs["L0_attn_W_k"])
    Wr = np.asarray(inputs["L0_attn_W_r"])

    def blk128(w):
        return np.ascontiguousarray(w.reshape(8, 128, 128).transpose(1, 0, 2))

    h["l0a_Wqkr"] = np.ascontiguousarray(
        np.stack([blk128(Wq), blk128(Wk), blk128(Wr)]))
    h["l0a_Wv"] = np.ascontiguousarray(
        np.asarray(inputs["L0_attn_W_v"]).reshape(8, 128, 2048).transpose(1, 0, 2))
    h["l0a_Wg"] = np.ascontiguousarray(
        np.asarray(inputs["L0_attn_W_g"]).reshape(8, 128, 2048).transpose(1, 0, 2))
    h["l0a_Wo"] = np.ascontiguousarray(
        np.asarray(inputs["L0_attn_W_o"]).reshape(16, 128, 1024).transpose(1, 0, 2))
    h["l0a_glnrep"] = np.broadcast_to(np.asarray(inputs["L0_attn_g_ln"]), (128, 1024)).copy()
    uv = np.stack([np.asarray(inputs["L0_attn_xl_u"]),
                   np.asarray(inputs["L0_attn_xl_v"])], axis=1)
    h["l0a_uv"] = np.ascontiguousarray(uv.astype(np.float32))
    mask = np.zeros((4, 128, 512), dtype=np.float32)
    for qt in range(4):
        a = np.arange(128)[:, None]
        jj = np.arange(512)[None, :]
        mask[qt] = np.where(jj <= qt * 128 + a, 0.0, -1e30)
    h["l0a_mask"] = mask
    pos = np.arange(T, dtype=np.float64)
    inv_lams = 1.0 / (10000.0 ** (np.arange(0, 1024, 2, dtype=np.float64) / 1024.0))
    pre = pos[:, None] * inv_lams[None, :]
    emb = np.concatenate([np.sin(pre), np.cos(pre)], axis=1)
    emb = emb[::-1]
    h["l0a_sin"] = np.ascontiguousarray(emb.T.astype(np.float32))
    return h


def declare_l0a(nc, kind_act="ExternalInput"):
    io = {}
    io["l0a_x_full"] = nc.dram_tensor("l0a_x_full", [T, 1024], F32, kind=kind_act)
    io["l0a_y"] = nc.dram_tensor("l0a_y", [T, 1024], F32, kind="ExternalOutput")
    io["l0a_ybf"] = nc.dram_tensor("l0a_ybf", [T, 1024], BF16, kind="ExternalOutput")
    io["l0a_Wqkr"] = nc.dram_tensor("l0a_Wqkr", [3, 128, 8, 128], F32, kind="ExternalInput")
    io["l0a_Wv"] = nc.dram_tensor("l0a_Wv", [128, 8, 2048], F32, kind="ExternalInput")
    io["l0a_Wg"] = nc.dram_tensor("l0a_Wg", [128, 8, 2048], F32, kind="ExternalInput")
    io["l0a_Wo"] = nc.dram_tensor("l0a_Wo", [128, 16, 1024], F32, kind="ExternalInput")
    io["l0a_glnrep"] = nc.dram_tensor("l0a_glnrep", [128, 1024], F32, kind="ExternalInput")
    io["l0a_uv"] = nc.dram_tensor("l0a_uv", [128, 2], F32, kind="ExternalInput")
    io["l0a_mask"] = nc.dram_tensor("l0a_mask", [4, 128, 512], F32, kind="ExternalInput")
    io["l0a_sin"] = nc.dram_tensor("l0a_sin", [1024, T], F32, kind="ExternalInput")
    io["ident_f32"] = nc.dram_tensor("ident_f32", [128, 128], F32, kind="ExternalInput")
    io["ident_bf16"] = nc.dram_tensor("ident_bf16", [128, 128], BF16, kind="ExternalInput")
    io["l0a_v"] = nc.dram_tensor("l0a_v", [T * 2048], BF16, kind="Internal")
    io["l0a_gT"] = nc.dram_tensor("l0a_gT", [2048 * T], BF16, kind="Internal")
    io["l0a_c2"] = nc.dram_tensor("l0a_c2", [512 * 8704], F32, kind="Internal")
    io["l0a_pt"] = nc.dram_tensor("l0a_pt", [64 * 128 * 512], BF16, kind="Internal")
    io["l0a_s"] = nc.dram_tensor("l0a_s", [512 * 8192], F32, kind="Internal")
    return io


def build_l0a_program():
    nc = bass.Bass("TRN2", target_bir_lowering=False)
    io = declare_l0a(nc)
    p = Prog(nc)
    emit_l0_attn(p, io)
    return nc


def build_fused_program():
    nc = bass.Bass("TRN2", target_bir_lowering=False)
    io = {}
    io["x_full"] = nc.dram_tensor("x_full", [T, 1024], F32, kind="ExternalInput")
    io["out_y"] = nc.dram_tensor("out_y", [HTOK, 1024], F32, kind="ExternalOutput")
    x1f = nc.dram_tensor("act_x1", [T, 1024], F32, kind="Internal")
    x2f = nc.dram_tensor("act_x2", [T, 1024], F32, kind="Internal")
    x3 = nc.dram_tensor("act_x3", [HTOK, 1024], F32, kind="Internal")
    x1b = nc.dram_tensor("act_x1b", [T, 1024], BF16, kind="Internal")
    x2b = nc.dram_tensor("act_x2b", [T, 1024], BF16, kind="Internal")
    x3b = nc.dram_tensor("act_x3b", [HTOK, 1024], BF16, kind="Internal")

    io["ident_f32"] = nc.dram_tensor("ident_f32", [128, 128], F32, kind="ExternalInput")
    io["ident_bf16"] = nc.dram_tensor("ident_bf16", [128, 128], BF16, kind="ExternalInput")
    io["l0a_Wqkr"] = nc.dram_tensor("l0a_Wqkr", [3, 128, 8, 128], F32, kind="ExternalInput")
    io["l0a_Wv"] = nc.dram_tensor("l0a_Wv", [128, 8, 2048], F32, kind="ExternalInput")
    io["l0a_Wg"] = nc.dram_tensor("l0a_Wg", [128, 8, 2048], F32, kind="ExternalInput")
    io["l0a_Wo"] = nc.dram_tensor("l0a_Wo", [128, 16, 1024], F32, kind="ExternalInput")
    io["l0a_glnrep"] = nc.dram_tensor("l0a_glnrep", [128, 1024], F32, kind="ExternalInput")
    io["l0a_uv"] = nc.dram_tensor("l0a_uv", [128, 2], F32, kind="ExternalInput")
    io["l0a_mask"] = nc.dram_tensor("l0a_mask", [4, 128, 512], F32, kind="ExternalInput")
    io["l0a_sin"] = nc.dram_tensor("l0a_sin", [1024, T], F32, kind="ExternalInput")
    io["l0a_v"] = nc.dram_tensor("l0a_v", [T * 2048], BF16, kind="Internal")
    io["l0a_gT"] = nc.dram_tensor("l0a_gT", [2048 * T], BF16, kind="Internal")
    io["l0a_c2"] = nc.dram_tensor("l0a_c2", [512 * 8704], F32, kind="Internal")
    io["l0a_pt"] = nc.dram_tensor("l0a_pt", [64 * 128 * 512], BF16, kind="Internal")
    io["l0a_s"] = nc.dram_tensor("l0a_s", [512 * 8192], F32, kind="Internal")
    io["l0a_x_full"] = io["x_full"]
    io["l0a_y"] = x1f
    io["l0a_ybf"] = x1b
    io["l0f_We1b"] = nc.dram_tensor("l0f_We1b", [16, 128, 8, 512], F32, kind="ExternalInput")
    io["l0f_We3b"] = nc.dram_tensor("l0f_We3b", [16, 128, 8, 512], F32, kind="ExternalInput")
    io["l0f_We2b"] = nc.dram_tensor("l0f_We2b", [16, 128, 4, 1024], F32, kind="ExternalInput")
    io["l0f_ws1b"] = nc.dram_tensor("l0f_ws1b", [128, 8, 1024], F32, kind="ExternalInput")
    io["l0f_ws3b"] = nc.dram_tensor("l0f_ws3b", [128, 8, 1024], F32, kind="ExternalInput")
    io["l0f_ws2b"] = nc.dram_tensor("l0f_ws2b", [128, 8, 1024], F32, kind="ExternalInput")
    io["l0f_gwb"] = nc.dram_tensor("l0f_gwb", [128, 8, 16], F32, kind="ExternalInput")
    io["l0f_gbrep"] = nc.dram_tensor("l0f_gbrep", [128, 16], F32, kind="ExternalInput")
    io["l0f_be1"] = nc.dram_tensor("l0f_be1", [16, 512], F32, kind="ExternalInput")
    io["l0f_be3"] = nc.dram_tensor("l0f_be3", [16, 512], F32, kind="ExternalInput")
    io["l0f_be2"] = nc.dram_tensor("l0f_be2", [16, 1024], F32, kind="ExternalInput")
    io["l0f_bs1"] = nc.dram_tensor("l0f_bs1", [1024], F32, kind="ExternalInput")
    io["l0f_bs3"] = nc.dram_tensor("l0f_bs3", [1024], F32, kind="ExternalInput")
    io["l0f_bs2"] = nc.dram_tensor("l0f_bs2", [1024], F32, kind="ExternalInput")
    io["l0f_onehot"] = nc.dram_tensor("l0f_onehot", [16, 16 * 128], BF16, kind="ExternalInput")
    io["l0f_x"] = x1f
    io["l0f_xbf"] = x1b
    io["l0f_y"] = x2f
    io["l0f_ybf"] = x2b
    io["l1a_Wb"] = nc.dram_tensor("l1a_Wb", [4, 128, 8, 1024], F32, kind="ExternalInput")
    io["l1a_tab"] = nc.dram_tensor("l1a_tab", [8, 128, 2944], F32, kind="ExternalInput")
    io["l1a_rblast"] = nc.dram_tensor("l1a_rblast", [128, 8], F32, kind="ExternalInput")
    io["l1a_KT"] = nc.dram_tensor("l1a_KT", [8 * 128 * 8192], BF16, kind="Internal")
    io["l1a_km"] = nc.dram_tensor("l1a_km", [8 * 8192], F32, kind="Internal")
    io["l1a_V"] = nc.dram_tensor("l1a_V", [8192 * 1024], BF16, kind="Internal")
    io["l1a_QT"] = nc.dram_tensor("l1a_QT", [8 * 128 * 8192], BF16, kind="Internal")
    io["l1a_UT"] = nc.dram_tensor("l1a_UT", [8 * 128 * 8192], BF16, kind="Internal")
    io["l1a_x_full"] = x2f
    io["l1a_xbf"] = x2b
    io["l1a_y"] = x3
    io["l1a_ybf"] = x3b
    io["l1f_w1T"] = nc.dram_tensor("l1f_w1T", [1024, 4096], F32, kind="ExternalInput")
    io["l1f_w2T"] = nc.dram_tensor("l1f_w2T", [1024, 4096], F32, kind="ExternalInput")
    io["l1f_h1Tb"] = nc.dram_tensor("l1f_h1Tb", [32, 128, 32, 128], F32, kind="ExternalInput")
    io["l1f_h2Tb"] = nc.dram_tensor("l1f_h2Tb", [32, 128, 32, 128], F32, kind="ExternalInput")
    io["l1f_w3Tb"] = nc.dram_tensor("l1f_w3Tb", [8, 128, 32, 128], F32, kind="ExternalInput")
    io["l1f_scales"] = nc.dram_tensor("l1f_scales", [5, 4096], F32, kind="ExternalInput")
    io["scr_a"] = nc.dram_tensor("scr_a", [4096 * 4096], BF16, kind="Internal")
    io["scr_b"] = nc.dram_tensor("scr_b", [4096 * 4096], BF16, kind="Internal")
    io["l1f_x"] = x3
    io["l1f_xbf"] = x3b

    ALLE = [nc.sync, nc.gpsimd, nc.tensor, nc.vector, nc.scalar]
    p = Prog(nc)
    emit_l0_attn(p, io)
    p.fence_all(ALLE)
    emit_l0_ffn(p, io)
    p.fence_all(ALLE)
    emit_l1_attn(p, io)
    p.fence_all(ALLE)
    emit_l1_ffn(p, io)
    return nc


def prep_host_all(inputs):
    h = {}
    h.update(prep_host_l0a(inputs))
    h.update(prep_host_l0f(inputs))
    h.update(prep_host_l1a(inputs))
    h.update(prep_host_l1f(inputs))
    return h


FUSED = True


def kernel_fused(inputs):
    x = np.asarray(inputs["L0_attn_x"])
    h = prep_host_all(inputs)
    if "f" not in _PROG_CACHE:
        _PROG_CACHE["f"] = build_fused_program()
    nc = _PROG_CACHE["f"]
    rows = [own_rows_for(c) for c in range(8)]
    cores = list(range(8))
    im = []
    for c in cores:
        b = c // 2
        d = dict(h)
        d["x_full"] = np.ascontiguousarray(x[b])
        im.append(d)
    r = run_bass_kernel_spmd(nc, im, core_ids=cores)
    out = np.empty((4, T, 1024), dtype=np.float32)
    for c in cores:
        out[c // 2][rows[c]] = r.results[c]["out_y"]
    return out


def own_rows_for(core):
    sups = SUP_EVEN if core % 2 == 0 else SUP_ODD
    return np.concatenate([np.arange(sp * 512, (sp + 1) * 512) for sp in sups])


_PROG_CACHE = {}


def _progs():
    if "p" not in _PROG_CACHE:
        _PROG_CACHE["p"] = (build_l0a_program(), build_l0f_program(),
                            build_l1a_program(), build_l1f_program())
    return _PROG_CACHE["p"]


def kernel(**inputs):
    if FUSED:
        return kernel_fused(inputs)
    x = np.asarray(inputs["L0_attn_x"])
    h0a = prep_host_l0a(inputs)
    h0f = prep_host_l0f(inputs)
    h1a = prep_host_l1a(inputs)
    h1f = prep_host_l1f(inputs)
    nc0a, nc0f, nc1a, nc1f = _progs()
    rows = [own_rows_for(c) for c in range(8)]
    cores = list(range(8))

    im = []
    for c in cores:
        b = c // 2
        d = dict(h0a)
        d["l0a_x_own"] = np.ascontiguousarray(x[b][rows[c]])
        d["l0a_x_full"] = np.ascontiguousarray(x[b])
        im.append(d)
    r = run_bass_kernel_spmd(nc0a, im, core_ids=cores)
    x1 = [r.results[c]["l0a_y"] for c in cores]

    im = []
    for c in cores:
        d = dict(h0f)
        d["l0f_x"] = x1[c]
        im.append(d)
    r = run_bass_kernel_spmd(nc0f, im, core_ids=cores)
    x2 = [r.results[c]["l0f_y"] for c in cores]

    x2_full = []
    for b in range(4):
        xb = np.empty((T, 1024), dtype=np.float32)
        xb[rows[2 * b]] = x2[2 * b]
        xb[rows[2 * b + 1]] = x2[2 * b + 1]
        x2_full.append(xb)
    im = []
    for c in cores:
        b = c // 2
        d = dict(h1a)
        d["l1a_x_own"] = x2[c]
        d["l1a_x_full"] = x2_full[b]
        im.append(d)
    r = run_bass_kernel_spmd(nc1a, im, core_ids=cores)
    x3 = [r.results[c]["l1a_y"] for c in cores]

    im = []
    for c in cores:
        d = dict(h1f)
        d["l1f_x"] = x3[c]
        im.append(d)
    r = run_bass_kernel_spmd(nc1f, im, core_ids=cores)
    y = [r.results[c]["out_y"] for c in cores]

    out = np.empty((4, T, 1024), dtype=np.float32)
    for c in cores:
        out[c // 2][rows[c]] = y[c]
    return out
